# version 01
# baseline (speedup 1.0000x reference)
import numpy as np
import ml_dtypes
import concourse.bass as bass
import concourse.mybir as mybir
from concourse.bass_utils import run_bass_kernel_spmd
from concourse.alu_op_type import AluOpType
from contextlib import ExitStack

F32 = mybir.dt.float32
BF16 = mybir.dt.bfloat16
AF = mybir.ActivationFunctionType
AX = mybir.AxisListType
ADD = AluOpType.add
SUB = AluOpType.subtract
MUL = AluOpType.mult
MIN = AluOpType.min

P = 128
DM = 1024
NH = 16
HD = 64
NSEQT = 16


def build(n_seq: int, nseqt: int = None, stage: str = 'full'):
    lvl2b = {'2b0': 0, '2bb': 5, '2bh1': 6, '2bh2': 7, '2bh': 10, '2bc': 20, '2bz': 30, '2bt': 40, '2bx': 45}.get(stage, 50)
    if stage in ('2b0', '2bb', '2bh1', '2bh2', '2bh', '2bc', '2bz', '2bt', '2bx'):
        stage = 'full'
    lvl = {'ln': 1, 'qkv': 2, 'stk': 3, 'gramA': 35, 'gramB': 36, 'gramC': 37, 'gramD': 38, 'gram': 4, 'appl': 5}.get(stage, 6)
    gramb = lvl in (36, 37)
    gramc = lvl == 37
    gramd = lvl == 38
    if lvl in (35, 36, 37, 38):
        lvl = 4
        grama = True
    else:
        grama = False
    if lvl < 4 or grama or lvl == 4:
        pass
    if stage not in ('full', '2a', 'gf'):
        stage = 'p1'
    global NSEQT
    if nseqt is not None:
        NSEQT = nseqt
    n_tiles = n_seq * NSEQT
    NT = n_tiles * P
    n_super = n_tiles // 4
    nc = bass.Bass("TRN2", target_bir_lowering=False)

    x_in = nc.dram_tensor("x", [NT, DM], F32, kind="ExternalInput").ap()
    wqT = nc.dram_tensor("wqT", [DM, DM], F32, kind="ExternalInput").ap()
    wkT = nc.dram_tensor("wkT", [DM, DM], F32, kind="ExternalInput").ap()
    wvT = nc.dram_tensor("wvT", [DM, DM], F32, kind="ExternalInput").ap()
    w1_in = nc.dram_tensor("w1", [DM, DM], F32, kind="ExternalInput").ap()
    w2_in = nc.dram_tensor("w2", [DM, 512], F32, kind="ExternalInput").ap()
    w3_in = nc.dram_tensor("w3", [512, 256], F32, kind="ExternalInput").ap()
    w4c_in = nc.dram_tensor("w4c", [P, 2], F32, kind="ExternalInput").ap()
    gw1_in = nc.dram_tensor("gw1", [DM, 512], F32, kind="ExternalInput").ap()
    gw2_in = nc.dram_tensor("gw2", [512, 256], F32, kind="ExternalInput").ap()
    gw3c_in = nc.dram_tensor("gw3c", [P, 2], F32, kind="ExternalInput").ap()
    b1c_in = nc.dram_tensor("b1c", [P, 8], F32, kind="ExternalInput").ap()
    b2c_in = nc.dram_tensor("b2c", [P, 4], F32, kind="ExternalInput").ap()
    b3c_in = nc.dram_tensor("b3c", [P, 2], F32, kind="ExternalInput").ap()
    b4_in = nc.dram_tensor("b4", [1, 1], F32, kind="ExternalInput").ap()
    gb1_in = nc.dram_tensor("gb1", [P, 4], F32, kind="ExternalInput").ap()
    gb2_in = nc.dram_tensor("gb2", [P, 2], F32, kind="ExternalInput").ap()
    gb3_in = nc.dram_tensor("gb3", [1, 1], F32, kind="ExternalInput").ap()
    ar_in = nc.dram_tensor("ar", [1, 1], F32, kind="ExternalInput").ap()
    identb_in = nc.dram_tensor("identb", [P, P], BF16, kind="ExternalInput").ap()
    identf_in = nc.dram_tensor("identf", [P, P], F32, kind="ExternalInput").ap()
    mask_in = nc.dram_tensor("maskc", [P, 512], F32, kind="ExternalInput").ap()
    onescol_in = nc.dram_tensor("onescol", [P, 1], BF16, kind="ExternalInput").ap()
    e0_in = nc.dram_tensor("e0c", [P, P], F32, kind="ExternalInput").ap()
    eps_in = nc.dram_tensor("epsc", [P, 1], F32, kind="ExternalInput").ap()

    out_d = nc.dram_tensor("out", [NT, DM], F32, kind="ExternalOutput").ap()
    sq_scr = nc.dram_tensor("sq_scr", [NT, DM], BF16, kind="Internal").ap()
    sk_scr = nc.dram_tensor("sk_scr", [NT, DM], BF16, kind="Internal").ap()
    v_scr = nc.dram_tensor("v_scr", [NT, DM], BF16, kind="Internal").ap()
    x1_scr = nc.dram_tensor("x1_scr", [NT, DM], F32, kind="Internal").ap()

    st = ExitStack()
    E_ = st.enter_context

    WQ = E_(nc.sbuf_tensor("WQ", [P, 8 * DM], BF16))
    WK = E_(nc.sbuf_tensor("WK", [P, 8 * DM], BF16))
    WV = E_(nc.sbuf_tensor("WV", [P, 8 * DM], BF16))
    W1 = E_(nc.sbuf_tensor("W1", [P, 8 * DM], BF16))
    W2 = E_(nc.sbuf_tensor("W2", [P, 8 * 512], BF16))
    W3 = E_(nc.sbuf_tensor("W3", [P, 4 * 256], BF16))
    W4 = E_(nc.sbuf_tensor("W4", [P, 2], BF16))
    G1 = E_(nc.sbuf_tensor("G1", [P, 8 * 512], BF16))
    G2 = E_(nc.sbuf_tensor("G2", [P, 4 * 256], BF16))
    G3 = E_(nc.sbuf_tensor("G3", [P, 2], BF16))
    B1C = E_(nc.sbuf_tensor("B1C", [P, 8], F32))
    B2C = E_(nc.sbuf_tensor("B2C", [P, 4], F32))
    B3C = E_(nc.sbuf_tensor("B3C", [P, 2], F32))
    B4 = E_(nc.sbuf_tensor("B4", [1, 1], F32))
    GB1C = E_(nc.sbuf_tensor("GB1C", [P, 4], F32))
    GB2C = E_(nc.sbuf_tensor("GB2C", [P, 2], F32))
    E0C = E_(nc.sbuf_tensor("E0C", [P, P], F32))
    GB3 = E_(nc.sbuf_tensor("GB3", [1, 1], F32))
    ARC = E_(nc.sbuf_tensor("ARC", [1, 1], F32))
    IDB = E_(nc.sbuf_tensor("IDB", [P, P], BF16))
    IDF = E_(nc.sbuf_tensor("IDF", [P, P], F32))
    MASKC = E_(nc.sbuf_tensor("MASKC", [P, 512], F32))
    ONESC = E_(nc.sbuf_tensor("ONESC", [P, 1], BF16))
    EPSC = E_(nc.sbuf_tensor("EPSC", [P, 1], F32))
    STR = E_(nc.sbuf_tensor("STR", [P, n_tiles], F32))
    STB = E_(nc.sbuf_tensor("STB", [P, n_tiles], F32))
    GCT = E_(nc.sbuf_tensor("GCT", [P, 4 * 8], BF16))
    GCTA = E_(nc.sbuf_tensor("GCTA", [P, 4 * 8], F32))
    GFROW = E_(nc.sbuf_tensor("GFROW", [1, 4], F32))
    ALPHAR = E_(nc.sbuf_tensor("ALPHAR", [1, 4], F32))
    BETA = E_(nc.sbuf_tensor("BETA", [1, 1], F32))
    GS1F = E_(nc.sbuf_tensor("GS1F", [P, 16], BF16))
    GS2F = E_(nc.sbuf_tensor("GS2F", [P, 8], BF16))
    W4F = E_(nc.sbuf_tensor("W4F", [P, 2], F32))
    G3F = E_(nc.sbuf_tensor("G3F", [P, 2], F32))

    cast_plan = []

    def plan_w(dram, sbuf, total_cols):
        v = dram.rearrange("(c p) n -> p c n", p=P)
        nchunk = v.shape[1]
        ncols = v.shape[2]
        per = max(1, 2048 // ncols)
        c0 = 0
        while c0 < nchunk:
            k = min(per, nchunk - c0)
            cast_plan.append((v[:, c0:c0 + k, :],
                              sbuf[:, c0 * ncols:(c0 + k) * ncols], k * ncols))
            c0 += k

    plan_w(wqT, WQ, 8 * DM)
    plan_w(wkT, WK, 8 * DM)
    plan_w(wvT, WV, 8 * DM)
    plan_w(w1_in, W1, 8 * DM)
    plan_w(w2_in, W2, 8 * 512)
    plan_w(w3_in, W3, 4 * 256)
    plan_w(gw1_in, G1, 8 * 512)
    plan_w(gw2_in, G2, 4 * 256)
    n_rounds = len(cast_plan)

    st1 = ExitStack()
    E1 = st1.enter_context
    XB = [E1(nc.sbuf_tensor(f"XB{k}", [P, DM], F32)) for k in range(2)]
    XM = E1(nc.sbuf_tensor("XM", [P, DM], F32))
    SQB = E1(nc.sbuf_tensor("SQB", [P, DM], F32))
    EBUF = E1(nc.sbuf_tensor("EBUF", [P, 2 * DM], F32))
    NM = E1(nc.sbuf_tensor("NM", [P, 1], F32))
    VAR = E1(nc.sbuf_tensor("VAR", [P, 1], F32))
    RSTD = [E1(nc.sbuf_tensor(f"RSTD{k}", [P, 1], F32)) for k in range(2)]
    NMQ = E1(nc.sbuf_tensor("NMQ", [P, 1], F32))
    NMK = E1(nc.sbuf_tensor("NMK", [P, 1], F32))
    TMP1 = E1(nc.sbuf_tensor("TMP1", [P, 1], F32))
    TMP2 = E1(nc.sbuf_tensor("TMP2", [P, 1], F32))
    ZQ = E1(nc.sbuf_tensor("ZQ", [P, NH], F32))
    ZK = E1(nc.sbuf_tensor("ZK", [P, HD], F32))
    HBF = [E1(nc.sbuf_tensor(f"HBF{k}", [P, DM], BF16)) for k in range(2)]
    HFM = [E1(nc.sbuf_tensor(f"HFM{k}", [P, DM], BF16)) for k in range(2)]
    SQTM = [E1(nc.sbuf_tensor(f"SQTM{k}", [P, DM], BF16)) for k in range(2)]
    SKTM = [E1(nc.sbuf_tensor(f"SKTM{k}", [P, DM], BF16)) for k in range(2)]
    VTM = [E1(nc.sbuf_tensor(f"VTM{k}", [P, DM], BF16)) for k in range(2)]
    SQS = [E1(nc.sbuf_tensor(f"SQS{k}", [P, DM], BF16)) for k in range(2)]
    SKS = [E1(nc.sbuf_tensor(f"SKS{k}", [P, DM], BF16)) for k in range(2)]
    VS = [E1(nc.sbuf_tensor(f"VS{k}", [P, DM], BF16)) for k in range(2)]
    TSQ = [E1(nc.sbuf_tensor(f"TSQ{k}", [P, 2 * DM], BF16)) for k in range(2)]
    TSK = [E1(nc.sbuf_tensor(f"TSK{k}", [P, 2 * DM], BF16)) for k in range(2)]
    MBD = [E1(nc.sbuf_tensor(f"MBD{k}", [P, 512], BF16)) for k in range(2)]
    AS = [E1(nc.sbuf_tensor(f"AS{k}", [P, DM], F32)) for k in range(2)]
    X1A = [E1(nc.sbuf_tensor(f"X1A{k}", [P, DM], F32)) for k in range(2)]
    H2A = [E1(nc.sbuf_tensor(f"H2A{k}", [P, DM], BF16)) for k in range(2)]

    p_t1 = E1(nc.psum_tensor("p_t1", [P, DM], BF16))
    p_tsq = E1(nc.psum_tensor("p_tsq", [64, DM], BF16))
    p_q = E1(nc.psum_tensor("p_q", [P, DM], F32))
    p_k = E1(nc.psum_tensor("p_k", [P, DM], F32))
    p_v = E1(nc.psum_tensor("p_v", [P, 512], F32))
    p_s2 = E1(nc.psum_tensor("p_s2", [P, 8], F32))

    sem_names = ("s_wld s_wcast s_cld s_x s_stk s_x1c s_wscr s_acc s_ln s_rstd1 "
                 "s_hbf s_nm s_sm s_tse s_mask s_hfm s_eq s_ek s_vev s_aev "
                 "s_ht s_qq s_qk s_qv s_ts1 s_ts2 s_gram s_app4 "
                 "s_x1l s_st s_rstd2a s_h2a s_gsT s_gct s_gfp s_gfv s_gfa "
                 "s_x1b s_h2b s_h2t s_obf s_of32 s_mmc s_evc s_zt s_tfr "
                 "s_crow s_cbc s_tw s_cast s_x2t s_x2 s_outw s_gse s_vch s_ach s_vc2").split()
    SM = {}
    for name in sem_names:
        SM[name] = E_(nc.semaphore(name))
    S = type("S", (), SM)

    def stk_view(dram, i):
        return dram[P * i:P * (i + 1), :].rearrange(
            "(g tl) (h d) -> (tl h) g d", g=16, tl=8, h=16)

    N_CLD = 16

    blk1_cm = nc.Block()
    b1 = blk1_cm.__enter__()

    @b1.sync
    def _(eng):
        for ap, dst in ((identb_in, IDB), (identf_in, IDF), (mask_in, MASKC),
                        (onescol_in, ONESC),
                        (e0_in, E0C),
                        (b1c_in, B1C), (b2c_in, B2C), (b3c_in, B3C),
                        (b4_in, B4), (gb1_in, GB1C), (gb2_in, GB2C),
                        (gb3_in, GB3), (ar_in, ARC), (eps_in, EPSC),
                        (w4c_in, W4F), (gw3c_in, G3F)):
            eng.dma_start(dst[:], ap).then_inc(S.s_cld, 16)
        for r, (src, dst, n) in enumerate(cast_plan):
            eng.wait_ge(S.s_wcast, r)
            eng.dma_start(EBUF[:, 0:n], src).then_inc(S.s_wld, 16)
        for i in range(n_tiles):
            if i >= 2:
                eng.wait_ge(S.s_ln, i - 1)
                eng.wait_ge(S.s_x1c, 16 * (i - 1))
            else:
                eng.wait_ge(S.s_wcast, n_rounds + 1)
            eng.dma_start(XB[i % 2][:], x_in[P * i:P * (i + 1), :]).then_inc(S.s_x, 16)
            if lvl >= 3:
                eng.wait_ge(S.s_wscr, 48 * (i + 1))
                if i >= 2:
                    eng.wait_ge(S.s_ts1, 2 * (i - 1))
                    eng.wait_ge(S.s_ts2, 2 * (i - 1))
                    if lvl >= 5:
                        eng.wait_ge(S.s_app4, 4 * (i - 1))
                eng.dma_start(SQS[i % 2][:], stk_view(sq_scr, i)).then_inc(S.s_stk, 16)
                eng.dma_start(SKS[i % 2][:], stk_view(sk_scr, i)).then_inc(S.s_stk, 16)
                eng.dma_start(VS[i % 2][:], stk_view(v_scr, i)).then_inc(S.s_stk, 16)
        if stage != 'p1':
            for i in range(n_tiles):
                eng.wait_ge(S.s_acc, 16 * (i + 1))
                if i >= 2:
                    eng.wait_ge(S.s_h2a, i - 1)
                eng.dma_start(X1A[i % 2][:],
                              x1_scr[P * i:P * (i + 1), :]).then_inc(S.s_x1l, 16)
        if stage in ('p1', '2a', 'gf'):
            if lvl >= 6:
                eng.wait_ge(S.s_acc, 16 * n_tiles)
            elif lvl == 5:
                eng.wait_ge(S.s_aev, 2 * n_tiles)
            elif lvl == 4:
                if grama:
                    eng.wait_ge(S.s_gram, 4 * n_tiles)
                else:
                    eng.wait_ge(S.s_mask, 4 * n_tiles)
            elif lvl == 3:
                eng.wait_ge(S.s_tse, 4 * n_tiles)
            elif lvl == 2:
                eng.wait_ge(S.s_wscr, 48 * n_tiles)
                import os
                dbg = os.environ.get("KDBG", "")
                if dbg == "xm":
                    eng.wait_ge(S.s_hbf, 1)
                    eng.dma_start(out_d[:, 0:1024], XM[:]).then_inc(S.s_outw, 16)
                    eng.wait_ge(S.s_outw, 16)
                    return
                if dbg == "lnstats":
                    eng.wait_ge(S.s_hbf, 1)
                    eng.dma_start(out_d[:, 0:1], NM[:]).then_inc(S.s_outw, 16)
                    eng.dma_start(out_d[:, 1:2], VAR[:]).then_inc(S.s_outw, 16)
                    eng.dma_start(out_d[:, 2:3], RSTD[0][:]).then_inc(S.s_outw, 16)
                    eng.wait_ge(S.s_outw, 48)
                    return
                if dbg in ("hfm", "hbf"):
                    eng.wait_ge(S.s_hfm, 1)
                    srcb = HFM[0] if dbg == "hfm" else HBF[0]
                    eng.dma_start(out_d.bitcast(BF16)[:, 0:1024], srcb[:]).then_inc(S.s_outw, 16)
                    eng.wait_ge(S.s_outw, 16)
                    return
                if dbg:
                    src = {"sq": sq_scr, "sk": sk_scr, "v": v_scr}[dbg]
                    eng.dma_start(out_d.bitcast(BF16)[:, 0:1024], src).then_inc(S.s_outw, 16)
                    eng.wait_ge(S.s_outw, 16)
                    return
            else:
                eng.wait_ge(S.s_hfm, n_tiles)
            if stage in ('2a', 'gf'):
                eng.wait_ge(S.s_h2a, n_tiles)
                eng.wait_ge(S.s_gse, n_tiles)
            if stage == 'gf':
                eng.wait_ge(S.s_gfa, 7)
                eng.wait_ge(S.s_gfv, 3)
            eng.dma_start(out_d, x1_scr).then_inc(S.s_outw, 16)
            eng.wait_ge(S.s_outw, 16)

    @b1.gpsimd
    def _(eng):
        for i in range(n_tiles):
            eng.wait_ge(S.s_x, 16 * (i + 1))
            eng.dma_start(x1_scr[P * i:P * (i + 1), :],
                          XB[i % 2][:]).then_inc(S.s_x1c, 16)
            if lvl < 2:
                continue
            eng.wait_ge(S.s_sm, 2 * (i + 1))
            eng.wait_ge(S.s_vev, 2 * (i + 1))
            eng.dma_start(sq_scr[P * i:P * (i + 1), :], SQTM[i % 2][:]).then_inc(S.s_wscr, 16)
            eng.dma_start(sk_scr[P * i:P * (i + 1), :], SKTM[i % 2][:]).then_inc(S.s_wscr, 16)
            eng.dma_start(v_scr[P * i:P * (i + 1), :], VTM[i % 2][:]).then_inc(S.s_wscr, 16)
            if lvl < 6:
                continue
            eng.wait_ge(S.s_aev, 2 * (i + 1))
            eng.wait_ge(S.s_x1c, 16 * (i + 1))
            eng.dma_start(
                x1_scr[P * i:P * (i + 1), :].rearrange(
                    "(g tl) (h e) -> (tl h) g e", g=16, tl=8, h=16),
                AS[i % 2][:], accum_op=ADD).then_inc(S.s_acc, 16)

    @b1.tensor
    def _(eng):
        eng.wait_ge(S.s_wcast, n_rounds + 1)
        eng.wait_ge(S.s_cld, 16 * N_CLD)
        for i in range(n_tiles):
            eng.wait_ge(S.s_hbf, i + 1)
            if i >= 1 and lvl >= 3:
                eng.wait_ge(S.s_tse, 4 * i)
            for c in range(8):
                ins = eng.transpose(p_t1[:, P * c:P * (c + 1)],
                                    HBF[i % 2][:, P * c:P * (c + 1)], IDB[:])
                if c == 7:
                    ins.then_inc(S.s_ht, 1)
            if lvl < 2:
                continue
            eng.wait_ge(S.s_hfm, i + 1)
            if i >= 1:
                if lvl >= 4 and not grama:
                    eng.wait_ge(S.s_mask, 4 * i)
                eng.wait_ge(S.s_eq, 2 * i)
            for j in range(8):
                lhs = HFM[i % 2][:, P * j:P * (j + 1)]
                eng.matmul(p_q[:, 0:512], lhs, WQ[:, DM * j:DM * j + 512],
                           start=(j == 0), stop=(j == 7))
                ins = eng.matmul(p_q[:, 512:DM], lhs,
                                 WQ[:, DM * j + 512:DM * (j + 1)],
                                 start=(j == 0), stop=(j == 7))
                if j == 7:
                    ins.then_inc(S.s_qq, 1)
            if i >= 1:
                eng.wait_ge(S.s_aev, 2 * i)
            for j in range(8):
                lhs = HFM[i % 2][:, P * j:P * (j + 1)]
                eng.matmul(p_k[:, 0:512], lhs, WK[:, DM * j:DM * j + 512],
                           start=(j == 0), stop=(j == 7))
                ins = eng.matmul(p_k[:, 512:DM], lhs,
                                 WK[:, DM * j + 512:DM * (j + 1)],
                                 start=(j == 0), stop=(j == 7))
                if j == 7:
                    ins.then_inc(S.s_qk, 1)
            if i >= 1:
                eng.wait_ge(S.s_vev, 2 * i)
            for half in range(2):
                if half == 1:
                    eng.wait_ge(S.s_vev, 2 * i + 1)
                for j in range(8):
                    lhs = HFM[i % 2][:, P * j:P * (j + 1)]
                    ins = eng.matmul(
                        p_v[:], lhs,
                        WV[:, DM * j + 512 * half:DM * j + 512 * half + 512],
                        start=(j == 0), stop=(j == 7))
                    if j == 7:
                        ins.then_inc(S.s_qv, 1)
            if lvl < 3:
                continue
            eng.wait_ge(S.s_stk, 48 * (i + 1))
            for r in range(2):
                if r == 0:
                    if i >= 1:
                        eng.wait_ge(S.s_tse, 4 * i - 2)
                else:
                    eng.wait_ge(S.s_tse, 4 * i + 1)
                for gg in range(8):
                    g = 8 * r + gg
                    ins = eng.transpose(p_tsq[0:64, P * gg:P * (gg + 1)],
                                        SQS[i % 2][:, HD * g:HD * (g + 1)],
                                        IDB[:])
                    if gg == 7:
                        ins.then_inc(S.s_ts1, 1)
            for r in range(2):
                if r == 0:
                    if i >= 1:
                        eng.wait_ge(S.s_tse, 4 * i)
                else:
                    eng.wait_ge(S.s_tse, 4 * i + 3)
                for gg in range(8):
                    g = 8 * r + gg
                    ins = eng.transpose(p_t1[0:64, P * gg:P * (gg + 1)],
                                        SKS[i % 2][:, HD * g:HD * (g + 1)],
                                        IDB[:])
                    if gg == 7:
                        ins.then_inc(S.s_ts2, 1)
            if lvl < 4:
                continue
            if not gramc:
                eng.wait_ge(S.s_tse, 4 * (i + 1))
                eng.wait_ge(S.s_eq, 2 * (i + 1))
                eng.wait_ge(S.s_ek, 2 * (i + 1))
            for r in range(4):
                if r > 0 and not grama:
                    eng.wait_ge(S.s_mask, 4 * i + r)
                if gramd:
                    eng.sem_inc(S.s_gram, 1)
                    continue
                for gg in range(4):
                    g = 4 * r + gg
                    ins = eng.matmul(
                        p_q[:, 512 + P * gg:512 + P * (gg + 1)],
                        TSK[i % 2][:, P * g:P * (g + 1)],
                        TSQ[i % 2][:, P * g:P * (g + 1)],
                        start=True, stop=True)
                    if gg == 3:
                        ins.then_inc(S.s_gram, 1)
                if r > 0 and lvl >= 5:
                    b = r - 1
                    for gg in range(4):
                        g = 4 * b + gg
                        ins = eng.matmul(
                            p_k[:, HD * g:HD * (g + 1)],
                            MBD[b % 2][:, P * gg:P * (gg + 1)],
                            VS[i % 2][:, HD * g:HD * (g + 1)],
                            start=True, stop=True)
                        if gg == 3:
                            ins.then_inc(S.s_app4, 1)
            for b in ((3,) if lvl >= 5 else ()):
                eng.wait_ge(S.s_mask, 4 * i + 4)
                for gg in range(4):
                    g = 4 * b + gg
                    ins = eng.matmul(
                        p_k[:, HD * g:HD * (g + 1)],
                        MBD[b % 2][:, P * gg:P * (gg + 1)],
                        VS[i % 2][:, HD * g:HD * (g + 1)],
                        start=True, stop=True)
                    if gg == 3:
                        ins.then_inc(S.s_app4, 1)
        for i in range(n_tiles if stage != 'p1' else 0):
            eng.wait_ge(S.s_h2a, i + 1)
            if i >= 1:
                eng.wait_ge(S.s_gse, i)
            for c in range(8):
                ins = eng.matmul(p_s2[:, c:c + 1],
                                 H2A[i % 2][:, P * c:P * (c + 1)], ONESC[:],
                                 start=True, stop=True)
                if c == 7:
                    ins.then_inc(S.s_gsT, 1)
        if stage in ('p1', '2a'):
            return
        eng.wait_ge(S.s_gct, 1)
        t = 0
        for oc in range(4):
            if t >= 1:
                eng.wait_ge(S.s_gfa, t)
            for c in range(8):
                gct_c = GCT[:].rearrange("p (s c) -> p c s", c=8)[:, c, :]
                ins = eng.matmul(p_q[:, 0:4], G1[:, 512 * c + P * oc:512 * c + P * (oc + 1)],
                                 gct_c, start=(c == 0), stop=(c == 7))
                if c == 7:
                    ins.then_inc(S.s_gfp, 1)
            t += 1
        for o2 in range(2):
            eng.wait_ge(S.s_gfa, t)
            for c2 in range(4):
                ins = eng.matmul(p_q[:, 0:4],
                                 G2[:, 256 * c2 + P * o2:256 * c2 + P * (o2 + 1)],
                                 GS1F[:, 4 * c2:4 * c2 + 4],
                                 start=(c2 == 0), stop=(c2 == 3))
                if c2 == 3:
                    ins.then_inc(S.s_gfp, 1)
            t += 1
        eng.wait_ge(S.s_gfa, t)
        for c3 in range(2):
            ins = eng.matmul(p_q[0:1, 0:4], G3[:, c3:c3 + 1],
                             GS2F[:, 4 * c3:4 * c3 + 4],
                             start=(c3 == 0), stop=(c3 == 1))
            if c3 == 1:
                ins.then_inc(S.s_gfp, 1)

    @b1.vector
    def _(eng):
        vch = [0]

        def vs(ins):
            ins.then_inc(S.s_vch, 1)
            vch[0] += 1

        def vw():
            eng.wait_ge(S.s_vch, vch[0])

        for r, (src, dst, n) in enumerate(cast_plan):
            eng.wait_ge(S.s_wld, 16 * (r + 1))
            eng.tensor_copy(dst, EBUF[:, 0:n]).then_inc(S.s_wcast, 1)
        eng.wait_ge(S.s_cld, 16 * N_CLD)
        eng.tensor_copy(W4[:], W4F[:])
        eng.tensor_copy(G3[:], G3F[:])
        eng.memset(GCTA[:], 0.0)
        for k_t, t in enumerate(TSQ + TSK):
            ins = eng.memset(t[64:P, :], 0.0)
        ins.then_inc(S.s_wcast, 1)

        for i in range(n_tiles):
            eng.wait_ge(S.s_x, 16 * (i + 1))
            vs(eng.reduce_sum(NM[:], XB[i % 2][:], axis=AX.X, negate=True))
            vw()
            vs(eng.tensor_scalar_mul(NM[:], NM[:], 1.0 / DM))
            vw()
            vs(eng.tensor_scalar_add(XM[:], XB[i % 2][:], NM[:, 0:1]))
            vw()
            vs(eng.tensor_tensor(SQB[:], XM[:], XM[:], op=MUL))
            vw()
            eng.reduce_sum(VAR[:], SQB[:], axis=AX.X).then_inc(S.s_ln, 1)
            eng.wait_ge(S.s_rstd1, i + 1)
            if i >= 2:
                eng.wait_ge(S.s_ht, i - 1)
            eng.tensor_scalar_mul(HBF[i % 2][:], XM[:],
                                  RSTD[i % 2][:, 0:1]).then_inc(S.s_hbf, 1)
            if lvl < 2:
                continue
            eng.wait_ge(S.s_qq, i + 1)
            vs(eng.reduce_max(NMQ[:], p_q[:, 0:512], axis=AX.X, negate=True))
            vs(eng.reduce_max(TMP1[:], p_q[:, 512:DM], axis=AX.X, negate=True))
            vw()
            eng.tensor_tensor(NMQ[:], NMQ[:], TMP1[:], op=MIN).then_inc(S.s_nm, 1)
            eng.wait_ge(S.s_qk, i + 1)
            vs(eng.reduce_max(NMK[:], p_k[:, 0:512], axis=AX.X, negate=True))
            vs(eng.reduce_max(TMP1[:], p_k[:, 512:DM], axis=AX.X, negate=True))
            vw()
            eng.tensor_tensor(NMK[:], NMK[:], TMP1[:], op=MIN).then_inc(S.s_nm, 1)
            eng.wait_ge(S.s_eq, 2 * (i + 1))
            eq = EBUF[:, 0:DM]
            vs(eng.reduce_sum(ZQ[:], eq.rearrange("p (h d) -> p h d", h=NH), axis=AX.X))
            vw()
            vs(eng.reciprocal(ZQ[:], ZQ[:]))
            vw()
            if i >= 2:
                eng.wait_ge(S.s_wscr, 48 * (i - 1))
            eng.tensor_tensor(
                SQTM[i % 2][:].rearrange("p (h d) -> p h d", h=NH),
                eq.rearrange("p (h d) -> p h d", h=NH),
                ZQ[:, :, None].broadcast_to([P, NH, HD]),
                op=MUL).then_inc(S.s_sm, 1)
            eng.wait_ge(S.s_ek, 2 * (i + 1))
            ek = EBUF[:, DM:2 * DM]
            vs(eng.reduce_sum(ZK[:], ek.rearrange("p (h d) -> p d h", h=NH), axis=AX.X))
            vw()
            vs(eng.reciprocal(ZK[:], ZK[:]))
            vw()
            eng.tensor_tensor(
                SKTM[i % 2][:].rearrange("p (h d) -> p h d", h=NH),
                ek.rearrange("p (h d) -> p h d", h=NH),
                ZK[:, None, :].broadcast_to([P, NH, HD]),
                op=MUL).then_inc(S.s_sm, 1)
            if lvl < 3:
                continue
            if i >= 1 and lvl >= 4:
                eng.wait_ge(S.s_gram, 4 * i)
            for r in range(2):
                eng.wait_ge(S.s_ts1, 2 * i + r + 1)
                eng.tensor_copy(TSQ[i % 2][0:64, DM * r:DM * (r + 1)],
                                p_tsq[0:64, :]).then_inc(S.s_tse, 1)
            for r in range(2):
                eng.wait_ge(S.s_ts2, 2 * i + r + 1)
                eng.tensor_copy(TSK[i % 2][0:64, DM * r:DM * (r + 1)],
                                p_t1[0:64, :]).then_inc(S.s_tse, 1)
            if lvl < 4 or grama:
                continue
            for r in range(4):
                eng.wait_ge(S.s_gram, 4 * i + r + 1)
                if r >= 2 and lvl >= 5:
                    eng.wait_ge(S.s_app4, 4 * i + r - 1)
                eng.tensor_tensor(MBD[r % 2][:], p_q[:, 512:DM], MASKC[:],
                                  op=MUL).then_inc(S.s_mask, 1)
        for i in range(n_tiles if stage != 'p1' else 0):
            eng.wait_ge(S.s_x1l, 16 * (i + 1))
            vs(eng.reduce_sum(NM[:], X1A[i % 2][:], axis=AX.X, negate=True))
            vw()
            vs(eng.tensor_scalar_mul(NM[:], NM[:], 1.0 / DM))
            vw()
            vs(eng.tensor_scalar_add(XM[:], X1A[i % 2][:], NM[:, 0:1]))
            vw()
            vs(eng.tensor_tensor(SQB[:], XM[:], XM[:], op=MUL))
            vw()
            eng.reduce_sum(VAR[:], SQB[:], axis=AX.X).then_inc(S.s_st, 1)
            eng.wait_ge(S.s_rstd2a, i + 1)
            vs(eng.tensor_tensor(STB[:, i:i + 1], NM[:], STR[:, i:i + 1], op=MUL))
            vw()
            if i >= 2:
                eng.wait_ge(S.s_gsT, i - 1)
            eng.tensor_scalar(out=H2A[i % 2][:], in0=X1A[i % 2][:],
                              scalar1=STR[:, i:i + 1], scalar2=STB[:, i:i + 1],
                              op0=MUL, op1=ADD).then_inc(S.s_h2a, 1)
            sq = i // NSEQT
            eng.wait_ge(S.s_gsT, i + 1)
            vs(eng.tensor_tensor(GCTA[:, 8 * sq:8 * sq + 8],
                                 GCTA[:, 8 * sq:8 * sq + 8],
                                 p_s2[:], op=ADD))
            vw()
            eng.sem_inc(S.s_gse, 1)
        if stage == 'p1':
            return
        eng.wait_ge(S.s_gse, n_tiles)
        vw()
        eng.tensor_scalar_mul(GCT[:], GCTA[:], 1.0 / (NSEQT * P)).then_inc(S.s_gct, 1)
        if stage == '2a':
            return
        eng.wait_ge(S.s_gfa, 7)
        vs(eng.tensor_scalar(out=ALPHAR[:], in0=GFROW[:], scalar1=1.4, scalar2=-1.0,
                             op0=MUL, op1=ADD))
        vw()
        vs(eng.tensor_scalar(out=ALPHAR[:], in0=ALPHAR[:], scalar1=ARC[0:1, 0:1],
                             scalar2=0.1, op0=MUL, op1=ADD))
        vw()
        eng.tensor_scalar_mul(BETA[:], ARC[0:1, 0:1], 0.6).then_inc(S.s_gfv, 3)

    @b1.scalar
    def _(eng):
        ach = [0]

        def as_(ins):
            ins.then_inc(S.s_ach, 1)
            ach[0] += 1

        def aw():
            eng.wait_ge(S.s_ach, ach[0])

        for i in range(n_tiles):
            eng.wait_ge(S.s_ln, i + 1)
            if i >= 2:
                eng.wait_ge(S.s_hbf, i - 1)
            as_(eng.activation(RSTD[i % 2][:], VAR[:], AF.Ln,
                               bias=EPSC[:, 0:1], scale=1.0 / DM))
            aw()
            eng.activation(RSTD[i % 2][:], RSTD[i % 2][:], AF.Exp,
                           scale=-0.5).then_inc(S.s_rstd1, 1)
            eng.wait_ge(S.s_ht, i + 1)
            if i >= 1:
                eng.wait_ge(S.s_qv, 2 * i)
            eng.copy(HFM[i % 2][:], p_t1[:]).then_inc(S.s_hfm, 1)
            if lvl < 2:
                continue
            eng.wait_ge(S.s_nm, 2 * i + 1)
            if i >= 1:
                eng.wait_ge(S.s_sm, 2 * i)
            eng.activation(EBUF[:, 0:512], p_q[:, 0:512], AF.Exp,
                           bias=NMQ[:, 0:1]).then_inc(S.s_eq, 1)
            eng.activation(EBUF[:, 512:DM], p_q[:, 512:DM], AF.Exp,
                           bias=NMQ[:, 0:1]).then_inc(S.s_eq, 1)
            eng.wait_ge(S.s_nm, 2 * i + 2)
            eng.activation(EBUF[:, DM:DM + 512], p_k[:, 0:512], AF.Exp,
                           bias=NMK[:, 0:1]).then_inc(S.s_ek, 1)
            eng.activation(EBUF[:, DM + 512:2 * DM], p_k[:, 512:DM], AF.Exp,
                           bias=NMK[:, 0:1]).then_inc(S.s_ek, 1)
            eng.wait_ge(S.s_qv, 2 * i + 1)
            if i >= 2:
                eng.wait_ge(S.s_wscr, 48 * (i - 1))
            eng.copy(VTM[i % 2][:, 0:512], p_v[:]).then_inc(S.s_vev, 1)
            eng.wait_ge(S.s_qv, 2 * i + 2)
            eng.copy(VTM[i % 2][:, 512:DM], p_v[:]).then_inc(S.s_vev, 1)
            if lvl < 5:
                continue
            eng.wait_ge(S.s_app4, 4 * i + 2)
            if i >= 2:
                eng.wait_ge(S.s_acc, 16 * (i - 1))
            eng.copy(AS[i % 2][:, 0:512], p_k[:, 0:512]).then_inc(S.s_aev, 1)
            eng.wait_ge(S.s_app4, 4 * i + 4)
            eng.copy(AS[i % 2][:, 512:DM], p_k[:, 512:DM]).then_inc(S.s_aev, 1)
        for i in range(n_tiles if stage != 'p1' else 0):
            eng.wait_ge(S.s_st, i + 1)
            as_(eng.activation(TMP2[:], VAR[:], AF.Ln, bias=EPSC[:, 0:1],
                               scale=1.0 / DM))
            aw()
            eng.activation(STR[:, i:i + 1], TMP2[:], AF.Exp,
                           scale=-0.5).then_inc(S.s_rstd2a, 1)
        if stage in ('p1', '2a'):
            return
        for oc in range(4):
            eng.wait_ge(S.s_gfp, oc + 1)
            eng.activation(GS1F[:, 4 * oc:4 * (oc + 1)], p_q[:, 0:4], AF.Gelu,
                           bias=GB1C[:, oc:oc + 1]).then_inc(S.s_gfa, 1)
        for o2 in range(2):
            eng.wait_ge(S.s_gfp, 4 + o2 + 1)
            eng.activation(GS2F[:, 4 * o2:4 * (o2 + 1)], p_q[:, 0:4], AF.Gelu,
                           bias=GB2C[:, o2:o2 + 1]).then_inc(S.s_gfa, 1)
        eng.wait_ge(S.s_gfp, 7)
        eng.activation(GFROW[:], p_q[0:1, 0:4], AF.Sigmoid,
                       bias=GB3[0:1, 0:1]).then_inc(S.s_gfa, 1)

    blk1_cm.__exit__(None, None, None)
    st1.close()
    E2 = st.enter_context
    X1S = [E2(nc.sbuf_tensor(f"X1S{k}", [P, 4 * DM], F32)) for k in range(2)]
    H2B = E2(nc.sbuf_tensor("H2B", [P, DM], BF16))
    OBF = E2(nc.sbuf_tensor("OBF", [P, 8 * 512], BF16))
    OF32 = E2(nc.sbuf_tensor("OF32", [P, 8 * 512], F32))
    T1B = E2(nc.sbuf_tensor("T1B", [P, 8 * 512], BF16))
    T2B = E2(nc.sbuf_tensor("T2B", [P, 4 * 512], BF16))
    T3B = E2(nc.sbuf_tensor("T3B", [P, 2 * 512], BF16))
    TFR = E2(nc.sbuf_tensor("TFR", [1, 512], F32))
    CROW = E2(nc.sbuf_tensor("CROW", [P, 512], F32))
    TWA = E2(nc.sbuf_tensor("TWA", [P, 512], F32))
    TWB = E2(nc.sbuf_tensor("TWB", [P, 512], F32))
    X2 = [E2(nc.sbuf_tensor(f"X2_{k}", [P, DM], F32)) for k in range(2)]
    pA = E2(nc.psum_tensor("pA", [P, 512], F32))
    pB = E2(nc.psum_tensor("pB", [P, 512], F32))
    pZ = E2(nc.psum_tensor("pZ", [1, 512], F32))
    pC = E2(nc.psum_tensor("pC", [P, 512], F32))
    pH = E2(nc.psum_tensor("pH", [P, DM], BF16))
    pX = E2(nc.psum_tensor("pX", [P, DM], F32))

    def barrier(eng):
        eng.wait_ge(S.s_acc, 16 * n_tiles)
        eng.wait_ge(S.s_h2a, n_tiles)
        eng.wait_ge(S.s_gsT, n_tiles)
        eng.wait_ge(S.s_gfa, 7)
        eng.wait_ge(S.s_gfv, 3)

    if stage in ('p1', '2a', 'gf'):
        st.close()
        return nc, n_tiles
    blk2_cm = nc.Block()
    b2 = blk2_cm.__enter__()

    import os as _os0
    n_cyc = 3 if lvl2b >= 40 else 1
    if _os0.environ.get('KCYC'):
        n_cyc = int(_os0.environ['KCYC'])

    @b2.sync
    def _(eng):
        barrier(eng)
        for s_i in range(n_super):
            for k in range(4):
                if s_i >= 2:
                    if lvl2b >= 45:
                        eng.wait_ge(S.s_x2, 4 * (s_i - 1))
                    else:
                        eng.wait_ge(S.s_h2b, 4 * (s_i - 1))
                eng.dma_start(X1S[s_i % 2][:, DM * k:DM * (k + 1)],
                              x1_scr[P * (4 * s_i + k):P * (4 * s_i + k + 1), :]
                              ).then_inc(S.s_x1b, 16)
        if lvl2b == 50:
            eng.wait_ge(S.s_outw, 16 * n_tiles)
            import os as _os
            if _os.environ.get("KDBG2"):
                eng.dma_start(out_d[0:1, 0:4], GFROW[:]).then_inc(S.s_outw, 16)
                eng.dma_start(out_d[1:2, 0:4], ALPHAR[:]).then_inc(S.s_outw, 16)
                eng.dma_start(out_d[2:3, 0:16], GCT.ap().bitcast(F32)[0:1, 0:16]).then_inc(S.s_outw, 16)
                eng.wait_ge(S.s_outw, 16 * n_tiles + 48)
        else:
            if lvl2b == 0:
                eng.wait_ge(S.s_x1b, 16 * 4 * n_super)
            elif lvl2b == 5:
                eng.wait_ge(S.s_h2b, 4 * n_super)
            elif lvl2b == 6:
                eng.wait_ge(S.s_obf, 4 * n_super)
            elif lvl2b == 7:
                eng.wait_ge(S.s_of32, 4 * n_super)
            elif lvl2b == 10:
                eng.wait_ge(S.s_of32, 4 * n_super)
                eng.wait_ge(S.s_obf, 4 * n_super)
            elif lvl2b == 20:
                eng.wait_ge(S.s_evc, 14 * n_super)
                import os as _os
                if _os.environ.get("KDBG3"):
                    ob = out_d.bitcast(BF16)
                    eng.dma_start(ob[0:P, :], T2B[:]).then_inc(S.s_outw, 16)
                    eng.wait_ge(S.s_outw, 16)
                    return
            elif lvl2b == 30:
                eng.wait_ge(S.s_cbc, n_super)
                import os as _os
                if _os.environ.get("KDBG4"):
                    ob = out_d.bitcast(BF16)
                    eng.dma_start(ob[0:P, 0:1024], T3B[:]).then_inc(S.s_outw, 16)
                    eng.dma_start(out_d[P:P + 1, 0:512], TFR[:]).then_inc(S.s_outw, 16)
                    eng.dma_start(out_d[P + 1:P + 2, 0:512], CROW[0:1, :]).then_inc(S.s_outw, 16)
                    eng.wait_ge(S.s_outw, 48)
                    return
            elif lvl2b == 40:
                eng.wait_ge(S.s_tw, n_cyc * n_super)
                import os as _os
                if _os.environ.get("KDBG5"):
                    eng.wait_ge(S.s_of32, 4 * n_super)
                    for rr in range(4):
                        eng.dma_start(out_d[P * rr:P * (rr + 1), :],
                                      OF32[:, 1024 * rr:1024 * (rr + 1)]).then_inc(S.s_outw, 16)
                    eng.wait_ge(S.s_outw, 64)
                    return
            elif lvl2b == 45:
                eng.wait_ge(S.s_x2, 4 * n_super)
            eng.dma_start(out_d, x1_scr).then_inc(S.s_outw, 16)
            eng.wait_ge(S.s_outw, 16)

    @b2.tensor
    def _(eng):
        barrier(eng)
        n_mm = 0
        n_h2t = 0
        n_x2t = 0
        n_cbc = 0
        n_cast = 0
        n_crow = 0
        n_zt = 0
        for s_i in range(n_super if lvl2b >= 6 else 0):
            for k in range(4):
                eng.wait_ge(S.s_h2b, 4 * s_i + k + 1)
                if n_h2t >= 1:
                    if lvl2b != 7:
                        eng.wait_ge(S.s_obf, n_h2t)
                    if lvl2b != 6:
                        eng.wait_ge(S.s_of32, n_h2t)
                for c in range(8):
                    ins = eng.transpose(pH[:, P * c:P * (c + 1)],
                                        H2B[:, P * c:P * (c + 1)], IDB[:])
                    if c == 7:
                        ins.then_inc(S.s_h2t, 1)
                n_h2t += 1
            for cyc in range(n_cyc if lvl2b >= 20 else 0):
                if cyc == 0:
                    eng.wait_ge(S.s_obf, n_h2t)
                else:
                    eng.wait_ge(S.s_cast, n_cast)
                for (n_o, nj, Wsb, src, in_w) in (
                        (8, 8, W1, OBF, DM), (4, 8, W2, T1B, 512),
                        (2, 4, W3, T2B, 256)):
                    for o in range(n_o):
                        pp = pA if o % 2 == 0 else pB
                        if o == 0:
                            if n_mm >= 1:
                                eng.wait_ge(S.s_evc, n_mm)
                        elif n_mm >= 2:
                            eng.wait_ge(S.s_evc, n_mm - 1)
                        for j in range(nj):
                            ins = eng.matmul(
                                pp[:],
                                Wsb[:, in_w * j + P * o:in_w * j + P * (o + 1)],
                                src[:, 512 * j:512 * (j + 1)],
                                start=(j == 0), stop=(j == nj - 1))
                            if j == nj - 1:
                                ins.then_inc(S.s_mmc, 1)
                        n_mm += 1
                if lvl2b < 30:
                    continue
                eng.wait_ge(S.s_evc, n_mm)
                if n_zt >= 1:
                    eng.wait_ge(S.s_crow, n_crow)
                for j in range(2):
                    ins = eng.matmul(pZ[:], W4[:, j:j + 1],
                                     T3B[:, 512 * j:512 * (j + 1)],
                                     start=(j == 0), stop=(j == 1))
                    if j == 1:
                        ins.then_inc(S.s_zt, 1)
                n_zt += 1
                eng.wait_ge(S.s_crow, n_crow + 1)
                n_crow += 1
                if n_cbc >= 1 and lvl2b >= 40:
                    eng.wait_ge(S.s_tw, n_cbc)
                eng.matmul(pC[:], E0C[:], CROW[:],
                           start=True, stop=True).then_inc(S.s_cbc, 1)
                n_cbc += 1
                if cyc < 2:
                    n_cast += 1
            if lvl2b < 45:
                continue
            eng.wait_ge(S.s_tw, n_cbc)
            for k in range(4):
                if n_x2t >= 1:
                    eng.wait_ge(S.s_x2, n_x2t)
                for c in range(8):
                    ins = eng.transpose(
                        pX[:, P * c:P * (c + 1)],
                        OF32[:, 512 * c + P * k:512 * c + P * (k + 1)], IDF[:])
                    if c == 7:
                        ins.then_inc(S.s_x2t, 1)
                n_x2t += 1

    @b2.vector
    def _(eng):
        barrier(eng)
        vch = [0]

        def vs(ins):
            ins.then_inc(S.s_vc2, 1)
            vch[0] += 1

        def vw():
            eng.wait_ge(S.s_vc2, vch[0])

        n_tw = 0
        n_x2 = 0
        for s_i in range(n_super if lvl2b >= 5 else 0):
            for k in range(4):
                i_tile = 4 * s_i + k
                eng.wait_ge(S.s_x1b, 16 * (4 * s_i + k + 1))
                if (s_i > 0 or k > 0) and lvl2b >= 6:
                    eng.wait_ge(S.s_h2t, 4 * s_i + k)
                eng.tensor_scalar(out=H2B[:],
                                  in0=X1S[s_i % 2][:, DM * k:DM * (k + 1)],
                                  scalar1=STR[:, i_tile:i_tile + 1],
                                  scalar2=STB[:, i_tile:i_tile + 1],
                                  op0=MUL, op1=ADD).then_inc(S.s_h2b, 1)
                if lvl2b < 6 or lvl2b == 7:
                    continue
                eng.wait_ge(S.s_h2t, 4 * s_i + k + 1)
                for c in range(8):
                    ins = eng.tensor_copy(
                        OBF[:, 512 * c + P * k:512 * c + P * (k + 1)],
                        pH[:, P * c:P * (c + 1)])
                    if c == 7:
                        ins.then_inc(S.s_obf, 1)
            for cyc in range(n_cyc if lvl2b >= 40 else 0):
                eng.wait_ge(S.s_cbc, n_tw + 1)
                if cyc == 0:
                    eng.wait_ge(S.s_of32, 4 * (s_i + 1))
                pairs = ((0, 4, ADD, SUB), (1, 5, ADD, SUB),
                         (2, 6, SUB, ADD), (3, 7, SUB, ADD))
                vw()
                last = None
                for (a, b, opa, opb) in pairs:
                    ca = OF32[:, 512 * a:512 * (a + 1)]
                    cb = OF32[:, 512 * b:512 * (b + 1)]
                    vs(eng.tensor_tensor(TWB[:], cb, pC[:], op=MUL))
                    vs(eng.tensor_tensor(TWA[:], ca, pC[:], op=MUL))
                    vw()
                    vs(eng.tensor_tensor(cb, cb, TWA[:], op=opb))
                    vs(eng.tensor_tensor(ca, ca, TWB[:], op=opa))
                    vw()
                    last = True
                eng.sem_inc(S.s_tw, 1)
                n_tw += 1
            for k in range(4 if lvl2b >= 45 else 0):
                eng.wait_ge(S.s_x2t, n_x2 + 1)
                if n_x2 >= 2 and lvl2b == 50:
                    eng.wait_ge(S.s_outw, 16 * (n_x2 - 1))
                eng.tensor_tensor(X2[n_x2 % 2][:], pX[:],
                                  X1S[s_i % 2][:, DM * k:DM * (k + 1)],
                                  op=ADD).then_inc(S.s_x2, 1)
                n_x2 += 1

    @b2.scalar
    def _(eng):
        barrier(eng)
        n_mm = 0
        n_tw = 0
        n_crow = 0
        for s_i in range(n_super if (lvl2b == 7 or lvl2b >= 10) else 0):
            for k in range(4):
                eng.wait_ge(S.s_h2t, 4 * s_i + k + 1)
                if lvl2b >= 10:
                    eng.wait_ge(S.s_obf, 4 * s_i + k + 1)
                if s_i >= 1 and lvl2b >= 45:
                    eng.wait_ge(S.s_x2t, 4 * s_i)
                for c in range(8):
                    ins = eng.copy(
                        OF32[:, 512 * c + P * k:512 * c + P * (k + 1)],
                        pH[:, P * c:P * (c + 1)])
                    if c == 7:
                        ins.then_inc(S.s_of32, 1)
            for cyc in range(n_cyc if lvl2b >= 20 else 0):
                for (n_o, bias_c, dst) in ((8, B1C, T1B), (4, B2C, T2B),
                                           (2, B3C, T3B)):
                    for o in range(n_o):
                        pp = pA if o % 2 == 0 else pB
                        eng.wait_ge(S.s_mmc, n_mm + 1)
                        n_mm += 1
                        eng.activation(dst[:, 512 * o:512 * (o + 1)], pp[:],
                                       AF.Gelu,
                                       bias=bias_c[:, o:o + 1]).then_inc(S.s_evc, 1)
                if lvl2b < 30:
                    continue
                eng.wait_ge(S.s_zt, n_cyc * s_i + cyc + 1)
                if n_crow >= 1:
                    eng.wait_ge(S.s_crow, n_crow)
                eng.activation(TFR[:], pZ[:], AF.Sigmoid,
                               bias=B4[0:1, 0:1]).then_inc(S.s_tfr, 1)
                n_crow += 1
                if cyc < 2 and lvl2b >= 40:
                    eng.wait_ge(S.s_tw, n_cyc * s_i + cyc + 1)
                    for c in range(8):
                        ins = eng.copy(OBF[:, 512 * c:512 * (c + 1)],
                                       OF32[:, 512 * c:512 * (c + 1)])
                        if c == 7:
                            ins.then_inc(S.s_cast, 1)
            if lvl2b == 50:
                for k in range(4):
                    ki = 4 * s_i + k
                    eng.wait_ge(S.s_x2, ki + 1)
                    eng.dma_start(out_d[P * ki:P * (ki + 1), :],
                                  X2[ki % 2][:]).then_inc(S.s_outw, 16)

    @b2.gpsimd
    def _(eng):
        barrier(eng)
        eng.memset(CROW[:], 0.0)
        for s_i in range(n_super if lvl2b >= 30 else 0):
            seq = s_i // (n_super // n_seq)
            for cyc in range(n_cyc):
                n = n_cyc * s_i + cyc
                eng.wait_ge(S.s_tfr, n + 1)
                eng.tensor_scalar(out=CROW[0:1, :], in0=TFR[:],
                                  scalar1=BETA[0:1, 0:1],
                                  scalar2=ALPHAR[0:1, seq:seq + 1],
                                  op0=MUL, op1=ADD).then_inc(S.s_crow, 1)

    blk2_cm.__exit__(None, None, None)
    st.close()
    return nc, n_tiles


def _host_inputs(inputs, shard):
    f32 = np.float32
    bf = ml_dtypes.bfloat16
    d = {}
    d["x"] = np.ascontiguousarray(shard.reshape(-1, DM))
    d["wqT"] = np.ascontiguousarray(inputs["L0_attn_W_Q"].T)
    d["wkT"] = np.ascontiguousarray(inputs["L0_attn_W_K"].T)
    d["wvT"] = np.ascontiguousarray(inputs["L0_attn_W_V"].T)
    d["w1"] = np.ascontiguousarray(inputs["L0_ffn_cn_w1"])
    d["w2"] = np.ascontiguousarray(inputs["L0_ffn_cn_w2"])
    d["w3"] = np.ascontiguousarray(inputs["L0_ffn_cn_w3"])
    d["w4c"] = np.ascontiguousarray(inputs["L0_ffn_cn_w4"].reshape(2, 128).T)
    d["gw1"] = np.ascontiguousarray(inputs["L0_ffn_gc_w1"])
    d["gw2"] = np.ascontiguousarray(inputs["L0_ffn_gc_w2"])
    d["gw3c"] = np.ascontiguousarray(inputs["L0_ffn_gc_w3"].reshape(2, 128).T)
    d["b1c"] = np.ascontiguousarray(inputs["L0_ffn_cn_b1"].reshape(8, 128).T)
    d["b2c"] = np.ascontiguousarray(inputs["L0_ffn_cn_b2"].reshape(4, 128).T)
    d["b3c"] = np.ascontiguousarray(inputs["L0_ffn_cn_b3"].reshape(2, 128).T)
    d["b4"] = np.ascontiguousarray(inputs["L0_ffn_cn_b4"].reshape(1, 1))
    d["gb1"] = np.ascontiguousarray(inputs["L0_ffn_gc_b1"].reshape(4, 128).T)
    d["gb2"] = np.ascontiguousarray(inputs["L0_ffn_gc_b2"].reshape(2, 128).T)
    d["gb3"] = np.ascontiguousarray(inputs["L0_ffn_gc_b3"].reshape(1, 1))
    d["ar"] = np.ascontiguousarray(inputs["L0_ffn_adaptive_range"].reshape(1, 1))
    d["identb"] = np.eye(P, dtype=bf)
    d["identf"] = np.eye(P, dtype=f32)
    mask1 = np.kron(np.eye(8, dtype=f32), np.ones((16, 16), dtype=f32))
    d["maskc"] = np.tile(mask1, (1, 4)).astype(f32)
    d["onescol"] = np.ones((P, 1), dtype=bf)
    e0 = np.zeros((P, P), dtype=f32)
    e0[0, :] = 1.0
    d["e0c"] = e0
    d["epsc"] = np.full((P, 1), 1e-5, dtype=f32)
    for k in d:
        if d[k].dtype == np.float64:
            raise ValueError(k)
    return d


def kernel(**inputs):
    x = np.asarray(inputs["L0_attn_input_seq_embs"])
    n_cores = 8
    shards = np.split(x, n_cores, axis=0)
    nc, n_tiles = build(n_seq=4)
    in_maps = [_host_inputs(inputs, s) for s in shards]
    res = run_bass_kernel_spmd(nc, in_maps, core_ids=list(range(n_cores)))
    outs = [np.asarray(r["out"], dtype=np.float32) for r in res.results]
    return np.concatenate(outs, axis=0).reshape(32, 2048, 1024)


# version 02
# speedup vs baseline: 1.0080x; 1.0080x over previous
import numpy as np
import ml_dtypes
import concourse.bass as bass
import concourse.mybir as mybir
from concourse.bass_utils import run_bass_kernel_spmd
from concourse.alu_op_type import AluOpType
from contextlib import ExitStack

F32 = mybir.dt.float32
BF16 = mybir.dt.bfloat16
AF = mybir.ActivationFunctionType
AX = mybir.AxisListType
ADD = AluOpType.add
SUB = AluOpType.subtract
MUL = AluOpType.mult
MIN = AluOpType.min

P = 128
DM = 1024
NH = 16
HD = 64
NSEQT = 16


def build(n_seq: int, nseqt: int = None, stage: str = 'full'):
    lvl2b = {'2b0': 0, '2bb': 5, '2bh1': 6, '2bh2': 7, '2bh': 10, '2bc': 20, '2bz': 30, '2bt': 40, '2bx': 45}.get(stage, 50)
    if stage in ('2b0', '2bb', '2bh1', '2bh2', '2bh', '2bc', '2bz', '2bt', '2bx'):
        stage = 'full'
    lvl = {'ln': 1, 'qkv': 2, 'stk': 3, 'gramA': 35, 'gramB': 36, 'gramC': 37, 'gramD': 38, 'gram': 4, 'appl': 5}.get(stage, 6)
    gramb = lvl in (36, 37)
    gramc = lvl == 37
    gramd = lvl == 38
    if lvl in (35, 36, 37, 38):
        lvl = 4
        grama = True
    else:
        grama = False
    if lvl < 4 or grama or lvl == 4:
        pass
    if stage not in ('full', '2a', 'gf'):
        stage = 'p1'
    global NSEQT
    if nseqt is not None:
        NSEQT = nseqt
    n_tiles = n_seq * NSEQT
    NT = n_tiles * P
    n_super = n_tiles // 4
    nc = bass.Bass("TRN2", target_bir_lowering=False)

    x_in = nc.dram_tensor("x", [NT, DM], F32, kind="ExternalInput").ap()
    wqT = nc.dram_tensor("wqT", [DM, DM], F32, kind="ExternalInput").ap()
    wkT = nc.dram_tensor("wkT", [DM, DM], F32, kind="ExternalInput").ap()
    wvT = nc.dram_tensor("wvT", [DM, DM], F32, kind="ExternalInput").ap()
    w1_in = nc.dram_tensor("w1", [DM, DM], F32, kind="ExternalInput").ap()
    w2_in = nc.dram_tensor("w2", [DM, 512], F32, kind="ExternalInput").ap()
    w3_in = nc.dram_tensor("w3", [512, 256], F32, kind="ExternalInput").ap()
    w4c_in = nc.dram_tensor("w4c", [P, 2], F32, kind="ExternalInput").ap()
    gw1_in = nc.dram_tensor("gw1", [DM, 512], F32, kind="ExternalInput").ap()
    gw2_in = nc.dram_tensor("gw2", [512, 256], F32, kind="ExternalInput").ap()
    gw3c_in = nc.dram_tensor("gw3c", [P, 2], F32, kind="ExternalInput").ap()
    b1c_in = nc.dram_tensor("b1c", [P, 8], F32, kind="ExternalInput").ap()
    b2c_in = nc.dram_tensor("b2c", [P, 4], F32, kind="ExternalInput").ap()
    b3c_in = nc.dram_tensor("b3c", [P, 2], F32, kind="ExternalInput").ap()
    b4_in = nc.dram_tensor("b4", [1, 1], F32, kind="ExternalInput").ap()
    gb1_in = nc.dram_tensor("gb1", [P, 4], F32, kind="ExternalInput").ap()
    gb2_in = nc.dram_tensor("gb2", [P, 2], F32, kind="ExternalInput").ap()
    gb3_in = nc.dram_tensor("gb3", [1, 1], F32, kind="ExternalInput").ap()
    ar_in = nc.dram_tensor("ar", [1, 1], F32, kind="ExternalInput").ap()
    identb_in = nc.dram_tensor("identb", [P, P], BF16, kind="ExternalInput").ap()
    identf_in = nc.dram_tensor("identf", [P, P], F32, kind="ExternalInput").ap()
    mask_in = nc.dram_tensor("maskc", [P, 512], F32, kind="ExternalInput").ap()
    onescol_in = nc.dram_tensor("onescol", [P, 1], BF16, kind="ExternalInput").ap()
    e0_in = nc.dram_tensor("e0c", [P, P], F32, kind="ExternalInput").ap()
    eps_in = nc.dram_tensor("epsc", [P, 1], F32, kind="ExternalInput").ap()

    out_d = nc.dram_tensor("out", [NT, DM], F32, kind="ExternalOutput").ap()
    sq_scr = nc.dram_tensor("sq_scr", [NT, DM], BF16, kind="Internal").ap()
    sk_scr = nc.dram_tensor("sk_scr", [NT, DM], BF16, kind="Internal").ap()
    v_scr = nc.dram_tensor("v_scr", [NT, DM], BF16, kind="Internal").ap()
    x1_scr = nc.dram_tensor("x1_scr", [NT, DM], F32, kind="Internal").ap()

    st = ExitStack()
    E_ = st.enter_context

    WQ = E_(nc.sbuf_tensor("WQ", [P, 8 * DM], BF16))
    WK = E_(nc.sbuf_tensor("WK", [P, 8 * DM], BF16))
    WV = E_(nc.sbuf_tensor("WV", [P, 8 * DM], BF16))
    W1 = E_(nc.sbuf_tensor("W1", [P, 8 * DM], BF16))
    W2 = E_(nc.sbuf_tensor("W2", [P, 8 * 512], BF16))
    W3 = E_(nc.sbuf_tensor("W3", [P, 4 * 256], BF16))
    W4 = E_(nc.sbuf_tensor("W4", [P, 2], BF16))
    G1 = E_(nc.sbuf_tensor("G1", [P, 8 * 512], BF16))
    G2 = E_(nc.sbuf_tensor("G2", [P, 4 * 256], BF16))
    G3 = E_(nc.sbuf_tensor("G3", [P, 2], BF16))
    B1C = E_(nc.sbuf_tensor("B1C", [P, 8], F32))
    B2C = E_(nc.sbuf_tensor("B2C", [P, 4], F32))
    B3C = E_(nc.sbuf_tensor("B3C", [P, 2], F32))
    B4 = E_(nc.sbuf_tensor("B4", [1, 1], F32))
    GB1C = E_(nc.sbuf_tensor("GB1C", [P, 4], F32))
    GB2C = E_(nc.sbuf_tensor("GB2C", [P, 2], F32))
    E0C = E_(nc.sbuf_tensor("E0C", [P, P], F32))
    GB3 = E_(nc.sbuf_tensor("GB3", [1, 1], F32))
    ARC = E_(nc.sbuf_tensor("ARC", [1, 1], F32))
    IDB = E_(nc.sbuf_tensor("IDB", [P, P], BF16))
    IDF = E_(nc.sbuf_tensor("IDF", [P, P], F32))
    MASKC = E_(nc.sbuf_tensor("MASKC", [P, 512], F32))
    ONESC = E_(nc.sbuf_tensor("ONESC", [P, 1], BF16))
    EPSC = E_(nc.sbuf_tensor("EPSC", [P, 1], F32))
    STR = E_(nc.sbuf_tensor("STR", [P, n_tiles], F32))
    STB = E_(nc.sbuf_tensor("STB", [P, n_tiles], F32))
    GCT = E_(nc.sbuf_tensor("GCT", [P, 4 * 8], BF16))
    GCTA = E_(nc.sbuf_tensor("GCTA", [P, 4 * 8], F32))
    GFROW = E_(nc.sbuf_tensor("GFROW", [1, 4], F32))
    ALPHAR = E_(nc.sbuf_tensor("ALPHAR", [1, 4], F32))
    BETA = E_(nc.sbuf_tensor("BETA", [1, 1], F32))
    GS1F = E_(nc.sbuf_tensor("GS1F", [P, 16], BF16))
    GS2F = E_(nc.sbuf_tensor("GS2F", [P, 8], BF16))
    W4F = E_(nc.sbuf_tensor("W4F", [P, 2], F32))
    G3F = E_(nc.sbuf_tensor("G3F", [P, 2], F32))

    cast_plan = []

    def plan_w(dram, sbuf, total_cols):
        v = dram.rearrange("(c p) n -> p c n", p=P)
        nchunk = v.shape[1]
        ncols = v.shape[2]
        per = max(1, 2048 // ncols)
        c0 = 0
        while c0 < nchunk:
            k = min(per, nchunk - c0)
            cast_plan.append((v[:, c0:c0 + k, :],
                              sbuf[:, c0 * ncols:(c0 + k) * ncols], k * ncols))
            c0 += k

    plan_w(wqT, WQ, 8 * DM)
    plan_w(wkT, WK, 8 * DM)
    plan_w(wvT, WV, 8 * DM)
    plan_w(w1_in, W1, 8 * DM)
    plan_w(w2_in, W2, 8 * 512)
    plan_w(w3_in, W3, 4 * 256)
    plan_w(gw1_in, G1, 8 * 512)
    plan_w(gw2_in, G2, 4 * 256)
    n_rounds = len(cast_plan)

    st1 = ExitStack()
    E1 = st1.enter_context
    XB = [E1(nc.sbuf_tensor(f"XB{k}", [P, DM], F32)) for k in range(2)]
    XM = E1(nc.sbuf_tensor("XM", [P, DM], F32))
    SQB = E1(nc.sbuf_tensor("SQB", [P, DM], F32))
    EBUF = E1(nc.sbuf_tensor("EBUF", [P, 2 * DM], F32))
    NM = E1(nc.sbuf_tensor("NM", [P, 1], F32))
    VAR = E1(nc.sbuf_tensor("VAR", [P, 1], F32))
    RSTD = [E1(nc.sbuf_tensor(f"RSTD{k}", [P, 1], F32)) for k in range(2)]
    NMQ = E1(nc.sbuf_tensor("NMQ", [P, 1], F32))
    NMK = E1(nc.sbuf_tensor("NMK", [P, 1], F32))
    TMP1 = E1(nc.sbuf_tensor("TMP1", [P, 1], F32))
    TMPK = E1(nc.sbuf_tensor("TMPK", [P, 1], F32))
    TMP2 = E1(nc.sbuf_tensor("TMP2", [P, 1], F32))
    ZQ = E1(nc.sbuf_tensor("ZQ", [P, NH], F32))
    ZK = E1(nc.sbuf_tensor("ZK", [P, HD], F32))
    HBF = [E1(nc.sbuf_tensor(f"HBF{k}", [P, DM], BF16)) for k in range(2)]
    HFM = [E1(nc.sbuf_tensor(f"HFM{k}", [P, DM], BF16)) for k in range(2)]
    SQTM = [E1(nc.sbuf_tensor(f"SQTM{k}", [P, DM], BF16)) for k in range(2)]
    SKTM = [E1(nc.sbuf_tensor(f"SKTM{k}", [P, DM], BF16)) for k in range(2)]
    VTM = [E1(nc.sbuf_tensor(f"VTM{k}", [P, DM], BF16)) for k in range(2)]
    SQS = [E1(nc.sbuf_tensor(f"SQS{k}", [P, DM], BF16)) for k in range(2)]
    SKS = [E1(nc.sbuf_tensor(f"SKS{k}", [P, DM], BF16)) for k in range(2)]
    VS = [E1(nc.sbuf_tensor(f"VS{k}", [P, DM], BF16)) for k in range(2)]
    TSQ = [E1(nc.sbuf_tensor(f"TSQ{k}", [P, 2 * DM], BF16)) for k in range(2)]
    TSK = [E1(nc.sbuf_tensor(f"TSK{k}", [P, 2 * DM], BF16)) for k in range(2)]
    MBD = [E1(nc.sbuf_tensor(f"MBD{k}", [P, 512], BF16)) for k in range(2)]
    AS = [E1(nc.sbuf_tensor(f"AS{k}", [P, DM], F32)) for k in range(2)]
    X1A = [E1(nc.sbuf_tensor(f"X1A{k}", [P, DM], F32)) for k in range(2)]
    H2A = [E1(nc.sbuf_tensor(f"H2A{k}", [P, DM], BF16)) for k in range(2)]

    p_t1 = E1(nc.psum_tensor("p_t1", [P, DM], BF16))
    p_tsq = E1(nc.psum_tensor("p_tsq", [64, DM], BF16))
    p_q = E1(nc.psum_tensor("p_q", [P, DM], F32))
    p_k = E1(nc.psum_tensor("p_k", [P, DM], F32))
    p_v = E1(nc.psum_tensor("p_v", [P, 512], F32))
    p_s2 = E1(nc.psum_tensor("p_s2", [P, 8], F32))

    sem_names = ("s_wld s_wcast s_cld s_x s_stk s_x1c s_wscr s_acc s_ln s_rstd1 "
                 "s_hbf s_nm s_sm s_tse s_mask s_hfm s_eq s_ek s_vev s_aev "
                 "s_ht s_qq s_qk s_qv s_ts1 s_ts2 s_gram s_app4 "
                 "s_x1l s_st s_rstd2a s_h2a s_gsT s_gct s_gfp s_gfv s_gfa "
                 "s_x1b s_h2b s_h2t s_obf s_of32 s_mmc s_evc s_zt s_tfr "
                 "s_crow s_cbc s_tw s_cast s_x2t s_x2 s_outw s_gse s_vch s_ach s_vc2 s_gch").split()
    SM = {}
    for name in sem_names:
        SM[name] = E_(nc.semaphore(name))
    S = type("S", (), SM)

    def stk_view(dram, i):
        return dram[P * i:P * (i + 1), :].rearrange(
            "(g tl) (h d) -> (tl h) g d", g=16, tl=8, h=16)

    N_CLD = 16

    blk1_cm = nc.Block()
    b1 = blk1_cm.__enter__()

    @b1.sync
    def _(eng):
        for ap, dst in ((identb_in, IDB), (identf_in, IDF), (mask_in, MASKC),
                        (onescol_in, ONESC),
                        (e0_in, E0C),
                        (b1c_in, B1C), (b2c_in, B2C), (b3c_in, B3C),
                        (b4_in, B4), (gb1_in, GB1C), (gb2_in, GB2C),
                        (gb3_in, GB3), (ar_in, ARC), (eps_in, EPSC),
                        (w4c_in, W4F), (gw3c_in, G3F)):
            eng.dma_start(dst[:], ap).then_inc(S.s_cld, 16)
        for r, (src, dst, n) in enumerate(cast_plan):
            eng.wait_ge(S.s_wcast, r)
            eng.dma_start(EBUF[:, 0:n], src).then_inc(S.s_wld, 16)
        for i in range(n_tiles):
            if i >= 2:
                eng.wait_ge(S.s_ln, i - 1)
                eng.wait_ge(S.s_x1c, 16 * (i - 1))
            else:
                eng.wait_ge(S.s_wcast, n_rounds + 1)
            eng.dma_start(XB[i % 2][:], x_in[P * i:P * (i + 1), :]).then_inc(S.s_x, 16)
            if lvl >= 3:
                eng.wait_ge(S.s_wscr, 48 * (i + 1))
                if i >= 2:
                    eng.wait_ge(S.s_ts1, 2 * (i - 1))
                    eng.wait_ge(S.s_ts2, 2 * (i - 1))
                    if lvl >= 5:
                        eng.wait_ge(S.s_app4, 4 * (i - 1))
                eng.dma_start(SQS[i % 2][:], stk_view(sq_scr, i)).then_inc(S.s_stk, 16)
                eng.dma_start(SKS[i % 2][:], stk_view(sk_scr, i)).then_inc(S.s_stk, 16)
                eng.dma_start(VS[i % 2][:], stk_view(v_scr, i)).then_inc(S.s_stk, 16)
        if stage != 'p1':
            for i in range(n_tiles):
                eng.wait_ge(S.s_acc, 16 * (i + 1))
                if i >= 2:
                    eng.wait_ge(S.s_h2a, i - 1)
                eng.dma_start(X1A[i % 2][:],
                              x1_scr[P * i:P * (i + 1), :]).then_inc(S.s_x1l, 16)
        if stage in ('p1', '2a', 'gf'):
            if lvl >= 6:
                eng.wait_ge(S.s_acc, 16 * n_tiles)
            elif lvl == 5:
                eng.wait_ge(S.s_aev, 2 * n_tiles)
            elif lvl == 4:
                if grama:
                    eng.wait_ge(S.s_gram, 4 * n_tiles)
                else:
                    eng.wait_ge(S.s_mask, 4 * n_tiles)
            elif lvl == 3:
                eng.wait_ge(S.s_tse, 4 * n_tiles)
            elif lvl == 2:
                eng.wait_ge(S.s_wscr, 48 * n_tiles)
                import os
                dbg = os.environ.get("KDBG", "")
                if dbg == "xm":
                    eng.wait_ge(S.s_hbf, 1)
                    eng.dma_start(out_d[:, 0:1024], XM[:]).then_inc(S.s_outw, 16)
                    eng.wait_ge(S.s_outw, 16)
                    return
                if dbg == "lnstats":
                    eng.wait_ge(S.s_hbf, 1)
                    eng.dma_start(out_d[:, 0:1], NM[:]).then_inc(S.s_outw, 16)
                    eng.dma_start(out_d[:, 1:2], VAR[:]).then_inc(S.s_outw, 16)
                    eng.dma_start(out_d[:, 2:3], RSTD[0][:]).then_inc(S.s_outw, 16)
                    eng.wait_ge(S.s_outw, 48)
                    return
                if dbg in ("hfm", "hbf"):
                    eng.wait_ge(S.s_hfm, 1)
                    srcb = HFM[0] if dbg == "hfm" else HBF[0]
                    eng.dma_start(out_d.bitcast(BF16)[:, 0:1024], srcb[:]).then_inc(S.s_outw, 16)
                    eng.wait_ge(S.s_outw, 16)
                    return
                if dbg:
                    src = {"sq": sq_scr, "sk": sk_scr, "v": v_scr}[dbg]
                    eng.dma_start(out_d.bitcast(BF16)[:, 0:1024], src).then_inc(S.s_outw, 16)
                    eng.wait_ge(S.s_outw, 16)
                    return
            else:
                eng.wait_ge(S.s_hfm, n_tiles)
            if stage in ('2a', 'gf'):
                eng.wait_ge(S.s_h2a, n_tiles)
                eng.wait_ge(S.s_gse, n_tiles)
            if stage == 'gf':
                eng.wait_ge(S.s_gfa, 7)
                eng.wait_ge(S.s_gfv, 3)
            eng.dma_start(out_d, x1_scr).then_inc(S.s_outw, 16)
            eng.wait_ge(S.s_outw, 16)

    @b1.gpsimd
    def _(eng):
        for i in range(n_tiles):
            eng.wait_ge(S.s_x, 16 * (i + 1))
            eng.dma_start(x1_scr[P * i:P * (i + 1), :],
                          XB[i % 2][:]).then_inc(S.s_x1c, 16)
            if lvl < 2:
                continue
            eng.wait_ge(S.s_sm, 2 * (i + 1))
            eng.wait_ge(S.s_vev, 2 * (i + 1))
            eng.dma_start(sq_scr[P * i:P * (i + 1), :], SQTM[i % 2][:]).then_inc(S.s_wscr, 16)
            eng.dma_start(sk_scr[P * i:P * (i + 1), :], SKTM[i % 2][:]).then_inc(S.s_wscr, 16)
            eng.dma_start(v_scr[P * i:P * (i + 1), :], VTM[i % 2][:]).then_inc(S.s_wscr, 16)
            if lvl < 6:
                continue
            eng.wait_ge(S.s_aev, 2 * (i + 1))
            eng.wait_ge(S.s_x1c, 16 * (i + 1))
            eng.dma_start(
                x1_scr[P * i:P * (i + 1), :].rearrange(
                    "(g tl) (h e) -> (tl h) g e", g=16, tl=8, h=16),
                AS[i % 2][:], accum_op=ADD).then_inc(S.s_acc, 16)

    @b1.tensor
    def _(eng):
        eng.wait_ge(S.s_wcast, n_rounds + 1)
        eng.wait_ge(S.s_cld, 16 * N_CLD)
        for i in range(n_tiles):
            eng.wait_ge(S.s_hbf, i + 1)
            if i >= 1 and lvl >= 3:
                eng.wait_ge(S.s_tse, 4 * i)
            for c in range(8):
                ins = eng.transpose(p_t1[:, P * c:P * (c + 1)],
                                    HBF[i % 2][:, P * c:P * (c + 1)], IDB[:])
                if c == 7:
                    ins.then_inc(S.s_ht, 1)
            if lvl < 2:
                continue
            eng.wait_ge(S.s_hfm, i + 1)
            if i >= 1:
                if lvl >= 4 and not grama:
                    eng.wait_ge(S.s_mask, 4 * i)
                eng.wait_ge(S.s_eq, 2 * i)
            for j in range(8):
                lhs = HFM[i % 2][:, P * j:P * (j + 1)]
                eng.matmul(p_q[:, 0:512], lhs, WQ[:, DM * j:DM * j + 512],
                           start=(j == 0), stop=(j == 7))
                ins = eng.matmul(p_q[:, 512:DM], lhs,
                                 WQ[:, DM * j + 512:DM * (j + 1)],
                                 start=(j == 0), stop=(j == 7))
                if j == 7:
                    ins.then_inc(S.s_qq, 1)
            if i >= 1:
                eng.wait_ge(S.s_aev, 2 * i)
            for j in range(8):
                lhs = HFM[i % 2][:, P * j:P * (j + 1)]
                eng.matmul(p_k[:, 0:512], lhs, WK[:, DM * j:DM * j + 512],
                           start=(j == 0), stop=(j == 7))
                ins = eng.matmul(p_k[:, 512:DM], lhs,
                                 WK[:, DM * j + 512:DM * (j + 1)],
                                 start=(j == 0), stop=(j == 7))
                if j == 7:
                    ins.then_inc(S.s_qk, 1)
            if i >= 1:
                eng.wait_ge(S.s_vev, 2 * i)
            for half in range(2):
                if half == 1:
                    eng.wait_ge(S.s_vev, 2 * i + 1)
                for j in range(8):
                    lhs = HFM[i % 2][:, P * j:P * (j + 1)]
                    ins = eng.matmul(
                        p_v[:], lhs,
                        WV[:, DM * j + 512 * half:DM * j + 512 * half + 512],
                        start=(j == 0), stop=(j == 7))
                    if j == 7:
                        ins.then_inc(S.s_qv, 1)
            if lvl < 3:
                continue
            eng.wait_ge(S.s_stk, 48 * (i + 1))
            for r in range(2):
                if r == 0:
                    if i >= 1:
                        eng.wait_ge(S.s_tse, 4 * i - 2)
                else:
                    eng.wait_ge(S.s_tse, 4 * i + 1)
                for gg in range(8):
                    g = 8 * r + gg
                    ins = eng.transpose(p_tsq[0:64, P * gg:P * (gg + 1)],
                                        SQS[i % 2][:, HD * g:HD * (g + 1)],
                                        IDB[:])
                    if gg == 7:
                        ins.then_inc(S.s_ts1, 1)
            for r in range(2):
                if r == 0:
                    if i >= 1:
                        eng.wait_ge(S.s_tse, 4 * i)
                else:
                    eng.wait_ge(S.s_tse, 4 * i + 3)
                for gg in range(8):
                    g = 8 * r + gg
                    ins = eng.transpose(p_t1[0:64, P * gg:P * (gg + 1)],
                                        SKS[i % 2][:, HD * g:HD * (g + 1)],
                                        IDB[:])
                    if gg == 7:
                        ins.then_inc(S.s_ts2, 1)
            if lvl < 4:
                continue
            if not gramc:
                eng.wait_ge(S.s_tse, 4 * (i + 1))
                eng.wait_ge(S.s_eq, 2 * (i + 1))
                eng.wait_ge(S.s_ek, 2 * (i + 1))
            for r in range(4):
                if r > 0 and not grama:
                    eng.wait_ge(S.s_mask, 4 * i + r)
                if gramd:
                    eng.sem_inc(S.s_gram, 1)
                    continue
                for gg in range(4):
                    g = 4 * r + gg
                    ins = eng.matmul(
                        p_q[:, 512 + P * gg:512 + P * (gg + 1)],
                        TSK[i % 2][:, P * g:P * (g + 1)],
                        TSQ[i % 2][:, P * g:P * (g + 1)],
                        start=True, stop=True)
                    if gg == 3:
                        ins.then_inc(S.s_gram, 1)
                if r > 0 and lvl >= 5:
                    b = r - 1
                    for gg in range(4):
                        g = 4 * b + gg
                        ins = eng.matmul(
                            p_k[:, HD * g:HD * (g + 1)],
                            MBD[b % 2][:, P * gg:P * (gg + 1)],
                            VS[i % 2][:, HD * g:HD * (g + 1)],
                            start=True, stop=True)
                        if gg == 3:
                            ins.then_inc(S.s_app4, 1)
            for b in ((3,) if lvl >= 5 else ()):
                eng.wait_ge(S.s_mask, 4 * i + 4)
                for gg in range(4):
                    g = 4 * b + gg
                    ins = eng.matmul(
                        p_k[:, HD * g:HD * (g + 1)],
                        MBD[b % 2][:, P * gg:P * (gg + 1)],
                        VS[i % 2][:, HD * g:HD * (g + 1)],
                        start=True, stop=True)
                    if gg == 3:
                        ins.then_inc(S.s_app4, 1)
        for i in range(n_tiles if stage != 'p1' else 0):
            eng.wait_ge(S.s_h2a, i + 1)
            if i >= 1:
                eng.wait_ge(S.s_gse, i)
            for c in range(8):
                ins = eng.matmul(p_s2[:, c:c + 1],
                                 H2A[i % 2][:, P * c:P * (c + 1)], ONESC[:],
                                 start=True, stop=True)
                if c == 7:
                    ins.then_inc(S.s_gsT, 1)
        if stage in ('p1', '2a'):
            return
        eng.wait_ge(S.s_gct, 1)
        t = 0
        for oc in range(4):
            if t >= 1:
                eng.wait_ge(S.s_gfa, t)
            for c in range(8):
                gct_c = GCT[:].rearrange("p (s c) -> p c s", c=8)[:, c, :]
                ins = eng.matmul(p_q[:, 0:4], G1[:, 512 * c + P * oc:512 * c + P * (oc + 1)],
                                 gct_c, start=(c == 0), stop=(c == 7))
                if c == 7:
                    ins.then_inc(S.s_gfp, 1)
            t += 1
        for o2 in range(2):
            eng.wait_ge(S.s_gfa, t)
            for c2 in range(4):
                ins = eng.matmul(p_q[:, 0:4],
                                 G2[:, 256 * c2 + P * o2:256 * c2 + P * (o2 + 1)],
                                 GS1F[:, 4 * c2:4 * c2 + 4],
                                 start=(c2 == 0), stop=(c2 == 3))
                if c2 == 3:
                    ins.then_inc(S.s_gfp, 1)
            t += 1
        eng.wait_ge(S.s_gfa, t)
        for c3 in range(2):
            ins = eng.matmul(p_q[0:1, 0:4], G3[:, c3:c3 + 1],
                             GS2F[:, 4 * c3:4 * c3 + 4],
                             start=(c3 == 0), stop=(c3 == 1))
            if c3 == 1:
                ins.then_inc(S.s_gfp, 1)

    @b1.vector
    def _(eng):
        vch = [0]

        def vs(ins):
            ins.then_inc(S.s_vch, 1)
            vch[0] += 1

        def vw():
            eng.wait_ge(S.s_vch, vch[0])

        for r, (src, dst, n) in enumerate(cast_plan):
            eng.wait_ge(S.s_wld, 16 * (r + 1))
            eng.tensor_copy(dst, EBUF[:, 0:n]).then_inc(S.s_wcast, 1)
        eng.wait_ge(S.s_cld, 16 * N_CLD)
        eng.tensor_copy(W4[:], W4F[:])
        eng.tensor_copy(G3[:], G3F[:])
        eng.memset(GCTA[:], 0.0)
        for k_t, t in enumerate(TSQ + TSK):
            ins = eng.memset(t[64:P, :], 0.0)
        ins.then_inc(S.s_wcast, 1)

        for i in range(n_tiles):
            eng.wait_ge(S.s_x, 16 * (i + 1))
            vs(eng.reduce_sum(NM[:], XB[i % 2][:], axis=AX.X, negate=True))
            vw()
            vs(eng.tensor_scalar_mul(NM[:], NM[:], 1.0 / DM))
            vw()
            vs(eng.tensor_scalar_add(XM[:], XB[i % 2][:], NM[:, 0:1]))
            vw()
            vs(eng.tensor_tensor(SQB[:], XM[:], XM[:], op=MUL))
            vw()
            eng.reduce_sum(VAR[:], SQB[:], axis=AX.X).then_inc(S.s_ln, 1)
            eng.wait_ge(S.s_rstd1, i + 1)
            if i >= 2:
                eng.wait_ge(S.s_ht, i - 1)
            eng.tensor_scalar_mul(HBF[i % 2][:], XM[:],
                                  RSTD[i % 2][:, 0:1]).then_inc(S.s_hbf, 1)
            if lvl < 2:
                continue
            eng.wait_ge(S.s_qq, i + 1)
            vs(eng.reduce_max(NMQ[:], p_q[:, 0:512], axis=AX.X, negate=True))
            vs(eng.reduce_max(TMP1[:], p_q[:, 512:DM], axis=AX.X, negate=True))
            vw()
            eng.tensor_tensor(NMQ[:], NMQ[:], TMP1[:], op=MIN).then_inc(S.s_nm, 1)
            eng.wait_ge(S.s_qk, i + 1)
            vs(eng.reduce_max(NMK[:], p_k[:, 0:512], axis=AX.X, negate=True))
            vs(eng.reduce_max(TMPK[:], p_k[:, 512:DM], axis=AX.X, negate=True))
            vw()
            eng.tensor_tensor(NMK[:], NMK[:], TMPK[:], op=MIN).then_inc(S.s_nm, 1)
            eng.wait_ge(S.s_eq, 2 * (i + 1))
            eq = EBUF[:, 0:DM]
            vs(eng.reduce_sum(ZQ[:], eq.rearrange("p (h d) -> p h d", h=NH), axis=AX.X))
            vw()
            vs(eng.reciprocal(ZQ[:], ZQ[:]))
            vw()
            if i >= 2:
                eng.wait_ge(S.s_wscr, 48 * (i - 1))
            eng.tensor_tensor(
                SQTM[i % 2][:].rearrange("p (h d) -> p h d", h=NH),
                eq.rearrange("p (h d) -> p h d", h=NH),
                ZQ[:, :, None].broadcast_to([P, NH, HD]),
                op=MUL).then_inc(S.s_sm, 1)
            eng.wait_ge(S.s_ek, 2 * (i + 1))
            ek = EBUF[:, DM:2 * DM]
            vs(eng.reduce_sum(ZK[:], ek.rearrange("p (h d) -> p d h", h=NH), axis=AX.X))
            vw()
            vs(eng.reciprocal(ZK[:], ZK[:]))
            vw()
            eng.tensor_tensor(
                SKTM[i % 2][:].rearrange("p (h d) -> p h d", h=NH),
                ek.rearrange("p (h d) -> p h d", h=NH),
                ZK[:, None, :].broadcast_to([P, NH, HD]),
                op=MUL).then_inc(S.s_sm, 1)
            if lvl < 3:
                continue
            if i >= 1 and lvl >= 4:
                eng.wait_ge(S.s_gram, 4 * i)
            for r in range(2):
                eng.wait_ge(S.s_ts1, 2 * i + r + 1)
                eng.tensor_copy(TSQ[i % 2][0:64, DM * r:DM * (r + 1)],
                                p_tsq[0:64, :]).then_inc(S.s_tse, 1)
            for r in range(2):
                eng.wait_ge(S.s_ts2, 2 * i + r + 1)
                eng.tensor_copy(TSK[i % 2][0:64, DM * r:DM * (r + 1)],
                                p_t1[0:64, :]).then_inc(S.s_tse, 1)
            if lvl < 4 or grama:
                continue
            for r in range(4):
                eng.wait_ge(S.s_gram, 4 * i + r + 1)
                if r >= 2 and lvl >= 5:
                    eng.wait_ge(S.s_app4, 4 * i + r - 1)
                eng.tensor_tensor(MBD[r % 2][:], p_q[:, 512:DM], MASKC[:],
                                  op=MUL).then_inc(S.s_mask, 1)
        for i in range(n_tiles if stage != 'p1' else 0):
            eng.wait_ge(S.s_x1l, 16 * (i + 1))
            vs(eng.reduce_sum(NM[:], X1A[i % 2][:], axis=AX.X, negate=True))
            vw()
            vs(eng.tensor_scalar_mul(NM[:], NM[:], 1.0 / DM))
            vw()
            vs(eng.tensor_scalar_add(XM[:], X1A[i % 2][:], NM[:, 0:1]))
            vw()
            vs(eng.tensor_tensor(SQB[:], XM[:], XM[:], op=MUL))
            vw()
            eng.reduce_sum(VAR[:], SQB[:], axis=AX.X).then_inc(S.s_st, 1)
            eng.wait_ge(S.s_rstd2a, i + 1)
            vs(eng.tensor_tensor(STB[:, i:i + 1], NM[:], STR[:, i:i + 1], op=MUL))
            vw()
            if i >= 2:
                eng.wait_ge(S.s_gsT, i - 1)
            eng.tensor_scalar(out=H2A[i % 2][:], in0=X1A[i % 2][:],
                              scalar1=STR[:, i:i + 1], scalar2=STB[:, i:i + 1],
                              op0=MUL, op1=ADD).then_inc(S.s_h2a, 1)
            sq = i // NSEQT
            eng.wait_ge(S.s_gsT, i + 1)
            vs(eng.tensor_tensor(GCTA[:, 8 * sq:8 * sq + 8],
                                 GCTA[:, 8 * sq:8 * sq + 8],
                                 p_s2[:], op=ADD))
            vw()
            eng.sem_inc(S.s_gse, 1)
        if stage == 'p1':
            return
        eng.wait_ge(S.s_gse, n_tiles)
        vw()
        eng.tensor_scalar_mul(GCT[:], GCTA[:], 1.0 / (NSEQT * P)).then_inc(S.s_gct, 1)
        if stage == '2a':
            return
        eng.wait_ge(S.s_gfa, 7)
        vs(eng.tensor_scalar(out=ALPHAR[:], in0=GFROW[:], scalar1=1.4, scalar2=-1.0,
                             op0=MUL, op1=ADD))
        vw()
        vs(eng.tensor_scalar(out=ALPHAR[:], in0=ALPHAR[:], scalar1=ARC[0:1, 0:1],
                             scalar2=0.1, op0=MUL, op1=ADD))
        vw()
        eng.tensor_scalar_mul(BETA[:], ARC[0:1, 0:1], 0.6).then_inc(S.s_gfv, 3)

    @b1.scalar
    def _(eng):
        ach = [0]

        def as_(ins):
            ins.then_inc(S.s_ach, 1)
            ach[0] += 1

        def aw():
            eng.wait_ge(S.s_ach, ach[0])

        for i in range(n_tiles):
            eng.wait_ge(S.s_ln, i + 1)
            if i >= 2:
                eng.wait_ge(S.s_hbf, i - 1)
            as_(eng.activation(RSTD[i % 2][:], VAR[:], AF.Ln,
                               bias=EPSC[:, 0:1], scale=1.0 / DM))
            aw()
            eng.activation(RSTD[i % 2][:], RSTD[i % 2][:], AF.Exp,
                           scale=-0.5).then_inc(S.s_rstd1, 1)
            eng.wait_ge(S.s_ht, i + 1)
            if i >= 1:
                eng.wait_ge(S.s_qv, 2 * i)
            eng.copy(HFM[i % 2][:], p_t1[:]).then_inc(S.s_hfm, 1)
            if lvl < 2:
                continue
            eng.wait_ge(S.s_nm, 2 * i + 1)
            if i >= 1:
                eng.wait_ge(S.s_sm, 2 * i)
            eng.activation(EBUF[:, 0:512], p_q[:, 0:512], AF.Exp,
                           bias=NMQ[:, 0:1]).then_inc(S.s_eq, 1)
            eng.activation(EBUF[:, 512:DM], p_q[:, 512:DM], AF.Exp,
                           bias=NMQ[:, 0:1]).then_inc(S.s_eq, 1)
            eng.wait_ge(S.s_nm, 2 * i + 2)
            eng.activation(EBUF[:, DM:DM + 512], p_k[:, 0:512], AF.Exp,
                           bias=NMK[:, 0:1]).then_inc(S.s_ek, 1)
            eng.activation(EBUF[:, DM + 512:2 * DM], p_k[:, 512:DM], AF.Exp,
                           bias=NMK[:, 0:1]).then_inc(S.s_ek, 1)
            eng.wait_ge(S.s_qv, 2 * i + 1)
            if i >= 2:
                eng.wait_ge(S.s_wscr, 48 * (i - 1))
            eng.copy(VTM[i % 2][:, 0:512], p_v[:]).then_inc(S.s_vev, 1)
            eng.wait_ge(S.s_qv, 2 * i + 2)
            eng.copy(VTM[i % 2][:, 512:DM], p_v[:]).then_inc(S.s_vev, 1)
            if lvl < 5:
                continue
            eng.wait_ge(S.s_app4, 4 * i + 2)
            if i >= 2:
                eng.wait_ge(S.s_acc, 16 * (i - 1))
            eng.copy(AS[i % 2][:, 0:512], p_k[:, 0:512]).then_inc(S.s_aev, 1)
            eng.wait_ge(S.s_app4, 4 * i + 4)
            eng.copy(AS[i % 2][:, 512:DM], p_k[:, 512:DM]).then_inc(S.s_aev, 1)
        for i in range(n_tiles if stage != 'p1' else 0):
            eng.wait_ge(S.s_st, i + 1)
            as_(eng.activation(TMP2[:], VAR[:], AF.Ln, bias=EPSC[:, 0:1],
                               scale=1.0 / DM))
            aw()
            eng.activation(STR[:, i:i + 1], TMP2[:], AF.Exp,
                           scale=-0.5).then_inc(S.s_rstd2a, 1)
        if stage in ('p1', '2a'):
            return
        for oc in range(4):
            eng.wait_ge(S.s_gfp, oc + 1)
            eng.activation(GS1F[:, 4 * oc:4 * (oc + 1)], p_q[:, 0:4], AF.Gelu,
                           bias=GB1C[:, oc:oc + 1]).then_inc(S.s_gfa, 1)
        for o2 in range(2):
            eng.wait_ge(S.s_gfp, 4 + o2 + 1)
            eng.activation(GS2F[:, 4 * o2:4 * (o2 + 1)], p_q[:, 0:4], AF.Gelu,
                           bias=GB2C[:, o2:o2 + 1]).then_inc(S.s_gfa, 1)
        eng.wait_ge(S.s_gfp, 7)
        eng.activation(GFROW[:], p_q[0:1, 0:4], AF.Sigmoid,
                       bias=GB3[0:1, 0:1]).then_inc(S.s_gfa, 1)

    blk1_cm.__exit__(None, None, None)
    st1.close()
    E2 = st.enter_context
    X1S = [E2(nc.sbuf_tensor(f"X1S{k}", [P, 4 * DM], F32)) for k in range(2)]
    H2B = E2(nc.sbuf_tensor("H2B", [P, DM], BF16))
    OBF = E2(nc.sbuf_tensor("OBF", [P, 8 * 512], BF16))
    OF32 = E2(nc.sbuf_tensor("OF32", [P, 8 * 512], F32))
    T1B = E2(nc.sbuf_tensor("T1B", [P, 8 * 512], BF16))
    T2B = E2(nc.sbuf_tensor("T2B", [P, 4 * 512], BF16))
    T3B = E2(nc.sbuf_tensor("T3B", [P, 2 * 512], BF16))
    TFR = E2(nc.sbuf_tensor("TFR", [1, 512], F32))
    CROW = E2(nc.sbuf_tensor("CROW", [P, 512], F32))
    TWA = E2(nc.sbuf_tensor("TWA", [P, 512], F32))
    TWB = E2(nc.sbuf_tensor("TWB", [P, 512], F32))
    X2 = [E2(nc.sbuf_tensor(f"X2_{k}", [P, DM], F32)) for k in range(2)]
    pA = E2(nc.psum_tensor("pA", [P, 512], F32))
    pB = E2(nc.psum_tensor("pB", [P, 512], F32))
    pZ = E2(nc.psum_tensor("pZ", [1, 512], F32))
    pC = E2(nc.psum_tensor("pC", [P, 512], F32))
    pH = E2(nc.psum_tensor("pH", [P, DM], BF16))
    pX = E2(nc.psum_tensor("pX", [P, DM], F32))

    def barrier(eng):
        eng.wait_ge(S.s_acc, 16 * n_tiles)
        eng.wait_ge(S.s_h2a, n_tiles)
        eng.wait_ge(S.s_gsT, n_tiles)
        eng.wait_ge(S.s_gfa, 7)
        eng.wait_ge(S.s_gfv, 3)

    if stage in ('p1', '2a', 'gf'):
        st.close()
        return nc, n_tiles
    blk2_cm = nc.Block()
    b2 = blk2_cm.__enter__()

    import os as _os0
    n_cyc = 3 if lvl2b >= 40 else 1
    if _os0.environ.get('KCYC'):
        n_cyc = int(_os0.environ['KCYC'])

    @b2.sync
    def _(eng):
        barrier(eng)
        for s_i in range(n_super):
            for k in range(4):
                if s_i >= 2:
                    if lvl2b >= 45:
                        eng.wait_ge(S.s_x2, 4 * (s_i - 1))
                    else:
                        eng.wait_ge(S.s_h2b, 4 * (s_i - 1))
                eng.dma_start(X1S[s_i % 2][:, DM * k:DM * (k + 1)],
                              x1_scr[P * (4 * s_i + k):P * (4 * s_i + k + 1), :]
                              ).then_inc(S.s_x1b, 16)
        if lvl2b == 50:
            eng.wait_ge(S.s_outw, 16 * n_tiles)
            import os as _os
            if _os.environ.get("KDBG2"):
                eng.dma_start(out_d[0:1, 0:4], GFROW[:]).then_inc(S.s_outw, 16)
                eng.dma_start(out_d[1:2, 0:4], ALPHAR[:]).then_inc(S.s_outw, 16)
                eng.dma_start(out_d[2:3, 0:16], GCT.ap().bitcast(F32)[0:1, 0:16]).then_inc(S.s_outw, 16)
                eng.wait_ge(S.s_outw, 16 * n_tiles + 48)
        else:
            if lvl2b == 0:
                eng.wait_ge(S.s_x1b, 16 * 4 * n_super)
            elif lvl2b == 5:
                eng.wait_ge(S.s_h2b, 4 * n_super)
            elif lvl2b == 6:
                eng.wait_ge(S.s_obf, 4 * n_super)
            elif lvl2b == 7:
                eng.wait_ge(S.s_of32, 4 * n_super)
            elif lvl2b == 10:
                eng.wait_ge(S.s_of32, 4 * n_super)
                eng.wait_ge(S.s_obf, 4 * n_super)
            elif lvl2b == 20:
                eng.wait_ge(S.s_evc, 14 * n_super)
                import os as _os
                if _os.environ.get("KDBG3"):
                    ob = out_d.bitcast(BF16)
                    eng.dma_start(ob[0:P, :], T2B[:]).then_inc(S.s_outw, 16)
                    eng.wait_ge(S.s_outw, 16)
                    return
            elif lvl2b == 30:
                eng.wait_ge(S.s_cbc, n_super)
                import os as _os
                if _os.environ.get("KDBG4"):
                    ob = out_d.bitcast(BF16)
                    eng.dma_start(ob[0:P, 0:1024], T3B[:]).then_inc(S.s_outw, 16)
                    eng.dma_start(out_d[P:P + 1, 0:512], TFR[:]).then_inc(S.s_outw, 16)
                    eng.dma_start(out_d[P + 1:P + 2, 0:512], CROW[0:1, :]).then_inc(S.s_outw, 16)
                    eng.wait_ge(S.s_outw, 48)
                    return
            elif lvl2b == 40:
                eng.wait_ge(S.s_tw, n_cyc * n_super)
                import os as _os
                if _os.environ.get("KDBG5"):
                    eng.wait_ge(S.s_of32, 4 * n_super)
                    for rr in range(4):
                        eng.dma_start(out_d[P * rr:P * (rr + 1), :],
                                      OF32[:, 1024 * rr:1024 * (rr + 1)]).then_inc(S.s_outw, 16)
                    eng.wait_ge(S.s_outw, 64)
                    return
            elif lvl2b == 45:
                eng.wait_ge(S.s_x2, 4 * n_super)
            eng.dma_start(out_d, x1_scr).then_inc(S.s_outw, 16)
            eng.wait_ge(S.s_outw, 16)

    @b2.tensor
    def _(eng):
        barrier(eng)
        n_mm = 0
        n_h2t = 0
        n_x2t = 0
        n_cbc = 0
        n_cast = 0
        n_crow = 0
        n_zt = 0
        for s_i in range(n_super if lvl2b >= 6 else 0):
            for k in range(4):
                eng.wait_ge(S.s_h2b, 4 * s_i + k + 1)
                if n_h2t >= 1:
                    if lvl2b != 7:
                        eng.wait_ge(S.s_obf, n_h2t)
                    if lvl2b != 6:
                        eng.wait_ge(S.s_of32, n_h2t)
                for c in range(8):
                    ins = eng.transpose(pH[:, P * c:P * (c + 1)],
                                        H2B[:, P * c:P * (c + 1)], IDB[:])
                    if c == 7:
                        ins.then_inc(S.s_h2t, 1)
                n_h2t += 1
            for cyc in range(n_cyc if lvl2b >= 20 else 0):
                if cyc == 0:
                    eng.wait_ge(S.s_obf, n_h2t)
                else:
                    eng.wait_ge(S.s_cast, n_cast)
                for (n_o, nj, Wsb, src, in_w) in (
                        (8, 8, W1, OBF, DM), (4, 8, W2, T1B, 512),
                        (2, 4, W3, T2B, 256)):
                    for o in range(n_o):
                        pp = pA if o % 2 == 0 else pB
                        if o == 0:
                            if n_mm >= 1:
                                eng.wait_ge(S.s_evc, n_mm)
                        elif n_mm >= 2:
                            eng.wait_ge(S.s_evc, n_mm - 1)
                        for j in range(nj):
                            ins = eng.matmul(
                                pp[:],
                                Wsb[:, in_w * j + P * o:in_w * j + P * (o + 1)],
                                src[:, 512 * j:512 * (j + 1)],
                                start=(j == 0), stop=(j == nj - 1))
                            if j == nj - 1:
                                ins.then_inc(S.s_mmc, 1)
                        n_mm += 1
                if lvl2b < 30:
                    continue
                eng.wait_ge(S.s_evc, n_mm)
                if n_zt >= 1:
                    eng.wait_ge(S.s_crow, n_crow)
                for j in range(2):
                    ins = eng.matmul(pZ[:], W4[:, j:j + 1],
                                     T3B[:, 512 * j:512 * (j + 1)],
                                     start=(j == 0), stop=(j == 1))
                    if j == 1:
                        ins.then_inc(S.s_zt, 1)
                n_zt += 1
                eng.wait_ge(S.s_crow, n_crow + 1)
                n_crow += 1
                if n_cbc >= 1 and lvl2b >= 40:
                    eng.wait_ge(S.s_tw, n_cbc)
                eng.matmul(pC[:], E0C[:], CROW[:],
                           start=True, stop=True).then_inc(S.s_cbc, 1)
                n_cbc += 1
                if cyc < 2:
                    n_cast += 1
            if lvl2b < 45:
                continue
            eng.wait_ge(S.s_tw, n_cbc)
            for k in range(4):
                if n_x2t >= 1:
                    eng.wait_ge(S.s_x2, n_x2t)
                for c in range(8):
                    ins = eng.transpose(
                        pX[:, P * c:P * (c + 1)],
                        OF32[:, 512 * c + P * k:512 * c + P * (k + 1)], IDF[:])
                    if c == 7:
                        ins.then_inc(S.s_x2t, 1)
                n_x2t += 1

    @b2.vector
    def _(eng):
        barrier(eng)
        vch = [0]

        def vs(ins):
            ins.then_inc(S.s_vc2, 1)
            vch[0] += 1

        def vw():
            eng.wait_ge(S.s_vc2, vch[0])

        n_tw = 0
        n_x2 = 0
        for s_i in range(n_super if lvl2b >= 5 else 0):
            for k in range(4):
                i_tile = 4 * s_i + k
                eng.wait_ge(S.s_x1b, 16 * (4 * s_i + k + 1))
                if (s_i > 0 or k > 0) and lvl2b >= 6:
                    eng.wait_ge(S.s_h2t, 4 * s_i + k)
                eng.tensor_scalar(out=H2B[:],
                                  in0=X1S[s_i % 2][:, DM * k:DM * (k + 1)],
                                  scalar1=STR[:, i_tile:i_tile + 1],
                                  scalar2=STB[:, i_tile:i_tile + 1],
                                  op0=MUL, op1=ADD).then_inc(S.s_h2b, 1)
                if lvl2b < 6 or lvl2b == 7:
                    continue
                eng.wait_ge(S.s_h2t, 4 * s_i + k + 1)
                for c in range(8):
                    ins = eng.tensor_copy(
                        OBF[:, 512 * c + P * k:512 * c + P * (k + 1)],
                        pH[:, P * c:P * (c + 1)])
                    if c == 7:
                        ins.then_inc(S.s_obf, 1)
            for cyc in range(n_cyc if lvl2b >= 40 else 0):
                eng.wait_ge(S.s_cbc, n_tw + 1)
                if cyc == 0:
                    eng.wait_ge(S.s_of32, 4 * (s_i + 1))
                pairs = ((0, 4, ADD, SUB), (1, 5, ADD, SUB),
                         (2, 6, SUB, ADD), (3, 7, SUB, ADD))
                vw()
                last = None
                for (a, b, opa, opb) in pairs:
                    ca = OF32[:, 512 * a:512 * (a + 1)]
                    cb = OF32[:, 512 * b:512 * (b + 1)]
                    vs(eng.tensor_tensor(TWB[:], cb, pC[:], op=MUL))
                    vs(eng.tensor_tensor(TWA[:], ca, pC[:], op=MUL))
                    vw()
                    vs(eng.tensor_tensor(cb, cb, TWA[:], op=opb))
                    vs(eng.tensor_tensor(ca, ca, TWB[:], op=opa))
                    vw()
                    last = True
                eng.sem_inc(S.s_tw, 1)
                n_tw += 1
            for k in range(4 if lvl2b >= 45 else 0):
                eng.wait_ge(S.s_x2t, n_x2 + 1)
                if n_x2 >= 2 and lvl2b == 50:
                    eng.wait_ge(S.s_outw, 16 * (n_x2 - 1))
                eng.tensor_tensor(X2[n_x2 % 2][:], pX[:],
                                  X1S[s_i % 2][:, DM * k:DM * (k + 1)],
                                  op=ADD).then_inc(S.s_x2, 1)
                n_x2 += 1

    @b2.scalar
    def _(eng):
        barrier(eng)
        n_mm = 0
        n_tw = 0
        n_crow = 0
        for s_i in range(n_super if (lvl2b == 7 or lvl2b >= 10) else 0):
            for k in range(4):
                eng.wait_ge(S.s_h2t, 4 * s_i + k + 1)
                if lvl2b >= 10:
                    eng.wait_ge(S.s_obf, 4 * s_i + k + 1)
                if s_i >= 1 and lvl2b >= 45:
                    eng.wait_ge(S.s_x2t, 4 * s_i)
                for c in range(8):
                    ins = eng.copy(
                        OF32[:, 512 * c + P * k:512 * c + P * (k + 1)],
                        pH[:, P * c:P * (c + 1)])
                    if c == 7:
                        ins.then_inc(S.s_of32, 1)
            for cyc in range(n_cyc if lvl2b >= 20 else 0):
                for (n_o, bias_c, dst) in ((8, B1C, T1B), (4, B2C, T2B),
                                           (2, B3C, T3B)):
                    for o in range(n_o):
                        pp = pA if o % 2 == 0 else pB
                        eng.wait_ge(S.s_mmc, n_mm + 1)
                        n_mm += 1
                        eng.activation(dst[:, 512 * o:512 * (o + 1)], pp[:],
                                       AF.Gelu,
                                       bias=bias_c[:, o:o + 1]).then_inc(S.s_evc, 1)
                if lvl2b < 30:
                    continue
                eng.wait_ge(S.s_zt, n_cyc * s_i + cyc + 1)
                if n_crow >= 1:
                    eng.wait_ge(S.s_crow, n_crow)
                eng.activation(TFR[:], pZ[:], AF.Sigmoid,
                               bias=B4[0:1, 0:1]).then_inc(S.s_tfr, 1)
                n_crow += 1
                if cyc < 2 and lvl2b >= 40:
                    eng.wait_ge(S.s_tw, n_cyc * s_i + cyc + 1)
                    for c in range(8):
                        ins = eng.copy(OBF[:, 512 * c:512 * (c + 1)],
                                       OF32[:, 512 * c:512 * (c + 1)])
                        if c == 7:
                            ins.then_inc(S.s_cast, 1)
            if lvl2b == 50:
                for k in range(4):
                    ki = 4 * s_i + k
                    eng.wait_ge(S.s_x2, ki + 1)
                    eng.dma_start(out_d[P * ki:P * (ki + 1), :],
                                  X2[ki % 2][:]).then_inc(S.s_outw, 16)

    @b2.gpsimd
    def _(eng):
        barrier(eng)
        eng.memset(CROW[:], 0.0).then_inc(S.s_gch, 1)
        eng.wait_ge(S.s_gch, 1)
        for s_i in range(n_super if lvl2b >= 30 else 0):
            seq = s_i // (n_super // n_seq)
            for cyc in range(n_cyc):
                n = n_cyc * s_i + cyc
                eng.wait_ge(S.s_tfr, n + 1)
                eng.tensor_scalar(out=CROW[0:1, :], in0=TFR[:],
                                  scalar1=BETA[0:1, 0:1],
                                  scalar2=ALPHAR[0:1, seq:seq + 1],
                                  op0=MUL, op1=ADD).then_inc(S.s_crow, 1)

    blk2_cm.__exit__(None, None, None)
    st.close()
    return nc, n_tiles


def _host_inputs(inputs, shard):
    f32 = np.float32
    bf = ml_dtypes.bfloat16
    d = {}
    d["x"] = np.ascontiguousarray(shard.reshape(-1, DM))
    d["wqT"] = np.ascontiguousarray(inputs["L0_attn_W_Q"].T)
    d["wkT"] = np.ascontiguousarray(inputs["L0_attn_W_K"].T)
    d["wvT"] = np.ascontiguousarray(inputs["L0_attn_W_V"].T)
    d["w1"] = np.ascontiguousarray(inputs["L0_ffn_cn_w1"])
    d["w2"] = np.ascontiguousarray(inputs["L0_ffn_cn_w2"])
    d["w3"] = np.ascontiguousarray(inputs["L0_ffn_cn_w3"])
    d["w4c"] = np.ascontiguousarray(inputs["L0_ffn_cn_w4"].reshape(2, 128).T)
    d["gw1"] = np.ascontiguousarray(inputs["L0_ffn_gc_w1"])
    d["gw2"] = np.ascontiguousarray(inputs["L0_ffn_gc_w2"])
    d["gw3c"] = np.ascontiguousarray(inputs["L0_ffn_gc_w3"].reshape(2, 128).T)
    d["b1c"] = np.ascontiguousarray(inputs["L0_ffn_cn_b1"].reshape(8, 128).T)
    d["b2c"] = np.ascontiguousarray(inputs["L0_ffn_cn_b2"].reshape(4, 128).T)
    d["b3c"] = np.ascontiguousarray(inputs["L0_ffn_cn_b3"].reshape(2, 128).T)
    d["b4"] = np.ascontiguousarray(inputs["L0_ffn_cn_b4"].reshape(1, 1))
    d["gb1"] = np.ascontiguousarray(inputs["L0_ffn_gc_b1"].reshape(4, 128).T)
    d["gb2"] = np.ascontiguousarray(inputs["L0_ffn_gc_b2"].reshape(2, 128).T)
    d["gb3"] = np.ascontiguousarray(inputs["L0_ffn_gc_b3"].reshape(1, 1))
    d["ar"] = np.ascontiguousarray(inputs["L0_ffn_adaptive_range"].reshape(1, 1))
    d["identb"] = np.eye(P, dtype=bf)
    d["identf"] = np.eye(P, dtype=f32)
    mask1 = np.kron(np.eye(8, dtype=f32), np.ones((16, 16), dtype=f32))
    d["maskc"] = np.tile(mask1, (1, 4)).astype(f32)
    d["onescol"] = np.ones((P, 1), dtype=bf)
    e0 = np.zeros((P, P), dtype=f32)
    e0[0, :] = 1.0
    d["e0c"] = e0
    d["epsc"] = np.full((P, 1), 1e-5, dtype=f32)
    for k in d:
        if d[k].dtype == np.float64:
            raise ValueError(k)
    return d


def kernel(**inputs):
    x = np.asarray(inputs["L0_attn_input_seq_embs"])
    n_cores = 8
    shards = np.split(x, n_cores, axis=0)
    nc, n_tiles = build(n_seq=4)
    in_maps = [_host_inputs(inputs, s) for s in shards]
    res = run_bass_kernel_spmd(nc, in_maps, core_ids=list(range(n_cores)))
    outs = [np.asarray(r["out"], dtype=np.float32) for r in res.results]
    return np.concatenate(outs, axis=0).reshape(32, 2048, 1024)


# version 03
# speedup vs baseline: 1.0526x; 1.0442x over previous
import numpy as np
import ml_dtypes
import concourse.bass as bass
import concourse.mybir as mybir
from concourse.bass_utils import run_bass_kernel_spmd
from concourse.alu_op_type import AluOpType
from contextlib import ExitStack

F32 = mybir.dt.float32
BF16 = mybir.dt.bfloat16
AF = mybir.ActivationFunctionType
AX = mybir.AxisListType
ADD = AluOpType.add
SUB = AluOpType.subtract
MUL = AluOpType.mult
MIN = AluOpType.min

P = 128
DM = 1024
NH = 16
HD = 64
NSEQT = 16


def build(n_seq: int, nseqt: int = None, stage: str = 'full'):
    lvl2b = {'2b0': 0, '2bb': 5, '2bh1': 6, '2bh2': 7, '2bh': 10, '2bc': 20, '2bz': 30, '2bt': 40, '2bx': 45}.get(stage, 50)
    if stage in ('2b0', '2bb', '2bh1', '2bh2', '2bh', '2bc', '2bz', '2bt', '2bx'):
        stage = 'full'
    lvl = {'ln': 1, 'qkv': 2, 'stk': 3, 'gramA': 35, 'gramB': 36, 'gramC': 37, 'gramD': 38, 'gram': 4, 'appl': 5}.get(stage, 6)
    gramb = lvl in (36, 37)
    gramc = lvl == 37
    gramd = lvl == 38
    if lvl in (35, 36, 37, 38):
        lvl = 4
        grama = True
    else:
        grama = False
    if lvl < 4 or grama or lvl == 4:
        pass
    if stage not in ('full', '2a', 'gf'):
        stage = 'p1'
    global NSEQT
    if nseqt is not None:
        NSEQT = nseqt
    n_tiles = n_seq * NSEQT
    NT = n_tiles * P
    n_super = n_tiles // 4
    nc = bass.Bass("TRN2", target_bir_lowering=False)

    x_in = nc.dram_tensor("x", [NT, DM], F32, kind="ExternalInput").ap()
    wqT = nc.dram_tensor("wqT", [DM, DM], F32, kind="ExternalInput").ap()
    wkT = nc.dram_tensor("wkT", [DM, DM], F32, kind="ExternalInput").ap()
    wvT = nc.dram_tensor("wvT", [DM, DM], F32, kind="ExternalInput").ap()
    w1_in = nc.dram_tensor("w1", [DM, DM], F32, kind="ExternalInput").ap()
    w2_in = nc.dram_tensor("w2", [DM, 512], F32, kind="ExternalInput").ap()
    w3_in = nc.dram_tensor("w3", [512, 256], F32, kind="ExternalInput").ap()
    w4c_in = nc.dram_tensor("w4c", [P, 2], F32, kind="ExternalInput").ap()
    gw1_in = nc.dram_tensor("gw1", [DM, 512], F32, kind="ExternalInput").ap()
    gw2_in = nc.dram_tensor("gw2", [512, 256], F32, kind="ExternalInput").ap()
    gw3c_in = nc.dram_tensor("gw3c", [P, 2], F32, kind="ExternalInput").ap()
    b1c_in = nc.dram_tensor("b1c", [P, 8], F32, kind="ExternalInput").ap()
    b2c_in = nc.dram_tensor("b2c", [P, 4], F32, kind="ExternalInput").ap()
    b3c_in = nc.dram_tensor("b3c", [P, 2], F32, kind="ExternalInput").ap()
    b4_in = nc.dram_tensor("b4", [1, 1], F32, kind="ExternalInput").ap()
    gb1_in = nc.dram_tensor("gb1", [P, 4], F32, kind="ExternalInput").ap()
    gb2_in = nc.dram_tensor("gb2", [P, 2], F32, kind="ExternalInput").ap()
    gb3_in = nc.dram_tensor("gb3", [1, 1], F32, kind="ExternalInput").ap()
    ar_in = nc.dram_tensor("ar", [1, 1], F32, kind="ExternalInput").ap()
    identb_in = nc.dram_tensor("identb", [P, P], BF16, kind="ExternalInput").ap()
    identf_in = nc.dram_tensor("identf", [P, P], F32, kind="ExternalInput").ap()
    mask_in = nc.dram_tensor("maskc", [P, 512], F32, kind="ExternalInput").ap()
    onescol_in = nc.dram_tensor("onescol", [P, 1], BF16, kind="ExternalInput").ap()
    e0_in = nc.dram_tensor("e0c", [P, P], F32, kind="ExternalInput").ap()
    eps_in = nc.dram_tensor("epsc", [P, 1], F32, kind="ExternalInput").ap()

    out_d = nc.dram_tensor("out", [NT, DM], F32, kind="ExternalOutput").ap()
    sq_scr = nc.dram_tensor("sq_scr", [NT, DM], BF16, kind="Internal").ap()
    sk_scr = nc.dram_tensor("sk_scr", [NT, DM], BF16, kind="Internal").ap()
    v_scr = nc.dram_tensor("v_scr", [NT, DM], BF16, kind="Internal").ap()
    x1_scr = nc.dram_tensor("x1_scr", [NT, DM], F32, kind="Internal").ap()

    st = ExitStack()
    E_ = st.enter_context

    WQ = E_(nc.sbuf_tensor("WQ", [P, 8 * DM], BF16))
    WK = E_(nc.sbuf_tensor("WK", [P, 8 * DM], BF16))
    WV = E_(nc.sbuf_tensor("WV", [P, 8 * DM], BF16))
    W1 = E_(nc.sbuf_tensor("W1", [P, 8 * DM], BF16))
    W2 = E_(nc.sbuf_tensor("W2", [P, 8 * 512], BF16))
    W3 = E_(nc.sbuf_tensor("W3", [P, 4 * 256], BF16))
    W4 = E_(nc.sbuf_tensor("W4", [P, 2], BF16))
    G1 = E_(nc.sbuf_tensor("G1", [P, 8 * 512], BF16))
    G2 = E_(nc.sbuf_tensor("G2", [P, 4 * 256], BF16))
    G3 = E_(nc.sbuf_tensor("G3", [P, 2], BF16))
    B1C = E_(nc.sbuf_tensor("B1C", [P, 8], F32))
    B2C = E_(nc.sbuf_tensor("B2C", [P, 4], F32))
    B3C = E_(nc.sbuf_tensor("B3C", [P, 2], F32))
    B4 = E_(nc.sbuf_tensor("B4", [1, 1], F32))
    GB1C = E_(nc.sbuf_tensor("GB1C", [P, 4], F32))
    GB2C = E_(nc.sbuf_tensor("GB2C", [P, 2], F32))
    E0C = E_(nc.sbuf_tensor("E0C", [P, P], F32))
    GB3 = E_(nc.sbuf_tensor("GB3", [1, 1], F32))
    ARC = E_(nc.sbuf_tensor("ARC", [1, 1], F32))
    IDB = E_(nc.sbuf_tensor("IDB", [P, P], BF16))
    IDF = E_(nc.sbuf_tensor("IDF", [P, P], F32))
    MASKC = E_(nc.sbuf_tensor("MASKC", [P, 512], F32))
    ONESC = E_(nc.sbuf_tensor("ONESC", [P, 1], BF16))
    EPSC = E_(nc.sbuf_tensor("EPSC", [P, 1], F32))
    STR = E_(nc.sbuf_tensor("STR", [P, n_tiles], F32))
    STB = E_(nc.sbuf_tensor("STB", [P, n_tiles], F32))
    GCT = E_(nc.sbuf_tensor("GCT", [P, 4 * 8], BF16))
    GCTA = E_(nc.sbuf_tensor("GCTA", [P, 4 * 8], F32))
    GFROW = E_(nc.sbuf_tensor("GFROW", [1, 4], F32))
    ALPHAR = E_(nc.sbuf_tensor("ALPHAR", [1, 4], F32))
    BETA = E_(nc.sbuf_tensor("BETA", [1, 1], F32))
    B4H = E_(nc.sbuf_tensor("B4H", [1, 1], F32))
    GB3H = E_(nc.sbuf_tensor("GB3H", [1, 1], F32))
    GS1F = E_(nc.sbuf_tensor("GS1F", [P, 16], BF16))
    GS2F = E_(nc.sbuf_tensor("GS2F", [P, 8], BF16))
    W4F = E_(nc.sbuf_tensor("W4F", [P, 2], F32))
    G3F = E_(nc.sbuf_tensor("G3F", [P, 2], F32))

    cast_plan = []

    def plan_w(dram, sbuf, total_cols):
        v = dram.rearrange("(c p) n -> p c n", p=P)
        nchunk = v.shape[1]
        ncols = v.shape[2]
        per = max(1, 2048 // ncols)
        c0 = 0
        while c0 < nchunk:
            k = min(per, nchunk - c0)
            cast_plan.append((v[:, c0:c0 + k, :],
                              sbuf[:, c0 * ncols:(c0 + k) * ncols], k * ncols))
            c0 += k

    plan_w(wqT, WQ, 8 * DM)
    plan_w(wkT, WK, 8 * DM)
    plan_w(wvT, WV, 8 * DM)
    plan_w(w1_in, W1, 8 * DM)
    plan_w(w2_in, W2, 8 * 512)
    plan_w(w3_in, W3, 4 * 256)
    plan_w(gw1_in, G1, 8 * 512)
    plan_w(gw2_in, G2, 4 * 256)
    n_rounds = len(cast_plan)

    st1 = ExitStack()
    E1 = st1.enter_context
    XB = [E1(nc.sbuf_tensor(f"XB{k}", [P, DM], F32)) for k in range(2)]
    XM = E1(nc.sbuf_tensor("XM", [P, DM], F32))
    SQB = E1(nc.sbuf_tensor("SQB", [P, DM], F32))
    EBUF = E1(nc.sbuf_tensor("EBUF", [P, 2 * DM], F32))
    NM = E1(nc.sbuf_tensor("NM", [P, 1], F32))
    VAR = E1(nc.sbuf_tensor("VAR", [P, 1], F32))
    RSTD = [E1(nc.sbuf_tensor(f"RSTD{k}", [P, 1], F32)) for k in range(2)]
    NMQ = E1(nc.sbuf_tensor("NMQ", [P, 1], F32))
    NMK = E1(nc.sbuf_tensor("NMK", [P, 1], F32))
    TMP1 = E1(nc.sbuf_tensor("TMP1", [P, 1], F32))
    TMPK = E1(nc.sbuf_tensor("TMPK", [P, 1], F32))
    NM2A = E1(nc.sbuf_tensor("NM2A", [P, 1], F32))
    VAR2A = E1(nc.sbuf_tensor("VAR2A", [P, 1], F32))
    XM2A = E1(nc.sbuf_tensor("XM2A", [P, DM], F32))
    SQ2A = E1(nc.sbuf_tensor("SQ2A", [P, DM], F32))
    TMP2 = E1(nc.sbuf_tensor("TMP2", [P, 1], F32))
    ZQ = E1(nc.sbuf_tensor("ZQ", [P, NH], F32))
    ZK = E1(nc.sbuf_tensor("ZK", [P, HD], F32))
    HBF = [E1(nc.sbuf_tensor(f"HBF{k}", [P, DM], BF16)) for k in range(2)]
    HFM = [E1(nc.sbuf_tensor(f"HFM{k}", [P, DM], BF16)) for k in range(2)]
    SQTM = [E1(nc.sbuf_tensor(f"SQTM{k}", [P, DM], BF16)) for k in range(2)]
    SKTM = [E1(nc.sbuf_tensor(f"SKTM{k}", [P, DM], BF16)) for k in range(2)]
    VTM = [E1(nc.sbuf_tensor(f"VTM{k}", [P, DM], BF16)) for k in range(2)]
    SQS = [E1(nc.sbuf_tensor(f"SQS{k}", [P, DM], BF16)) for k in range(2)]
    SKS = [E1(nc.sbuf_tensor(f"SKS{k}", [P, DM], BF16)) for k in range(2)]
    VS = [E1(nc.sbuf_tensor(f"VS{k}", [P, DM], BF16)) for k in range(2)]
    TSQ = [E1(nc.sbuf_tensor(f"TSQ{k}", [P, 2 * DM], BF16)) for k in range(2)]
    TSK = [E1(nc.sbuf_tensor(f"TSK{k}", [P, 2 * DM], BF16)) for k in range(2)]
    MBD = [E1(nc.sbuf_tensor(f"MBD{k}", [P, 512], BF16)) for k in range(2)]
    AS = [E1(nc.sbuf_tensor(f"AS{k}", [P, DM], F32)) for k in range(2)]
    X1A = [E1(nc.sbuf_tensor(f"X1A{k}", [P, DM], F32)) for k in range(2)]
    H2A = [E1(nc.sbuf_tensor(f"H2A{k}", [P, DM], BF16)) for k in range(2)]

    p_t1 = E1(nc.psum_tensor("p_t1", [P, DM], BF16))
    p_tsq = E1(nc.psum_tensor("p_tsq", [64, DM], BF16))
    p_q = E1(nc.psum_tensor("p_q", [P, DM], F32))
    p_k = E1(nc.psum_tensor("p_k", [P, DM], F32))
    p_v = E1(nc.psum_tensor("p_v", [P, 512], F32))
    p_s2 = E1(nc.psum_tensor("p_s2", [P, 8], F32))

    sem_names = ("s_wld s_wcast s_cld s_x s_stk s_x1c s_wscr s_acc s_ln s_rstd1 "
                 "s_hbf s_nm s_sm s_tse s_mask s_hfm s_eq s_ek s_vev s_aev "
                 "s_ht s_qq s_qk s_qv s_ts1 s_ts2 s_gram s_app4 "
                 "s_x1l s_st s_rstd2a s_h2a s_gsT s_gct s_gfp s_gfv s_gfa "
                 "s_x1b s_h2b s_h2t s_obf s_of32 s_mmc s_evc s_zt s_tfr "
                 "s_crow s_cbc s_tw s_cast s_x2t s_x2 s_outw s_gse s_vch s_ach s_vc2 s_gch s_twp").split()
    SM = {}
    for name in sem_names:
        SM[name] = E_(nc.semaphore(name))
    S = type("S", (), SM)

    def stk_view(dram, i):
        return dram[P * i:P * (i + 1), :].rearrange(
            "(g tl) (h d) -> (tl h) g d", g=16, tl=8, h=16)

    N_CLD = 16

    blk1_cm = nc.Block()
    b1 = blk1_cm.__enter__()

    @b1.sync
    def _(eng):
        for ap, dst in ((identb_in, IDB), (identf_in, IDF), (mask_in, MASKC),
                        (onescol_in, ONESC),
                        (e0_in, E0C),
                        (b1c_in, B1C), (b2c_in, B2C), (b3c_in, B3C),
                        (b4_in, B4), (gb1_in, GB1C), (gb2_in, GB2C),
                        (gb3_in, GB3), (ar_in, ARC), (eps_in, EPSC),
                        (w4c_in, W4F), (gw3c_in, G3F)):
            eng.dma_start(dst[:], ap).then_inc(S.s_cld, 16)
        for r, (src, dst, n) in enumerate(cast_plan):
            eng.wait_ge(S.s_wcast, r)
            eng.dma_start(EBUF[:, 0:n], src).then_inc(S.s_wld, 16)

        def load_2a(j):
            eng.wait_ge(S.s_acc, 16 * (j + 1))
            if j >= 2:
                eng.wait_ge(S.s_h2a, j - 1)
            eng.dma_start(X1A[j % 2][:],
                          x1_scr[P * j:P * (j + 1), :]).then_inc(S.s_x1l, 16)

        for i in range(n_tiles):
            if i >= 2:
                eng.wait_ge(S.s_ln, i - 1)
                eng.wait_ge(S.s_x1c, 16 * (i - 1))
            else:
                eng.wait_ge(S.s_wcast, n_rounds + 1)
            eng.dma_start(XB[i % 2][:], x_in[P * i:P * (i + 1), :]).then_inc(S.s_x, 16)
            if lvl >= 3:
                eng.wait_ge(S.s_wscr, 48 * (i + 1))
                if i >= 2:
                    eng.wait_ge(S.s_ts1, 2 * (i - 1))
                    eng.wait_ge(S.s_ts2, 2 * (i - 1))
                    if lvl >= 5:
                        eng.wait_ge(S.s_app4, 4 * (i - 1))
                eng.dma_start(SQS[i % 2][:], stk_view(sq_scr, i)).then_inc(S.s_stk, 16)
                eng.dma_start(SKS[i % 2][:], stk_view(sk_scr, i)).then_inc(S.s_stk, 16)
                eng.dma_start(VS[i % 2][:], stk_view(v_scr, i)).then_inc(S.s_stk, 16)
            if stage != 'p1' and i >= 3:
                load_2a(i - 3)
        if stage != 'p1':
            for j in range(max(0, n_tiles - 3), n_tiles):
                load_2a(j)
        if stage in ('p1', '2a', 'gf'):
            if lvl >= 6:
                eng.wait_ge(S.s_acc, 16 * n_tiles)
            elif lvl == 5:
                eng.wait_ge(S.s_aev, 2 * n_tiles)
            elif lvl == 4:
                if grama:
                    eng.wait_ge(S.s_gram, 4 * n_tiles)
                else:
                    eng.wait_ge(S.s_mask, 4 * n_tiles)
            elif lvl == 3:
                eng.wait_ge(S.s_tse, 4 * n_tiles)
            elif lvl == 2:
                eng.wait_ge(S.s_wscr, 48 * n_tiles)
                import os
                dbg = os.environ.get("KDBG", "")
                if dbg == "xm":
                    eng.wait_ge(S.s_hbf, 1)
                    eng.dma_start(out_d[:, 0:1024], XM[:]).then_inc(S.s_outw, 16)
                    eng.wait_ge(S.s_outw, 16)
                    return
                if dbg == "lnstats":
                    eng.wait_ge(S.s_hbf, 1)
                    eng.dma_start(out_d[:, 0:1], NM[:]).then_inc(S.s_outw, 16)
                    eng.dma_start(out_d[:, 1:2], VAR[:]).then_inc(S.s_outw, 16)
                    eng.dma_start(out_d[:, 2:3], RSTD[0][:]).then_inc(S.s_outw, 16)
                    eng.wait_ge(S.s_outw, 48)
                    return
                if dbg in ("hfm", "hbf"):
                    eng.wait_ge(S.s_hfm, 1)
                    srcb = HFM[0] if dbg == "hfm" else HBF[0]
                    eng.dma_start(out_d.bitcast(BF16)[:, 0:1024], srcb[:]).then_inc(S.s_outw, 16)
                    eng.wait_ge(S.s_outw, 16)
                    return
                if dbg:
                    src = {"sq": sq_scr, "sk": sk_scr, "v": v_scr}[dbg]
                    eng.dma_start(out_d.bitcast(BF16)[:, 0:1024], src).then_inc(S.s_outw, 16)
                    eng.wait_ge(S.s_outw, 16)
                    return
            else:
                eng.wait_ge(S.s_hfm, n_tiles)
            if stage in ('2a', 'gf'):
                eng.wait_ge(S.s_h2a, n_tiles)
                eng.wait_ge(S.s_gse, n_tiles)
            if stage == 'gf':
                eng.wait_ge(S.s_gfa, 7)
                eng.wait_ge(S.s_gfv, 3)
            eng.dma_start(out_d, x1_scr).then_inc(S.s_outw, 16)
            eng.wait_ge(S.s_outw, 16)

    @b1.gpsimd
    def _(eng):
        for i in range(n_tiles):
            eng.wait_ge(S.s_x, 16 * (i + 1))
            eng.dma_start(x1_scr[P * i:P * (i + 1), :],
                          XB[i % 2][:]).then_inc(S.s_x1c, 16)
            if lvl < 2:
                continue
            eng.wait_ge(S.s_sm, 2 * i + 1)
            eng.dma_start(sq_scr[P * i:P * (i + 1), :], SQTM[i % 2][:]).then_inc(S.s_wscr, 16)
            eng.wait_ge(S.s_sm, 2 * i + 2)
            eng.dma_start(sk_scr[P * i:P * (i + 1), :], SKTM[i % 2][:]).then_inc(S.s_wscr, 16)
            eng.wait_ge(S.s_vev, 2 * (i + 1))
            eng.dma_start(v_scr[P * i:P * (i + 1), :], VTM[i % 2][:]).then_inc(S.s_wscr, 16)
            if lvl < 6:
                continue
            eng.wait_ge(S.s_aev, 2 * (i + 1))
            eng.wait_ge(S.s_x1c, 16 * (i + 1))
            eng.dma_start(
                x1_scr[P * i:P * (i + 1), :].rearrange(
                    "(g tl) (h e) -> (tl h) g e", g=16, tl=8, h=16),
                AS[i % 2][:], accum_op=ADD).then_inc(S.s_acc, 16)

    @b1.tensor
    def _(eng):
        eng.wait_ge(S.s_wcast, n_rounds + 1)
        eng.wait_ge(S.s_cld, 16 * N_CLD)

        def pe_gsum(j):
            eng.wait_ge(S.s_h2a, j + 1)
            if j >= 1:
                eng.wait_ge(S.s_gse, j)
            for c in range(8):
                ins = eng.matmul(p_s2[:, c:c + 1],
                                 H2A[j % 2][:, P * c:P * (c + 1)], ONESC[:],
                                 start=True, stop=True)
                if c == 7:
                    ins.then_inc(S.s_gsT, 1)

        for i in range(n_tiles):
            eng.wait_ge(S.s_hbf, i + 1)
            if i >= 1 and lvl >= 3:
                eng.wait_ge(S.s_tse, 4 * i)
            for c in range(8):
                ins = eng.transpose(p_t1[:, P * c:P * (c + 1)],
                                    HBF[i % 2][:, P * c:P * (c + 1)], IDB[:])
                if c == 7:
                    ins.then_inc(S.s_ht, 1)
            if lvl < 2:
                continue
            eng.wait_ge(S.s_hfm, i + 1)
            if i >= 1:
                if lvl >= 4 and not grama:
                    eng.wait_ge(S.s_mask, 4 * i)
                eng.wait_ge(S.s_eq, 2 * i)
            for j in range(8):
                lhs = HFM[i % 2][:, P * j:P * (j + 1)]
                eng.matmul(p_q[:, 0:512], lhs, WQ[:, DM * j:DM * j + 512],
                           start=(j == 0), stop=(j == 7))
                ins = eng.matmul(p_q[:, 512:DM], lhs,
                                 WQ[:, DM * j + 512:DM * (j + 1)],
                                 start=(j == 0), stop=(j == 7))
                if j == 7:
                    ins.then_inc(S.s_qq, 1)
            if i >= 1:
                eng.wait_ge(S.s_aev, 2 * i)
            for j in range(8):
                lhs = HFM[i % 2][:, P * j:P * (j + 1)]
                eng.matmul(p_k[:, 0:512], lhs, WK[:, DM * j:DM * j + 512],
                           start=(j == 0), stop=(j == 7))
                ins = eng.matmul(p_k[:, 512:DM], lhs,
                                 WK[:, DM * j + 512:DM * (j + 1)],
                                 start=(j == 0), stop=(j == 7))
                if j == 7:
                    ins.then_inc(S.s_qk, 1)
            if i >= 1:
                eng.wait_ge(S.s_vev, 2 * i)
            for half in range(2):
                if half == 1:
                    eng.wait_ge(S.s_vev, 2 * i + 1)
                for j in range(8):
                    lhs = HFM[i % 2][:, P * j:P * (j + 1)]
                    ins = eng.matmul(
                        p_v[:], lhs,
                        WV[:, DM * j + 512 * half:DM * j + 512 * half + 512],
                        start=(j == 0), stop=(j == 7))
                    if j == 7:
                        ins.then_inc(S.s_qv, 1)
            if lvl < 3:
                continue
            eng.wait_ge(S.s_stk, 48 * (i + 1))
            for r in range(2):
                if r == 0:
                    if i >= 1:
                        eng.wait_ge(S.s_tse, 4 * i - 2)
                else:
                    eng.wait_ge(S.s_tse, 4 * i + 1)
                for gg in range(8):
                    g = 8 * r + gg
                    ins = eng.transpose(p_tsq[0:64, P * gg:P * (gg + 1)],
                                        SQS[i % 2][:, HD * g:HD * (g + 1)],
                                        IDB[:])
                    if gg == 7:
                        ins.then_inc(S.s_ts1, 1)
            for r in range(2):
                if r == 0:
                    if i >= 1:
                        eng.wait_ge(S.s_tse, 4 * i)
                else:
                    eng.wait_ge(S.s_tse, 4 * i + 3)
                for gg in range(8):
                    g = 8 * r + gg
                    ins = eng.transpose(p_t1[0:64, P * gg:P * (gg + 1)],
                                        SKS[i % 2][:, HD * g:HD * (g + 1)],
                                        IDB[:])
                    if gg == 7:
                        ins.then_inc(S.s_ts2, 1)
            if lvl < 4:
                continue
            if not gramc:
                eng.wait_ge(S.s_tse, 4 * (i + 1))
                eng.wait_ge(S.s_eq, 2 * (i + 1))
                eng.wait_ge(S.s_ek, 2 * (i + 1))
            for r in range(4):
                if r > 0 and not grama:
                    eng.wait_ge(S.s_mask, 4 * i + r)
                if gramd:
                    eng.sem_inc(S.s_gram, 1)
                    continue
                for gg in range(4):
                    g = 4 * r + gg
                    ins = eng.matmul(
                        p_q[:, 512 + P * gg:512 + P * (gg + 1)],
                        TSK[i % 2][:, P * g:P * (g + 1)],
                        TSQ[i % 2][:, P * g:P * (g + 1)],
                        start=True, stop=True)
                    if gg == 3:
                        ins.then_inc(S.s_gram, 1)
                if r > 0 and lvl >= 5:
                    b = r - 1
                    for gg in range(4):
                        g = 4 * b + gg
                        ins = eng.matmul(
                            p_k[:, HD * g:HD * (g + 1)],
                            MBD[b % 2][:, P * gg:P * (gg + 1)],
                            VS[i % 2][:, HD * g:HD * (g + 1)],
                            start=True, stop=True)
                        if gg == 3:
                            ins.then_inc(S.s_app4, 1)
            for b in ((3,) if lvl >= 5 else ()):
                eng.wait_ge(S.s_mask, 4 * i + 4)
                for gg in range(4):
                    g = 4 * b + gg
                    ins = eng.matmul(
                        p_k[:, HD * g:HD * (g + 1)],
                        MBD[b % 2][:, P * gg:P * (gg + 1)],
                        VS[i % 2][:, HD * g:HD * (g + 1)],
                        start=True, stop=True)
                    if gg == 3:
                        ins.then_inc(S.s_app4, 1)
            if stage != 'p1' and lvl >= 6 and i >= 3:
                pe_gsum(i - 3)
        for j in (range(max(0, n_tiles - 3), n_tiles)
                  if (stage != 'p1' and lvl >= 6) else ()):
            pe_gsum(j)
        if stage in ('p1', '2a'):
            return
        eng.wait_ge(S.s_gct, 1)
        t = 0
        for oc in range(4):
            if t >= 1:
                eng.wait_ge(S.s_gfa, t)
            for c in range(8):
                gct_c = GCT[:].rearrange("p (s c) -> p c s", c=8)[:, c, :]
                ins = eng.matmul(p_q[:, 0:4], G1[:, 512 * c + P * oc:512 * c + P * (oc + 1)],
                                 gct_c, start=(c == 0), stop=(c == 7))
                if c == 7:
                    ins.then_inc(S.s_gfp, 1)
            t += 1
        for o2 in range(2):
            eng.wait_ge(S.s_gfa, t)
            for c2 in range(4):
                ins = eng.matmul(p_q[:, 0:4],
                                 G2[:, 256 * c2 + P * o2:256 * c2 + P * (o2 + 1)],
                                 GS1F[:, 4 * c2:4 * c2 + 4],
                                 start=(c2 == 0), stop=(c2 == 3))
                if c2 == 3:
                    ins.then_inc(S.s_gfp, 1)
            t += 1
        eng.wait_ge(S.s_gfa, t)
        for c3 in range(2):
            ins = eng.matmul(p_q[0:1, 0:4], G3[:, c3:c3 + 1],
                             GS2F[:, 4 * c3:4 * c3 + 4],
                             start=(c3 == 0), stop=(c3 == 1))
            if c3 == 1:
                ins.then_inc(S.s_gfp, 1)

    @b1.vector
    def _(eng):
        vch = [0]

        def vs(ins):
            ins.then_inc(S.s_vch, 1)
            vch[0] += 1

        def vw():
            eng.wait_ge(S.s_vch, vch[0])

        for r, (src, dst, n) in enumerate(cast_plan):
            eng.wait_ge(S.s_wld, 16 * (r + 1))
            eng.tensor_copy(dst, EBUF[:, 0:n]).then_inc(S.s_wcast, 1)
        eng.wait_ge(S.s_cld, 16 * N_CLD)
        eng.tensor_copy(W4[:], W4F[:])
        eng.tensor_copy(G3[:], G3F[:])
        eng.memset(GCTA[:], 0.0)
        eng.tensor_scalar_mul(B4H[:], B4[:], 0.5)
        eng.tensor_scalar_mul(GB3H[:], GB3[:], 0.5)
        for k_t, t in enumerate(TSQ + TSK):
            ins = eng.memset(t[64:P, :], 0.0)
        ins.then_inc(S.s_wcast, 1)

        def dve_2a(j):
            eng.wait_ge(S.s_x1l, 16 * (j + 1))
            vs(eng.reduce_sum(NM2A[:], X1A[j % 2][:], axis=AX.X, negate=True))
            vw()
            vs(eng.tensor_scalar_mul(NM2A[:], NM2A[:], 1.0 / DM))
            vw()
            vs(eng.tensor_scalar_add(XM2A[:], X1A[j % 2][:], NM2A[:, 0:1]))
            vw()
            vs(eng.tensor_tensor(SQ2A[:], XM2A[:], XM2A[:], op=MUL))
            vw()
            eng.reduce_sum(VAR2A[:], SQ2A[:], axis=AX.X).then_inc(S.s_st, 1)
            eng.wait_ge(S.s_rstd2a, j + 1)
            vs(eng.tensor_tensor(STB[:, j:j + 1], NM2A[:], STR[:, j:j + 1], op=MUL))
            vw()
            if j >= 2:
                eng.wait_ge(S.s_gsT, j - 1)
            eng.tensor_scalar(out=H2A[j % 2][:], in0=X1A[j % 2][:],
                              scalar1=STR[:, j:j + 1], scalar2=STB[:, j:j + 1],
                              op0=MUL, op1=ADD).then_inc(S.s_h2a, 1)
            sq = j // NSEQT
            eng.wait_ge(S.s_gsT, j + 1)
            vs(eng.tensor_tensor(GCTA[:, 8 * sq:8 * sq + 8],
                                 GCTA[:, 8 * sq:8 * sq + 8],
                                 p_s2[:], op=ADD))
            vw()
            eng.sem_inc(S.s_gse, 1)

        for i in range(n_tiles):
            eng.wait_ge(S.s_x, 16 * (i + 1))
            vs(eng.reduce_sum(NM[:], XB[i % 2][:], axis=AX.X, negate=True))
            vw()
            vs(eng.tensor_scalar_mul(NM[:], NM[:], 1.0 / DM))
            vw()
            vs(eng.tensor_scalar_add(XM[:], XB[i % 2][:], NM[:, 0:1]))
            vw()
            vs(eng.tensor_tensor(SQB[:], XM[:], XM[:], op=MUL))
            vw()
            eng.reduce_sum(VAR[:], SQB[:], axis=AX.X).then_inc(S.s_ln, 1)
            eng.wait_ge(S.s_rstd1, i + 1)
            if i >= 2:
                eng.wait_ge(S.s_ht, i - 1)
            eng.tensor_scalar_mul(HBF[i % 2][:], XM[:],
                                  RSTD[i % 2][:, 0:1]).then_inc(S.s_hbf, 1)
            if lvl < 2:
                continue
            eng.wait_ge(S.s_qq, i + 1)
            vs(eng.reduce_max(NMQ[:], p_q[:, 0:512], axis=AX.X, negate=True))
            vs(eng.reduce_max(TMP1[:], p_q[:, 512:DM], axis=AX.X, negate=True))
            vw()
            eng.tensor_tensor(NMQ[:], NMQ[:], TMP1[:], op=MIN).then_inc(S.s_nm, 1)
            eng.wait_ge(S.s_qk, i + 1)
            vs(eng.reduce_max(NMK[:], p_k[:, 0:512], axis=AX.X, negate=True))
            vs(eng.reduce_max(TMPK[:], p_k[:, 512:DM], axis=AX.X, negate=True))
            vw()
            eng.tensor_tensor(NMK[:], NMK[:], TMPK[:], op=MIN).then_inc(S.s_nm, 1)
            eng.wait_ge(S.s_eq, 2 * (i + 1))
            eq = EBUF[:, 0:DM]
            vs(eng.reduce_sum(ZQ[:], eq.rearrange("p (h d) -> p h d", h=NH), axis=AX.X))
            vw()
            vs(eng.reciprocal(ZQ[:], ZQ[:]))
            vw()
            if i >= 2:
                eng.wait_ge(S.s_wscr, 48 * (i - 1))
            eng.tensor_tensor(
                SQTM[i % 2][:].rearrange("p (h d) -> p h d", h=NH),
                eq.rearrange("p (h d) -> p h d", h=NH),
                ZQ[:, :, None].broadcast_to([P, NH, HD]),
                op=MUL).then_inc(S.s_sm, 1)
            eng.wait_ge(S.s_ek, 2 * (i + 1))
            ek = EBUF[:, DM:2 * DM]
            vs(eng.reduce_sum(ZK[:], ek.rearrange("p (h d) -> p d h", h=NH), axis=AX.X))
            vw()
            vs(eng.reciprocal(ZK[:], ZK[:]))
            vw()
            eng.tensor_tensor(
                SKTM[i % 2][:].rearrange("p (h d) -> p h d", h=NH),
                ek.rearrange("p (h d) -> p h d", h=NH),
                ZK[:, None, :].broadcast_to([P, NH, HD]),
                op=MUL).then_inc(S.s_sm, 1)
            if lvl < 3:
                continue
            if i >= 1 and lvl >= 4:
                eng.wait_ge(S.s_gram, 4 * i)
            for r in range(2):
                eng.wait_ge(S.s_ts1, 2 * i + r + 1)
                eng.tensor_copy(TSQ[i % 2][0:64, DM * r:DM * (r + 1)],
                                p_tsq[0:64, :]).then_inc(S.s_tse, 1)
            for r in range(2):
                eng.wait_ge(S.s_ts2, 2 * i + r + 1)
                eng.tensor_copy(TSK[i % 2][0:64, DM * r:DM * (r + 1)],
                                p_t1[0:64, :]).then_inc(S.s_tse, 1)
            if lvl < 4 or grama:
                continue
            for r in range(4):
                eng.wait_ge(S.s_gram, 4 * i + r + 1)
                if r >= 2 and lvl >= 5:
                    eng.wait_ge(S.s_app4, 4 * i + r - 1)
                eng.tensor_tensor(MBD[r % 2][:], p_q[:, 512:DM], MASKC[:],
                                  op=MUL).then_inc(S.s_mask, 1)
            if stage != 'p1' and lvl >= 6 and i >= 3:
                dve_2a(i - 3)
        for j in (range(max(0, n_tiles - 3), n_tiles) if stage != 'p1' else ()):
            dve_2a(j)
        if stage == 'p1':
            return
        eng.wait_ge(S.s_gse, n_tiles)
        vw()
        eng.tensor_scalar_mul(GCT[:], GCTA[:], 1.0 / (NSEQT * P)).then_inc(S.s_gct, 1)
        if stage == '2a':
            return
        eng.wait_ge(S.s_gfa, 7)
        vs(eng.tensor_scalar_mul(ALPHAR[:], GFROW[:], 0.7))
        vw()
        vs(eng.tensor_scalar(out=ALPHAR[:], in0=ALPHAR[:], scalar1=ARC[0:1, 0:1],
                             scalar2=0.1, op0=MUL, op1=ADD))
        vw()
        eng.tensor_scalar_mul(BETA[:], ARC[0:1, 0:1], 0.3).then_inc(S.s_gfv, 3)

    @b1.scalar
    def _(eng):
        ach = [0]

        def as_(ins):
            ins.then_inc(S.s_ach, 1)
            ach[0] += 1

        def aw():
            eng.wait_ge(S.s_ach, ach[0])

        def act_2a(j):
            eng.wait_ge(S.s_st, j + 1)
            as_(eng.activation(TMP2[:], VAR2A[:], AF.Ln, bias=EPSC[:, 0:1],
                               scale=1.0 / DM))
            aw()
            eng.activation(STR[:, j:j + 1], TMP2[:], AF.Exp,
                           scale=-0.5).then_inc(S.s_rstd2a, 1)

        for i in range(n_tiles):
            eng.wait_ge(S.s_ln, i + 1)
            if i >= 2:
                eng.wait_ge(S.s_hbf, i - 1)
            as_(eng.activation(RSTD[i % 2][:], VAR[:], AF.Ln,
                               bias=EPSC[:, 0:1], scale=1.0 / DM))
            aw()
            eng.activation(RSTD[i % 2][:], RSTD[i % 2][:], AF.Exp,
                           scale=-0.5).then_inc(S.s_rstd1, 1)
            eng.wait_ge(S.s_ht, i + 1)
            if i >= 1:
                eng.wait_ge(S.s_qv, 2 * i)
            eng.copy(HFM[i % 2][:], p_t1[:]).then_inc(S.s_hfm, 1)
            if lvl < 2:
                continue
            eng.wait_ge(S.s_nm, 2 * i + 1)
            if i >= 1:
                eng.wait_ge(S.s_sm, 2 * i)
            eng.activation(EBUF[:, 0:512], p_q[:, 0:512], AF.Exp,
                           bias=NMQ[:, 0:1]).then_inc(S.s_eq, 1)
            eng.activation(EBUF[:, 512:DM], p_q[:, 512:DM], AF.Exp,
                           bias=NMQ[:, 0:1]).then_inc(S.s_eq, 1)
            eng.wait_ge(S.s_nm, 2 * i + 2)
            eng.activation(EBUF[:, DM:DM + 512], p_k[:, 0:512], AF.Exp,
                           bias=NMK[:, 0:1]).then_inc(S.s_ek, 1)
            eng.activation(EBUF[:, DM + 512:2 * DM], p_k[:, 512:DM], AF.Exp,
                           bias=NMK[:, 0:1]).then_inc(S.s_ek, 1)
            eng.wait_ge(S.s_qv, 2 * i + 1)
            if i >= 2:
                eng.wait_ge(S.s_wscr, 48 * (i - 1))
            eng.copy(VTM[i % 2][:, 0:512], p_v[:]).then_inc(S.s_vev, 1)
            eng.wait_ge(S.s_qv, 2 * i + 2)
            eng.copy(VTM[i % 2][:, 512:DM], p_v[:]).then_inc(S.s_vev, 1)
            if lvl < 5:
                continue
            eng.wait_ge(S.s_app4, 4 * i + 2)
            if i >= 2:
                eng.wait_ge(S.s_acc, 16 * (i - 1))
            eng.copy(AS[i % 2][:, 0:512], p_k[:, 0:512]).then_inc(S.s_aev, 1)
            eng.wait_ge(S.s_app4, 4 * i + 4)
            eng.copy(AS[i % 2][:, 512:DM], p_k[:, 512:DM]).then_inc(S.s_aev, 1)
            if stage != 'p1' and lvl >= 6 and i >= 3:
                act_2a(i - 3)
        for j in (range(max(0, n_tiles - 3), n_tiles) if stage != 'p1' else ()):
            act_2a(j)
        if stage in ('p1', '2a'):
            return
        for oc in range(4):
            eng.wait_ge(S.s_gfp, oc + 1)
            eng.activation(GS1F[:, 4 * oc:4 * (oc + 1)], p_q[:, 0:4], AF.Gelu,
                           bias=GB1C[:, oc:oc + 1]).then_inc(S.s_gfa, 1)
        for o2 in range(2):
            eng.wait_ge(S.s_gfp, 4 + o2 + 1)
            eng.activation(GS2F[:, 4 * o2:4 * (o2 + 1)], p_q[:, 0:4], AF.Gelu,
                           bias=GB2C[:, o2:o2 + 1]).then_inc(S.s_gfa, 1)
        eng.wait_ge(S.s_gfp, 7)
        eng.activation(GFROW[:], p_q[0:1, 0:4], AF.Tanh,
                       bias=GB3H[0:1, 0:1], scale=0.5).then_inc(S.s_gfa, 1)

    blk1_cm.__exit__(None, None, None)
    st1.close()
    E2 = st.enter_context
    X1S = [E2(nc.sbuf_tensor(f"X1S{k}", [P, 4 * DM], F32)) for k in range(2)]
    H2B = E2(nc.sbuf_tensor("H2B", [P, DM], BF16))
    OBF = E2(nc.sbuf_tensor("OBF", [P, 8 * 512], BF16))
    OF32 = E2(nc.sbuf_tensor("OF32", [P, 8 * 512], F32))
    T1B = E2(nc.sbuf_tensor("T1B", [P, 8 * 512], BF16))
    T2B = E2(nc.sbuf_tensor("T2B", [P, 4 * 512], BF16))
    T3B = E2(nc.sbuf_tensor("T3B", [P, 2 * 512], BF16))
    TFR = E2(nc.sbuf_tensor("TFR", [1, 512], F32))
    CROW = E2(nc.sbuf_tensor("CROW", [P, 512], F32))
    TWA = E2(nc.sbuf_tensor("TWA", [P, 512], F32))
    TWB = E2(nc.sbuf_tensor("TWB", [P, 512], F32))
    X2 = [E2(nc.sbuf_tensor(f"X2_{k}", [P, DM], F32)) for k in range(2)]
    pA = E2(nc.psum_tensor("pA", [P, 512], F32))
    pB = E2(nc.psum_tensor("pB", [P, 512], F32))
    pZ = E2(nc.psum_tensor("pZ", [1, 512], F32))
    pC = E2(nc.psum_tensor("pC", [P, 512], F32))
    pH = E2(nc.psum_tensor("pH", [P, DM], BF16))
    pX = E2(nc.psum_tensor("pX", [P, DM], F32))

    def barrier(eng):
        eng.wait_ge(S.s_wcast, n_rounds + 1)
        eng.wait_ge(S.s_acc, 16 * n_tiles)
        eng.wait_ge(S.s_h2a, n_tiles)
        eng.wait_ge(S.s_gsT, n_tiles)
        eng.wait_ge(S.s_gfa, 7)
        eng.wait_ge(S.s_gfv, 3)

    if stage in ('p1', '2a', 'gf'):
        st.close()
        return nc, n_tiles
    blk2_cm = nc.Block()
    b2 = blk2_cm.__enter__()

    import os as _os0
    n_cyc = 3 if lvl2b >= 40 else 1
    if _os0.environ.get('KCYC'):
        n_cyc = int(_os0.environ['KCYC'])

    @b2.sync
    def _(eng):
        barrier(eng)
        for s_i in range(n_super):
            for k in range(4):
                if s_i >= 2:
                    if lvl2b >= 45:
                        eng.wait_ge(S.s_x2, 4 * (s_i - 1))
                    else:
                        eng.wait_ge(S.s_h2b, 4 * (s_i - 1))
                eng.dma_start(X1S[s_i % 2][:, DM * k:DM * (k + 1)],
                              x1_scr[P * (4 * s_i + k):P * (4 * s_i + k + 1), :]
                              ).then_inc(S.s_x1b, 16)
        if lvl2b == 50:
            eng.wait_ge(S.s_outw, 16 * n_tiles)
            import os as _os
            if _os.environ.get("KDBG2"):
                eng.dma_start(out_d[0:1, 0:4], GFROW[:]).then_inc(S.s_outw, 16)
                eng.dma_start(out_d[1:2, 0:4], ALPHAR[:]).then_inc(S.s_outw, 16)
                eng.dma_start(out_d[2:3, 0:16], GCT.ap().bitcast(F32)[0:1, 0:16]).then_inc(S.s_outw, 16)
                eng.wait_ge(S.s_outw, 16 * n_tiles + 48)
        else:
            if lvl2b == 0:
                eng.wait_ge(S.s_x1b, 16 * 4 * n_super)
            elif lvl2b == 5:
                eng.wait_ge(S.s_h2b, 4 * n_super)
            elif lvl2b == 6:
                eng.wait_ge(S.s_obf, 4 * n_super)
            elif lvl2b == 7:
                eng.wait_ge(S.s_of32, 4 * n_super)
            elif lvl2b == 10:
                eng.wait_ge(S.s_of32, 4 * n_super)
                eng.wait_ge(S.s_obf, 4 * n_super)
            elif lvl2b == 20:
                eng.wait_ge(S.s_evc, 14 * n_super)
                import os as _os
                if _os.environ.get("KDBG3"):
                    ob = out_d.bitcast(BF16)
                    eng.dma_start(ob[0:P, :], T2B[:]).then_inc(S.s_outw, 16)
                    eng.wait_ge(S.s_outw, 16)
                    return
            elif lvl2b == 30:
                eng.wait_ge(S.s_cbc, n_super)
                import os as _os
                if _os.environ.get("KDBG4"):
                    ob = out_d.bitcast(BF16)
                    eng.dma_start(ob[0:P, 0:1024], T3B[:]).then_inc(S.s_outw, 16)
                    eng.dma_start(out_d[P:P + 1, 0:512], TFR[:]).then_inc(S.s_outw, 16)
                    eng.dma_start(out_d[P + 1:P + 2, 0:512], CROW[0:1, :]).then_inc(S.s_outw, 16)
                    eng.wait_ge(S.s_outw, 48)
                    return
            elif lvl2b == 40:
                eng.wait_ge(S.s_tw, n_cyc * n_super)
                import os as _os
                if _os.environ.get("KDBG5"):
                    eng.wait_ge(S.s_of32, 4 * n_super)
                    for rr in range(4):
                        eng.dma_start(out_d[P * rr:P * (rr + 1), :],
                                      OF32[:, 1024 * rr:1024 * (rr + 1)]).then_inc(S.s_outw, 16)
                    eng.wait_ge(S.s_outw, 64)
                    return
            elif lvl2b == 45:
                eng.wait_ge(S.s_x2, 4 * n_super)
            eng.dma_start(out_d, x1_scr).then_inc(S.s_outw, 16)
            eng.wait_ge(S.s_outw, 16)

    @b2.tensor
    def _(eng):
        barrier(eng)
        n_mm = 0
        n_h2t = 0
        n_x2t = 0
        n_cbc = 0
        n_cast = 0
        n_crow = 0
        n_zt = 0
        for s_i in range(n_super if lvl2b >= 6 else 0):
            for k in range(4):
                eng.wait_ge(S.s_h2b, 4 * s_i + k + 1)
                if n_h2t >= 1:
                    if lvl2b != 7:
                        eng.wait_ge(S.s_obf, n_h2t)
                    if lvl2b != 6:
                        eng.wait_ge(S.s_of32, n_h2t)
                for c in range(8):
                    ins = eng.transpose(pH[:, P * c:P * (c + 1)],
                                        H2B[:, P * c:P * (c + 1)], IDB[:])
                    if c == 7:
                        ins.then_inc(S.s_h2t, 1)
                n_h2t += 1
            for cyc in range(n_cyc if lvl2b >= 20 else 0):
                if cyc == 0:
                    eng.wait_ge(S.s_obf, n_h2t)
                jorder = (0, 4, 1, 5, 2, 6, 3, 7) if cyc > 0 else tuple(range(8))
                for (n_o, nj, Wsb, src, in_w) in (
                        (8, 8, W1, OBF, DM), (4, 8, W2, T1B, 512),
                        (2, 4, W3, T2B, 256)):
                    for o in range(n_o):
                        pp = pA if o % 2 == 0 else pB
                        if o == 0:
                            if n_mm >= 1:
                                eng.wait_ge(S.s_evc, n_mm)
                        elif n_mm >= 2:
                            eng.wait_ge(S.s_evc, n_mm - 1)
                        js = jorder if nj == 8 and src is OBF else range(nj)
                        for j_i, j in enumerate(js):
                            if (cyc > 0 and src is OBF and o == 0
                                    and lvl2b >= 40 and j_i % 2 == 0):
                                eng.wait_ge(S.s_cast,
                                            8 * s_i + 4 * (cyc - 1) + j_i // 2 + 1)
                            ins = eng.matmul(
                                pp[:],
                                Wsb[:, in_w * j + P * o:in_w * j + P * (o + 1)],
                                src[:, 512 * j:512 * (j + 1)],
                                start=(j_i == 0), stop=(j_i == nj - 1))
                            if j_i == nj - 1:
                                ins.then_inc(S.s_mmc, 1)
                        n_mm += 1
                if lvl2b < 30:
                    continue
                eng.wait_ge(S.s_evc, n_mm)
                if n_zt >= 1:
                    eng.wait_ge(S.s_crow, n_crow)
                for j in range(2):
                    ins = eng.matmul(pZ[:], W4[:, j:j + 1],
                                     T3B[:, 512 * j:512 * (j + 1)],
                                     start=(j == 0), stop=(j == 1))
                    if j == 1:
                        ins.then_inc(S.s_zt, 1)
                n_zt += 1
                eng.wait_ge(S.s_crow, n_crow + 1)
                n_crow += 1
                if n_cbc >= 1 and lvl2b >= 40:
                    eng.wait_ge(S.s_tw, n_cbc)
                eng.matmul(pC[:], E0C[:], CROW[:],
                           start=True, stop=True).then_inc(S.s_cbc, 1)
                n_cbc += 1
                if cyc < 2:
                    n_cast += 1
            if lvl2b < 45:
                continue
            eng.wait_ge(S.s_tw, n_cbc)
            for k in range(4):
                if n_x2t >= 1:
                    eng.wait_ge(S.s_x2, n_x2t)
                for c in range(8):
                    ins = eng.transpose(
                        pX[:, P * c:P * (c + 1)],
                        OF32[:, 512 * c + P * k:512 * c + P * (k + 1)], IDF[:])
                    if c == 7:
                        ins.then_inc(S.s_x2t, 1)
                n_x2t += 1

    @b2.vector
    def _(eng):
        barrier(eng)
        vch = [0]

        def vs(ins):
            ins.then_inc(S.s_vc2, 1)
            vch[0] += 1

        def vw():
            eng.wait_ge(S.s_vc2, vch[0])

        n_tw = 0
        n_x2 = 0
        for s_i in range(n_super if lvl2b >= 5 else 0):
            for k in range(4):
                i_tile = 4 * s_i + k
                eng.wait_ge(S.s_x1b, 16 * (4 * s_i + k + 1))
                if (s_i > 0 or k > 0) and lvl2b >= 6:
                    eng.wait_ge(S.s_h2t, 4 * s_i + k)
                eng.tensor_scalar(out=H2B[:],
                                  in0=X1S[s_i % 2][:, DM * k:DM * (k + 1)],
                                  scalar1=STR[:, i_tile:i_tile + 1],
                                  scalar2=STB[:, i_tile:i_tile + 1],
                                  op0=MUL, op1=ADD).then_inc(S.s_h2b, 1)
                if lvl2b < 6 or lvl2b == 7:
                    continue
                eng.wait_ge(S.s_h2t, 4 * s_i + k + 1)
                for c in range(8):
                    ins = eng.tensor_copy(
                        OBF[:, 512 * c + P * k:512 * c + P * (k + 1)],
                        pH[:, P * c:P * (c + 1)])
                    if c == 7:
                        ins.then_inc(S.s_obf, 1)
            for cyc in range(n_cyc if lvl2b >= 40 else 0):
                eng.wait_ge(S.s_cbc, n_tw + 1)
                if cyc == 0:
                    eng.wait_ge(S.s_of32, 4 * (s_i + 1))
                pairs = ((0, 4, ADD, SUB), (1, 5, ADD, SUB),
                         (2, 6, SUB, ADD), (3, 7, SUB, ADD))
                vw()
                for (a, b, opa, opb) in pairs:
                    ca = OF32[:, 512 * a:512 * (a + 1)]
                    cb = OF32[:, 512 * b:512 * (b + 1)]
                    vs(eng.tensor_tensor(TWB[:], cb, pC[:], op=MUL))
                    vs(eng.tensor_tensor(TWA[:], ca, pC[:], op=MUL))
                    vw()
                    vs(eng.tensor_tensor(cb, cb, TWA[:], op=opb))
                    vs(eng.tensor_tensor(ca, ca, TWB[:], op=opa))
                    vw()
                    eng.sem_inc(S.s_twp, 1)
                eng.sem_inc(S.s_tw, 1)
                n_tw += 1
            for k in range(4 if lvl2b >= 45 else 0):
                eng.wait_ge(S.s_x2t, n_x2 + 1)
                if n_x2 >= 2 and lvl2b == 50:
                    eng.wait_ge(S.s_outw, 16 * (n_x2 - 1))
                eng.tensor_tensor(X2[n_x2 % 2][:], pX[:],
                                  X1S[s_i % 2][:, DM * k:DM * (k + 1)],
                                  op=ADD).then_inc(S.s_x2, 1)
                n_x2 += 1

    @b2.scalar
    def _(eng):
        barrier(eng)
        n_mm = 0
        n_tw = 0
        n_crow = 0
        for s_i in range(n_super if (lvl2b == 7 or lvl2b >= 10) else 0):
            for k in range(4):
                eng.wait_ge(S.s_h2t, 4 * s_i + k + 1)
                if lvl2b >= 10:
                    eng.wait_ge(S.s_obf, 4 * s_i + k + 1)
                if s_i >= 1 and lvl2b >= 45:
                    eng.wait_ge(S.s_x2t, 4 * s_i)
                for c in range(8):
                    ins = eng.copy(
                        OF32[:, 512 * c + P * k:512 * c + P * (k + 1)],
                        pH[:, P * c:P * (c + 1)])
                    if c == 7:
                        ins.then_inc(S.s_of32, 1)
            for cyc in range(n_cyc if lvl2b >= 20 else 0):
                for (n_o, bias_c, dst) in ((8, B1C, T1B), (4, B2C, T2B),
                                           (2, B3C, T3B)):
                    for o in range(n_o):
                        pp = pA if o % 2 == 0 else pB
                        eng.wait_ge(S.s_mmc, n_mm + 1)
                        n_mm += 1
                        eng.activation(dst[:, 512 * o:512 * (o + 1)], pp[:],
                                       AF.Gelu,
                                       bias=bias_c[:, o:o + 1]).then_inc(S.s_evc, 1)
                if lvl2b < 30:
                    continue
                eng.wait_ge(S.s_zt, n_cyc * s_i + cyc + 1)
                if n_crow >= 1:
                    eng.wait_ge(S.s_crow, n_crow)
                eng.activation(TFR[:], pZ[:], AF.Tanh,
                               bias=B4H[0:1, 0:1], scale=0.5).then_inc(S.s_tfr, 1)
                n_crow += 1
                if cyc < 2 and lvl2b >= 40:
                    for p_i, (aa, bb) in enumerate(((0, 4), (1, 5), (2, 6), (3, 7))):
                        eng.wait_ge(S.s_twp, 4 * (n_cyc * s_i + cyc) + p_i + 1)
                        eng.copy(OBF[:, 512 * aa:512 * (aa + 1)],
                                 OF32[:, 512 * aa:512 * (aa + 1)])
                        eng.copy(OBF[:, 512 * bb:512 * (bb + 1)],
                                 OF32[:, 512 * bb:512 * (bb + 1)]).then_inc(S.s_cast, 1)
            if lvl2b == 50:
                for k in range(4):
                    ki = 4 * s_i + k
                    eng.wait_ge(S.s_x2, ki + 1)
                    eng.dma_start(out_d[P * ki:P * (ki + 1), :],
                                  X2[ki % 2][:]).then_inc(S.s_outw, 16)

    @b2.gpsimd
    def _(eng):
        barrier(eng)
        eng.memset(CROW[:], 0.0).then_inc(S.s_gch, 1)
        eng.wait_ge(S.s_gch, 1)
        for s_i in range(n_super if lvl2b >= 30 else 0):
            seq = s_i // (n_super // n_seq)
            for cyc in range(n_cyc):
                n = n_cyc * s_i + cyc
                eng.wait_ge(S.s_tfr, n + 1)
                eng.tensor_scalar(out=CROW[0:1, :], in0=TFR[:],
                                  scalar1=BETA[0:1, 0:1],
                                  scalar2=ALPHAR[0:1, seq:seq + 1],
                                  op0=MUL, op1=ADD).then_inc(S.s_crow, 1)

    blk2_cm.__exit__(None, None, None)
    st.close()
    return nc, n_tiles


def _host_inputs(inputs, shard):
    f32 = np.float32
    bf = ml_dtypes.bfloat16
    d = {}
    d["x"] = np.ascontiguousarray(shard.reshape(-1, DM))
    d["wqT"] = np.ascontiguousarray(inputs["L0_attn_W_Q"].T)
    d["wkT"] = np.ascontiguousarray(inputs["L0_attn_W_K"].T)
    d["wvT"] = np.ascontiguousarray(inputs["L0_attn_W_V"].T)
    d["w1"] = np.ascontiguousarray(inputs["L0_ffn_cn_w1"])
    d["w2"] = np.ascontiguousarray(inputs["L0_ffn_cn_w2"])
    d["w3"] = np.ascontiguousarray(inputs["L0_ffn_cn_w3"])
    d["w4c"] = np.ascontiguousarray(inputs["L0_ffn_cn_w4"].reshape(2, 128).T)
    d["gw1"] = np.ascontiguousarray(inputs["L0_ffn_gc_w1"])
    d["gw2"] = np.ascontiguousarray(inputs["L0_ffn_gc_w2"])
    d["gw3c"] = np.ascontiguousarray(inputs["L0_ffn_gc_w3"].reshape(2, 128).T)
    d["b1c"] = np.ascontiguousarray(inputs["L0_ffn_cn_b1"].reshape(8, 128).T)
    d["b2c"] = np.ascontiguousarray(inputs["L0_ffn_cn_b2"].reshape(4, 128).T)
    d["b3c"] = np.ascontiguousarray(inputs["L0_ffn_cn_b3"].reshape(2, 128).T)
    d["b4"] = np.ascontiguousarray(inputs["L0_ffn_cn_b4"].reshape(1, 1))
    d["gb1"] = np.ascontiguousarray(inputs["L0_ffn_gc_b1"].reshape(4, 128).T)
    d["gb2"] = np.ascontiguousarray(inputs["L0_ffn_gc_b2"].reshape(2, 128).T)
    d["gb3"] = np.ascontiguousarray(inputs["L0_ffn_gc_b3"].reshape(1, 1))
    d["ar"] = np.ascontiguousarray(inputs["L0_ffn_adaptive_range"].reshape(1, 1))
    d["identb"] = np.eye(P, dtype=bf)
    d["identf"] = np.eye(P, dtype=f32)
    mask1 = np.kron(np.eye(8, dtype=f32), np.ones((16, 16), dtype=f32))
    d["maskc"] = np.tile(mask1, (1, 4)).astype(f32)
    d["onescol"] = np.ones((P, 1), dtype=bf)
    e0 = np.zeros((P, P), dtype=f32)
    e0[0, :] = 1.0
    d["e0c"] = e0
    d["epsc"] = np.full((P, 1), 1e-5, dtype=f32)
    for k in d:
        if d[k].dtype == np.float64:
            raise ValueError(k)
    return d


def kernel(**inputs):
    x = np.asarray(inputs["L0_attn_input_seq_embs"])
    n_cores = 8
    shards = np.split(x, n_cores, axis=0)
    nc, n_tiles = build(n_seq=4)
    in_maps = [_host_inputs(inputs, s) for s in shards]
    res = run_bass_kernel_spmd(nc, in_maps, core_ids=list(range(n_cores)))
    outs = [np.asarray(r["out"], dtype=np.float32) for r in res.results]
    return np.concatenate(outs, axis=0).reshape(32, 2048, 1024)


# version 04
# speedup vs baseline: 1.3657x; 1.2974x over previous
import numpy as np
import ml_dtypes
import concourse.bass as bass
import concourse.mybir as mybir
from concourse.bass_utils import run_bass_kernel_spmd
from concourse.alu_op_type import AluOpType
from contextlib import ExitStack

F32 = mybir.dt.float32
BF16 = mybir.dt.bfloat16
AF = mybir.ActivationFunctionType
AX = mybir.AxisListType
ADD = AluOpType.add
SUB = AluOpType.subtract
MUL = AluOpType.mult
MIN = AluOpType.min

P = 128
DM = 1024
NH = 16
HD = 64
NSEQT = 16


def build(n_seq: int, nseqt: int = None, stage: str = 'full'):
    lvl2b = {'2b0': 0, '2bb': 5, '2bh1': 6, '2bh2': 7, '2bh': 10, '2bc': 20,
             '2bz': 30, '2bt': 40, '2bx': 45}.get(stage, 50)
    stage = 'full'
    global NSEQT
    if nseqt is not None:
        NSEQT = nseqt
    n_tiles = n_seq * NSEQT
    NT = n_tiles * P
    n_super = n_tiles // 4
    nc = bass.Bass("TRN2", target_bir_lowering=False)

    x_in = nc.dram_tensor("x", [NT, DM], F32, kind="ExternalInput").ap()
    wqT = nc.dram_tensor("wqT", [DM, DM], F32, kind="ExternalInput").ap()
    wkT = nc.dram_tensor("wkT", [DM, DM], F32, kind="ExternalInput").ap()
    wvT = nc.dram_tensor("wvT", [DM, DM], F32, kind="ExternalInput").ap()
    w1_in = nc.dram_tensor("w1", [DM, DM], F32, kind="ExternalInput").ap()
    w2_in = nc.dram_tensor("w2", [DM, 512], F32, kind="ExternalInput").ap()
    w3_in = nc.dram_tensor("w3", [512, 256], F32, kind="ExternalInput").ap()
    w4c_in = nc.dram_tensor("w4c", [P, 2], F32, kind="ExternalInput").ap()
    gw1_in = nc.dram_tensor("gw1", [DM, 512], F32, kind="ExternalInput").ap()
    gw2_in = nc.dram_tensor("gw2", [512, 256], F32, kind="ExternalInput").ap()
    gw3c_in = nc.dram_tensor("gw3c", [P, 2], F32, kind="ExternalInput").ap()
    b1c_in = nc.dram_tensor("b1c", [P, 8], F32, kind="ExternalInput").ap()
    b2c_in = nc.dram_tensor("b2c", [P, 4], F32, kind="ExternalInput").ap()
    b3c_in = nc.dram_tensor("b3c", [P, 2], F32, kind="ExternalInput").ap()
    b4_in = nc.dram_tensor("b4", [1, 1], F32, kind="ExternalInput").ap()
    gb1_in = nc.dram_tensor("gb1", [P, 4], F32, kind="ExternalInput").ap()
    gb2_in = nc.dram_tensor("gb2", [P, 2], F32, kind="ExternalInput").ap()
    gb3_in = nc.dram_tensor("gb3", [1, 1], F32, kind="ExternalInput").ap()
    ar_in = nc.dram_tensor("ar", [1, 1], F32, kind="ExternalInput").ap()
    identb_in = nc.dram_tensor("identb", [P, P], BF16, kind="ExternalInput").ap()
    identf_in = nc.dram_tensor("identf", [P, P], F32, kind="ExternalInput").ap()
    mask_in = nc.dram_tensor("maskc", [P, 512], F32, kind="ExternalInput").ap()
    onescol_in = nc.dram_tensor("onescol", [P, 1], BF16, kind="ExternalInput").ap()
    e0_in = nc.dram_tensor("e0c", [P, P], F32, kind="ExternalInput").ap()
    eps_in = nc.dram_tensor("epsc", [P, 1], F32, kind="ExternalInput").ap()

    out_d = nc.dram_tensor("out", [NT, DM], F32, kind="ExternalOutput").ap()
    sq_scr = nc.dram_tensor("sq_scr", [NT, DM], BF16, kind="Internal").ap()
    sk_scr = nc.dram_tensor("sk_scr", [NT, DM], BF16, kind="Internal").ap()
    v_scr = nc.dram_tensor("v_scr", [NT, DM], BF16, kind="Internal").ap()
    x1_scr = nc.dram_tensor("x1_scr", [NT, DM], F32, kind="Internal").ap()

    st = ExitStack()
    E_ = st.enter_context

    WQ = E_(nc.sbuf_tensor("WQ", [P, 8 * DM], BF16))
    WK = E_(nc.sbuf_tensor("WK", [P, 8 * DM], BF16))
    WV = E_(nc.sbuf_tensor("WV", [P, 8 * DM], BF16))
    W1 = E_(nc.sbuf_tensor("W1", [P, 8 * DM], BF16))
    W2 = E_(nc.sbuf_tensor("W2", [P, 8 * 512], BF16))
    W3 = E_(nc.sbuf_tensor("W3", [P, 4 * 256], BF16))
    W4 = E_(nc.sbuf_tensor("W4", [P, 2], BF16))
    G1 = E_(nc.sbuf_tensor("G1", [P, 8 * 512], BF16))
    G2 = E_(nc.sbuf_tensor("G2", [P, 4 * 256], BF16))
    G3 = E_(nc.sbuf_tensor("G3", [P, 2], BF16))
    B1C = E_(nc.sbuf_tensor("B1C", [P, 8], F32))
    B2C = E_(nc.sbuf_tensor("B2C", [P, 4], F32))
    B3C = E_(nc.sbuf_tensor("B3C", [P, 2], F32))
    B4 = E_(nc.sbuf_tensor("B4", [1, 1], F32))
    GB1C = E_(nc.sbuf_tensor("GB1C", [P, 4], F32))
    GB2C = E_(nc.sbuf_tensor("GB2C", [P, 2], F32))
    E0C = E_(nc.sbuf_tensor("E0C", [P, P], F32))
    GB3 = E_(nc.sbuf_tensor("GB3", [1, 1], F32))
    ARC = E_(nc.sbuf_tensor("ARC", [1, 1], F32))
    IDB = E_(nc.sbuf_tensor("IDB", [P, P], BF16))
    IDF = E_(nc.sbuf_tensor("IDF", [P, P], F32))
    MASKC = E_(nc.sbuf_tensor("MASKC", [P, 512], F32))
    ONESC = E_(nc.sbuf_tensor("ONESC", [P, 1], BF16))
    EPSC = E_(nc.sbuf_tensor("EPSC", [P, 1], F32))
    STR = E_(nc.sbuf_tensor("STR", [P, n_tiles], F32))
    STB = E_(nc.sbuf_tensor("STB", [P, n_tiles], F32))
    GCT = E_(nc.sbuf_tensor("GCT", [P, 4 * 8], BF16))
    GCTA = E_(nc.sbuf_tensor("GCTA", [P, 4 * 8], F32))
    GFROW = E_(nc.sbuf_tensor("GFROW", [1, 4], F32))
    ALPHAR = E_(nc.sbuf_tensor("ALPHAR", [1, 4], F32))
    BETA = E_(nc.sbuf_tensor("BETA", [1, 1], F32))
    B4H = E_(nc.sbuf_tensor("B4H", [1, 1], F32))
    GB3H = E_(nc.sbuf_tensor("GB3H", [1, 1], F32))
    GS1F = E_(nc.sbuf_tensor("GS1F", [P, 16], BF16))
    GS2F = E_(nc.sbuf_tensor("GS2F", [P, 8], BF16))
    W4F = E_(nc.sbuf_tensor("W4F", [P, 2], F32))
    G3F = E_(nc.sbuf_tensor("G3F", [P, 2], F32))

    cast_plan = []

    def plan_w(dram, sbuf, total_cols):
        v = dram.rearrange("(c p) n -> p c n", p=P)
        nchunk = v.shape[1]
        ncols = v.shape[2]
        per = max(1, 2048 // ncols)
        c0 = 0
        while c0 < nchunk:
            k = min(per, nchunk - c0)
            cast_plan.append((v[:, c0:c0 + k, :],
                              sbuf[:, c0 * ncols:(c0 + k) * ncols], k * ncols))
            c0 += k

    plan_w(wqT, WQ, 8 * DM)
    plan_w(wkT, WK, 8 * DM)
    plan_w(wvT, WV, 8 * DM)
    plan_w(w1_in, W1, 8 * DM)
    plan_w(w2_in, W2, 8 * 512)
    plan_w(w3_in, W3, 4 * 256)
    plan_w(gw1_in, G1, 8 * 512)
    plan_w(gw2_in, G2, 4 * 256)
    n_rounds = len(cast_plan)

    st1 = ExitStack()
    E1 = st1.enter_context
    XB = [E1(nc.sbuf_tensor(f"XB{k}", [P, DM], F32)) for k in range(2)]
    EBUF = E1(nc.sbuf_tensor("EBUF", [P, 2 * DM], F32))
    DUMP = E1(nc.sbuf_tensor("DUMP", [P, DM], F32))
    LSUM = E1(nc.sbuf_tensor("LSUM", [P, 1], F32))
    LNM = E1(nc.sbuf_tensor("LNM", [P, 1], F32))
    LVAR = E1(nc.sbuf_tensor("LVAR", [P, 1], F32))
    LRS = E1(nc.sbuf_tensor("LRS", [P, 1], F32))
    LNB = E1(nc.sbuf_tensor("LNB", [P, 1], F32))
    ASUM = E1(nc.sbuf_tensor("ASUM", [P, 1], F32))
    ANM = E1(nc.sbuf_tensor("ANM", [P, 1], F32))
    AVAR = E1(nc.sbuf_tensor("AVAR", [P, 1], F32))
    ATMP = E1(nc.sbuf_tensor("ATMP", [P, 1], F32))
    NMQ = E1(nc.sbuf_tensor("NMQ", [P, 1], F32))
    NMK = E1(nc.sbuf_tensor("NMK", [P, 1], F32))
    TMP1 = E1(nc.sbuf_tensor("TMP1", [P, 1], F32))
    TMPK = E1(nc.sbuf_tensor("TMPK", [P, 1], F32))
    ZQ = E1(nc.sbuf_tensor("ZQ", [P, NH], F32))
    ZK = E1(nc.sbuf_tensor("ZK", [P, HD], F32))
    HBF = [E1(nc.sbuf_tensor(f"HBF{k}", [P, DM], BF16)) for k in range(2)]
    HFM = [E1(nc.sbuf_tensor(f"HFM{k}", [P, DM], BF16)) for k in range(2)]
    SQTM = [E1(nc.sbuf_tensor(f"SQTM{k}", [P, DM], BF16)) for k in range(2)]
    SKTM = [E1(nc.sbuf_tensor(f"SKTM{k}", [P, DM], BF16)) for k in range(2)]
    VTM = [E1(nc.sbuf_tensor(f"VTM{k}", [P, DM], BF16)) for k in range(2)]
    SQS = [E1(nc.sbuf_tensor(f"SQS{k}", [P, DM], BF16)) for k in range(2)]
    SKS = [E1(nc.sbuf_tensor(f"SKS{k}", [P, DM], BF16)) for k in range(2)]
    VS = [E1(nc.sbuf_tensor(f"VS{k}", [P, DM], BF16)) for k in range(2)]
    TSQ = [E1(nc.sbuf_tensor(f"TSQ{k}", [P, 2 * DM], BF16)) for k in range(2)]
    TSK = [E1(nc.sbuf_tensor(f"TSK{k}", [P, 2 * DM], BF16)) for k in range(2)]
    MBD = [E1(nc.sbuf_tensor(f"MBD{k}", [P, 512], BF16)) for k in range(2)]
    AS = [E1(nc.sbuf_tensor(f"AS{k}", [P, DM], F32)) for k in range(2)]
    X1A = [E1(nc.sbuf_tensor(f"X1A{k}", [P, DM], F32)) for k in range(2)]
    H2A = [E1(nc.sbuf_tensor(f"H2A{k}", [P, DM], BF16)) for k in range(2)]

    p_t1 = E1(nc.psum_tensor("p_t1", [P, DM], BF16))
    p_tsq = E1(nc.psum_tensor("p_tsq", [64, DM], BF16))
    p_q = E1(nc.psum_tensor("p_q", [P, DM], F32))
    p_k = E1(nc.psum_tensor("p_k", [P, DM], F32))
    p_v = E1(nc.psum_tensor("p_v", [P, 512], F32))
    p_s2 = E1(nc.psum_tensor("p_s2", [P, 8], F32))

    sem_names = ("s_wld s_wcast s_cld s_x s_stk s_x1c s_wscr s_acc "
                 "s_hbf s_nm s_sm s_tse s_mask s_hfm s_eq s_ek s_vev s_aev "
                 "s_ht s_qq s_qk s_qv s_ts1 s_ts2 s_gram s_app4 "
                 "s_x1l s_h2a s_gsT s_gct s_gfp s_gfv s_gfa "
                 "s_x1b s_h2b s_h2t s_obf s_of32 s_mmc s_evc s_zt s_tfr "
                 "s_crow s_cbc s_tw s_cast s_x2t s_x2 s_outw s_gse s_vch s_ach s_vc2 s_gch s_twp").split()
    SM = {}
    for name in sem_names:
        SM[name] = E_(nc.semaphore(name))
    S = type("S", (), SM)

    def stk_view(dram, i):
        return dram[P * i:P * (i + 1), :].rearrange(
            "(g tl) (h d) -> (tl h) g d", g=16, tl=8, h=16)

    N_CLD = 16

    blk1_cm = nc.Block()
    b1 = blk1_cm.__enter__()

    @b1.sync
    def _(eng):
        for ap, dst in ((identb_in, IDB), (identf_in, IDF), (mask_in, MASKC),
                        (onescol_in, ONESC),
                        (e0_in, E0C),
                        (b1c_in, B1C), (b2c_in, B2C), (b3c_in, B3C),
                        (b4_in, B4), (gb1_in, GB1C), (gb2_in, GB2C),
                        (gb3_in, GB3), (ar_in, ARC), (eps_in, EPSC),
                        (w4c_in, W4F), (gw3c_in, G3F)):
            eng.dma_start(dst[:], ap).then_inc(S.s_cld, 16)
        for r, (src, dst, n) in enumerate(cast_plan):
            eng.wait_ge(S.s_wcast, r)
            eng.dma_start(EBUF[:, 0:n], src).then_inc(S.s_wld, 16)

        def load_2a(j):
            eng.wait_ge(S.s_acc, 16 * (j + 1))
            if j >= 2:
                eng.wait_ge(S.s_h2a, j - 1)
            eng.dma_start(X1A[j % 2][:],
                          x1_scr[P * j:P * (j + 1), :]).then_inc(S.s_x1l, 16)

        def stack_read(t):
            eng.wait_ge(S.s_wscr, 48 * (t + 1))
            if t >= 2:
                eng.wait_ge(S.s_gram, 4 * (t - 1))
                eng.wait_ge(S.s_app4, 4 * (t - 1))
            eng.dma_start(SQS[t % 2][:], stk_view(sq_scr, t)).then_inc(S.s_stk, 16)
            eng.dma_start(SKS[t % 2][:], stk_view(sk_scr, t)).then_inc(S.s_stk, 16)
            eng.dma_start(VS[t % 2][:], stk_view(v_scr, t)).then_inc(S.s_stk, 16)

        for i in range(n_tiles):
            if i >= 2:
                eng.wait_ge(S.s_hbf, i - 1)
                eng.wait_ge(S.s_x1c, 16 * (i - 1))
            else:
                eng.wait_ge(S.s_wcast, n_rounds + 1)
            eng.dma_start(XB[i % 2][:], x_in[P * i:P * (i + 1), :]).then_inc(S.s_x, 16)
            if i >= 1:
                stack_read(i - 1)
            if i >= 2:
                load_2a(i - 2)
        stack_read(n_tiles - 1)
        for j in range(n_tiles - 2, n_tiles):
            load_2a(j)

    @b1.gpsimd
    def _(eng):
        def accum(t):
            eng.wait_ge(S.s_aev, 2 * (t + 1))
            eng.dma_start(
                x1_scr[P * t:P * (t + 1), :].rearrange(
                    "(g tl) (h e) -> (tl h) g e", g=16, tl=8, h=16),
                AS[t % 2][:], accum_op=ADD).then_inc(S.s_acc, 16)

        for i in range(n_tiles):
            eng.wait_ge(S.s_x, 16 * (i + 1))
            eng.dma_start(x1_scr[P * i:P * (i + 1), :],
                          XB[i % 2][:]).then_inc(S.s_x1c, 16)
            eng.wait_ge(S.s_sm, 2 * i + 1)
            eng.dma_start(sq_scr[P * i:P * (i + 1), :], SQTM[i % 2][:]).then_inc(S.s_wscr, 16)
            eng.wait_ge(S.s_sm, 2 * i + 2)
            eng.dma_start(sk_scr[P * i:P * (i + 1), :], SKTM[i % 2][:]).then_inc(S.s_wscr, 16)
            eng.wait_ge(S.s_vev, 2 * (i + 1))
            eng.dma_start(v_scr[P * i:P * (i + 1), :], VTM[i % 2][:]).then_inc(S.s_wscr, 16)
            if i >= 1:
                accum(i - 1)
        accum(n_tiles - 1)

    @b1.tensor
    def _(eng):
        eng.wait_ge(S.s_wcast, n_rounds + 1)
        eng.wait_ge(S.s_cld, 16 * N_CLD)

        def pe_gsum(j):
            eng.wait_ge(S.s_h2a, j + 1)
            if j >= 1:
                eng.wait_ge(S.s_gse, j)
            for c in range(8):
                ins = eng.matmul(p_s2[:, c:c + 1],
                                 H2A[j % 2][:, P * c:P * (c + 1)], ONESC[:],
                                 start=True, stop=True)
                if c == 7:
                    ins.then_inc(S.s_gsT, 1)

        for i in range(n_tiles + 1):
            t = i - 1
            if i < n_tiles:
                eng.wait_ge(S.s_hbf, i + 1)
                if i >= 2:
                    eng.wait_ge(S.s_tse, 4 * (i - 1))
                for c in range(8):
                    ins = eng.transpose(p_t1[:, P * c:P * (c + 1)],
                                        HBF[i % 2][:, P * c:P * (c + 1)], IDB[:])
                    if c == 7:
                        ins.then_inc(S.s_ht, 1)
                eng.wait_ge(S.s_hfm, i + 1)
                if i >= 1:
                    eng.wait_ge(S.s_eq, 2 * i)
                    eng.wait_ge(S.s_mask, 4 * (i - 1))
                for j in range(8):
                    lhs = HFM[i % 2][:, P * j:P * (j + 1)]
                    eng.matmul(p_q[:, 0:512], lhs, WQ[:, DM * j:DM * j + 512],
                               start=(j == 0), stop=(j == 7))
                    ins = eng.matmul(p_q[:, 512:DM], lhs,
                                     WQ[:, DM * j + 512:DM * (j + 1)],
                                     start=(j == 0), stop=(j == 7))
                    if j == 7:
                        ins.then_inc(S.s_qq, 1)
            if i >= 1:
                eng.wait_ge(S.s_stk, 48 * i)
                if i >= 2:
                    eng.wait_ge(S.s_tse, 4 * (i - 2) + 2)
                for gg in range(8):
                    ins = eng.transpose(p_tsq[0:64, P * gg:P * (gg + 1)],
                                        SQS[t % 2][:, HD * gg:HD * (gg + 1)],
                                        IDB[:])
                    if gg == 7:
                        ins.then_inc(S.s_ts1, 1)
            if i < n_tiles:
                if i >= 1:
                    eng.wait_ge(S.s_ek, 2 * i)
                if i >= 2:
                    eng.wait_ge(S.s_aev, 2 * (i - 1))
                for j in range(8):
                    lhs = HFM[i % 2][:, P * j:P * (j + 1)]
                    eng.matmul(p_k[:, 0:512], lhs, WK[:, DM * j:DM * j + 512],
                               start=(j == 0), stop=(j == 7))
                    ins = eng.matmul(p_k[:, 512:DM], lhs,
                                     WK[:, DM * j + 512:DM * (j + 1)],
                                     start=(j == 0), stop=(j == 7))
                    if j == 7:
                        ins.then_inc(S.s_qk, 1)
            if i >= 1:
                eng.wait_ge(S.s_tse, 4 * t + 1)
                for gg in range(8):
                    g = 8 + gg
                    ins = eng.transpose(p_tsq[0:64, P * gg:P * (gg + 1)],
                                        SQS[t % 2][:, HD * g:HD * (g + 1)],
                                        IDB[:])
                    if gg == 7:
                        ins.then_inc(S.s_ts1, 1)
                if i == n_tiles:
                    eng.wait_ge(S.s_hfm, n_tiles)
                if i >= 2:
                    eng.wait_ge(S.s_tse, 4 * (i - 2) + 4)
                for gg in range(8):
                    ins = eng.transpose(p_t1[0:64, P * gg:P * (gg + 1)],
                                        SKS[t % 2][:, HD * gg:HD * (gg + 1)],
                                        IDB[:])
                    if gg == 7:
                        ins.then_inc(S.s_ts2, 1)
            if i < n_tiles:
                if i >= 1:
                    eng.wait_ge(S.s_vev, 2 * i)
                for j in range(8):
                    lhs = HFM[i % 2][:, P * j:P * (j + 1)]
                    ins = eng.matmul(p_v[:], lhs, WV[:, DM * j:DM * j + 512],
                                     start=(j == 0), stop=(j == 7))
                    if j == 7:
                        ins.then_inc(S.s_qv, 1)
            if i >= 1:
                eng.wait_ge(S.s_tse, 4 * t + 3)
                for gg in range(8):
                    g = 8 + gg
                    ins = eng.transpose(p_t1[0:64, P * gg:P * (gg + 1)],
                                        SKS[t % 2][:, HD * g:HD * (g + 1)],
                                        IDB[:])
                    if gg == 7:
                        ins.then_inc(S.s_ts2, 1)
            if i < n_tiles:
                eng.wait_ge(S.s_vev, 2 * i + 1)
                for j in range(8):
                    lhs = HFM[i % 2][:, P * j:P * (j + 1)]
                    ins = eng.matmul(p_v[:], lhs,
                                     WV[:, DM * j + 512:DM * (j + 1)],
                                     start=(j == 0), stop=(j == 7))
                    if j == 7:
                        ins.then_inc(S.s_qv, 1)
            if i >= 1:
                for r in range(4):
                    eng.wait_ge(S.s_tse, 4 * t + (3 if r < 2 else 4))
                    if r == 0:
                        eng.wait_ge(S.s_eq, min(2 * (i + 1), 2 * n_tiles))
                    else:
                        eng.wait_ge(S.s_mask, 4 * t + r)
                    for gg in range(4):
                        g = 4 * r + gg
                        ins = eng.matmul(
                            p_q[:, 512 + P * gg:512 + P * (gg + 1)],
                            TSK[t % 2][:, P * g:P * (g + 1)],
                            TSQ[t % 2][:, P * g:P * (g + 1)],
                            start=True, stop=True)
                        if gg == 3:
                            ins.then_inc(S.s_gram, 1)
                    if r >= 1:
                        b = r - 1
                        if b == 0:
                            eng.wait_ge(S.s_ek, min(2 * i + 1, 2 * n_tiles - 1))
                        elif b == 2:
                            eng.wait_ge(S.s_ek, min(2 * (i + 1), 2 * n_tiles))
                        for gg in range(4):
                            g = 4 * b + gg
                            ins = eng.matmul(
                                p_k[:, HD * g:HD * (g + 1)],
                                MBD[b % 2][:, P * gg:P * (gg + 1)],
                                VS[t % 2][:, HD * g:HD * (g + 1)],
                                start=True, stop=True)
                            if gg == 3:
                                ins.then_inc(S.s_app4, 1)
                eng.wait_ge(S.s_mask, 4 * t + 4)
                for gg in range(4):
                    g = 12 + gg
                    ins = eng.matmul(p_k[:, HD * g:HD * (g + 1)],
                                     MBD[1][:, P * gg:P * (gg + 1)],
                                     VS[t % 2][:, HD * g:HD * (g + 1)],
                                     start=True, stop=True)
                    if gg == 3:
                        ins.then_inc(S.s_app4, 1)
            if i >= 4:
                pe_gsum(i - 4)
        for j in range(n_tiles - 3, n_tiles):
            pe_gsum(j)
        eng.wait_ge(S.s_gct, 1)
        gft = 0
        for oc in range(4):
            if gft >= 1:
                eng.wait_ge(S.s_gfa, gft)
            for c in range(8):
                gct_c = GCT[:].rearrange("p (s c) -> p c s", c=8)[:, c, :]
                ins = eng.matmul(p_q[:, 0:4], G1[:, 512 * c + P * oc:512 * c + P * (oc + 1)],
                                 gct_c, start=(c == 0), stop=(c == 7))
                if c == 7:
                    ins.then_inc(S.s_gfp, 1)
            gft += 1
        for o2 in range(2):
            eng.wait_ge(S.s_gfa, gft)
            for c2 in range(4):
                ins = eng.matmul(p_q[:, 0:4],
                                 G2[:, 256 * c2 + P * o2:256 * c2 + P * (o2 + 1)],
                                 GS1F[:, 4 * c2:4 * c2 + 4],
                                 start=(c2 == 0), stop=(c2 == 3))
                if c2 == 3:
                    ins.then_inc(S.s_gfp, 1)
            gft += 1
        eng.wait_ge(S.s_gfa, gft)
        for c3 in range(2):
            ins = eng.matmul(p_q[0:1, 0:4], G3[:, c3:c3 + 1],
                             GS2F[:, 4 * c3:4 * c3 + 4],
                             start=(c3 == 0), stop=(c3 == 1))
            if c3 == 1:
                ins.then_inc(S.s_gfp, 1)

    @b1.vector
    def _(eng):
        vch = [0]

        def vs(ins):
            ins.then_inc(S.s_vch, 1)
            vch[0] += 1

        def vw():
            eng.wait_ge(S.s_vch, vch[0])

        for r, (src, dst, n) in enumerate(cast_plan):
            eng.wait_ge(S.s_wld, 16 * (r + 1))
            eng.tensor_copy(dst, EBUF[:, 0:n]).then_inc(S.s_wcast, 1)
        eng.wait_ge(S.s_cld, 16 * N_CLD)
        eng.tensor_copy(W4[:], W4F[:])
        eng.tensor_copy(G3[:], G3F[:])
        eng.memset(GCTA[:], 0.0)
        eng.tensor_scalar_mul(B4H[:], B4[:], 0.5)
        eng.tensor_scalar_mul(GB3H[:], GB3[:], 0.5)
        for k_t, t_b in enumerate(TSQ + TSK):
            ins = eng.memset(t_b[64:P, :], 0.0)
        ins.then_inc(S.s_wcast, 1)

        def gcta(j):
            eng.wait_ge(S.s_gsT, j + 1)
            sq = j // NSEQT
            eng.tensor_tensor(GCTA[:, 8 * sq:8 * sq + 8],
                              GCTA[:, 8 * sq:8 * sq + 8],
                              p_s2[:], op=ADD).then_inc(S.s_gse, 1)

        for i in range(n_tiles + 1):
            t = i - 1
            if i < n_tiles:
                eng.wait_ge(S.s_qq, i + 1)
                vs(eng.reduce_max(NMQ[:], p_q[:, 0:512], axis=AX.X, negate=True))
                vs(eng.reduce_max(TMP1[:], p_q[:, 512:DM], axis=AX.X, negate=True))
                vw()
                eng.tensor_tensor(NMQ[:], NMQ[:], TMP1[:], op=MIN).then_inc(S.s_nm, 1)
            if i >= 1:
                eng.wait_ge(S.s_ts1, 2 * t + 1)
                eng.tensor_copy(TSQ[t % 2][0:64, 0:DM],
                                p_tsq[0:64, :]).then_inc(S.s_tse, 1)
            if i < n_tiles:
                eng.wait_ge(S.s_qk, i + 1)
                vs(eng.reduce_max(NMK[:], p_k[:, 0:512], axis=AX.X, negate=True))
                vs(eng.reduce_max(TMPK[:], p_k[:, 512:DM], axis=AX.X, negate=True))
                vw()
                eng.tensor_tensor(NMK[:], NMK[:], TMPK[:], op=MIN).then_inc(S.s_nm, 1)
            if i >= 1:
                eng.wait_ge(S.s_ts1, 2 * t + 2)
                eng.tensor_copy(TSQ[t % 2][0:64, DM:2 * DM],
                                p_tsq[0:64, :]).then_inc(S.s_tse, 1)
                eng.wait_ge(S.s_ts2, 2 * t + 1)
                eng.tensor_copy(TSK[t % 2][0:64, 0:DM],
                                p_t1[0:64, :]).then_inc(S.s_tse, 1)
            if i < n_tiles:
                eng.wait_ge(S.s_eq, 2 * (i + 1))
                eq = EBUF[:, 0:DM]
                vs(eng.reduce_sum(ZQ[:], eq.rearrange("p (h d) -> p h d", h=NH), axis=AX.X))
                vw()
                vs(eng.reciprocal(ZQ[:], ZQ[:]))
                vw()
                if i >= 2:
                    eng.wait_ge(S.s_wscr, 48 * (i - 1))
                eng.tensor_tensor(
                    SQTM[i % 2][:].rearrange("p (h d) -> p h d", h=NH),
                    eq.rearrange("p (h d) -> p h d", h=NH),
                    ZQ[:, :, None].broadcast_to([P, NH, HD]),
                    op=MUL).then_inc(S.s_sm, 1)
            if i >= 1:
                eng.wait_ge(S.s_ts2, 2 * t + 2)
                eng.tensor_copy(TSK[t % 2][0:64, DM:2 * DM],
                                p_t1[0:64, :]).then_inc(S.s_tse, 1)
            if i < n_tiles:
                eng.wait_ge(S.s_ek, 2 * (i + 1))
                ek = EBUF[:, DM:2 * DM]
                vs(eng.reduce_sum(ZK[:], ek.rearrange("p (h d) -> p d h", h=NH), axis=AX.X))
                vw()
                vs(eng.reciprocal(ZK[:], ZK[:]))
                vw()
                eng.tensor_tensor(
                    SKTM[i % 2][:].rearrange("p (h d) -> p h d", h=NH),
                    ek.rearrange("p (h d) -> p h d", h=NH),
                    ZK[:, None, :].broadcast_to([P, NH, HD]),
                    op=MUL).then_inc(S.s_sm, 1)
            if i >= 1:
                for r in range(4):
                    eng.wait_ge(S.s_gram, 4 * t + r + 1)
                    if 4 * t + r - 1 >= 1:
                        eng.wait_ge(S.s_app4, 4 * t + r - 1)
                    eng.tensor_tensor(MBD[r % 2][:], p_q[:, 512:DM], MASKC[:],
                                      op=MUL).then_inc(S.s_mask, 1)
            if i >= 4:
                gcta(i - 4)
        for j in range(n_tiles - 3, n_tiles):
            gcta(j)
        eng.wait_ge(S.s_gse, n_tiles)
        eng.tensor_scalar_mul(GCT[:], GCTA[:], 1.0 / (NSEQT * P)).then_inc(S.s_gct, 1)
        eng.wait_ge(S.s_gfa, 7)
        vs(eng.tensor_scalar_mul(ALPHAR[:], GFROW[:], 0.7))
        vw()
        vs(eng.tensor_scalar(out=ALPHAR[:], in0=ALPHAR[:], scalar1=ARC[0:1, 0:1],
                             scalar2=0.1, op0=MUL, op1=ADD))
        vw()
        eng.tensor_scalar_mul(BETA[:], ARC[0:1, 0:1], 0.3).then_inc(S.s_gfv, 3)

    @b1.scalar
    def _(eng):
        eng.wait_ge(S.s_cld, 16 * N_CLD)
        ach = [0]

        def as_(ins):
            ins.then_inc(S.s_ach, 1)
            ach[0] += 1

        def aw():
            eng.wait_ge(S.s_ach, ach[0])

        def act_ln(j):
            eng.wait_ge(S.s_x, 16 * (j + 1))
            as_(eng.activation(DUMP[:], XB[j % 2][:], AF.Copy,
                               accum_out=LSUM[:, 0:1]))
            aw()
            as_(eng.mul(LNM[:], LSUM[:], -1.0 / DM))
            aw()
            as_(eng.activation(DUMP[:], XB[j % 2][:], AF.Square,
                               bias=LNM[:, 0:1], accum_out=LVAR[:, 0:1]))
            aw()
            as_(eng.activation(LRS[:], LVAR[:], AF.Ln, bias=EPSC[:, 0:1],
                               scale=1.0 / DM))
            aw()
            as_(eng.activation(LRS[:], LRS[:], AF.Exp, scale=-0.5))
            aw()
            as_(eng.mul(LNB[:], LNM[:], LRS[:, 0:1]))
            aw()
            if j >= 2:
                eng.wait_ge(S.s_ht, j - 1)
            eng.activation(HBF[j % 2][:], XB[j % 2][:], AF.Identity,
                           scale=LRS[:, 0:1], bias=LNB[:, 0:1]).then_inc(S.s_hbf, 1)

        def act_2a(j):
            eng.wait_ge(S.s_x1l, 16 * (j + 1))
            as_(eng.activation(DUMP[:], X1A[j % 2][:], AF.Copy,
                               accum_out=ASUM[:, 0:1]))
            aw()
            as_(eng.mul(ANM[:], ASUM[:], -1.0 / DM))
            aw()
            as_(eng.activation(DUMP[:], X1A[j % 2][:], AF.Square,
                               bias=ANM[:, 0:1], accum_out=AVAR[:, 0:1]))
            aw()
            as_(eng.activation(ATMP[:], AVAR[:], AF.Ln, bias=EPSC[:, 0:1],
                               scale=1.0 / DM))
            aw()
            as_(eng.activation(STR[:, j:j + 1], ATMP[:], AF.Exp, scale=-0.5))
            aw()
            as_(eng.mul(STB[:, j:j + 1], ANM[:], STR[:, j:j + 1]))
            aw()
            if j >= 2:
                eng.wait_ge(S.s_gsT, j - 1)
            eng.activation(H2A[j % 2][:], X1A[j % 2][:], AF.Identity,
                           scale=STR[:, j:j + 1],
                           bias=STB[:, j:j + 1]).then_inc(S.s_h2a, 1)

        def aev(t):
            eng.wait_ge(S.s_app4, 4 * t + 2)
            if t >= 2:
                eng.wait_ge(S.s_acc, 16 * (t - 1))
            eng.copy(AS[t % 2][:, 0:512], p_k[:, 0:512]).then_inc(S.s_aev, 1)
            eng.wait_ge(S.s_app4, 4 * t + 4)
            eng.copy(AS[t % 2][:, 512:DM], p_k[:, 512:DM]).then_inc(S.s_aev, 1)

        act_ln(0)
        for i in range(n_tiles):
            eng.wait_ge(S.s_ht, i + 1)
            if i >= 2:
                eng.wait_ge(S.s_qv, 2 * (i - 1))
            eng.copy(HFM[i % 2][:], p_t1[:]).then_inc(S.s_hfm, 1)
            if i + 1 < n_tiles:
                act_ln(i + 1)
            eng.wait_ge(S.s_nm, 2 * i + 1)
            if i >= 1:
                eng.wait_ge(S.s_sm, 2 * i - 1)
            eng.activation(EBUF[:, 0:512], p_q[:, 0:512], AF.Exp,
                           bias=NMQ[:, 0:1]).then_inc(S.s_eq, 1)
            eng.activation(EBUF[:, 512:DM], p_q[:, 512:DM], AF.Exp,
                           bias=NMQ[:, 0:1]).then_inc(S.s_eq, 1)
            eng.wait_ge(S.s_nm, 2 * i + 2)
            if i >= 1:
                eng.wait_ge(S.s_sm, 2 * i)
            eng.activation(EBUF[:, DM:DM + 512], p_k[:, 0:512], AF.Exp,
                           bias=NMK[:, 0:1]).then_inc(S.s_ek, 1)
            eng.activation(EBUF[:, DM + 512:2 * DM], p_k[:, 512:DM], AF.Exp,
                           bias=NMK[:, 0:1]).then_inc(S.s_ek, 1)
            eng.wait_ge(S.s_qv, 2 * i + 1)
            if i >= 2:
                eng.wait_ge(S.s_wscr, 48 * (i - 1))
            eng.copy(VTM[i % 2][:, 0:512], p_v[:]).then_inc(S.s_vev, 1)
            eng.wait_ge(S.s_qv, 2 * i + 2)
            eng.copy(VTM[i % 2][:, 512:DM], p_v[:]).then_inc(S.s_vev, 1)
            if i >= 3:
                act_2a(i - 3)
            if i >= 1:
                aev(i - 1)
        aev(n_tiles - 1)
        for j in range(n_tiles - 3, n_tiles):
            act_2a(j)
        for oc in range(4):
            eng.wait_ge(S.s_gfp, oc + 1)
            eng.activation(GS1F[:, 4 * oc:4 * (oc + 1)], p_q[:, 0:4], AF.Gelu,
                           bias=GB1C[:, oc:oc + 1]).then_inc(S.s_gfa, 1)
        for o2 in range(2):
            eng.wait_ge(S.s_gfp, 4 + o2 + 1)
            eng.activation(GS2F[:, 4 * o2:4 * (o2 + 1)], p_q[:, 0:4], AF.Gelu,
                           bias=GB2C[:, o2:o2 + 1]).then_inc(S.s_gfa, 1)
        eng.wait_ge(S.s_gfp, 7)
        eng.activation(GFROW[:], p_q[0:1, 0:4], AF.Tanh,
                       bias=GB3H[0:1, 0:1], scale=0.5).then_inc(S.s_gfa, 1)

    blk1_cm.__exit__(None, None, None)
    st1.close()
    E2 = st.enter_context
    X1S = [E2(nc.sbuf_tensor(f"X1S{k}", [P, 4 * DM], F32)) for k in range(2)]
    H2B = E2(nc.sbuf_tensor("H2B", [P, DM], BF16))
    OBF = E2(nc.sbuf_tensor("OBF", [P, 8 * 512], BF16))
    OF32 = E2(nc.sbuf_tensor("OF32", [P, 8 * 512], F32))
    T1B = E2(nc.sbuf_tensor("T1B", [P, 8 * 512], BF16))
    T2B = E2(nc.sbuf_tensor("T2B", [P, 4 * 512], BF16))
    T3B = E2(nc.sbuf_tensor("T3B", [P, 2 * 512], BF16))
    TFR = E2(nc.sbuf_tensor("TFR", [1, 512], F32))
    CROW = E2(nc.sbuf_tensor("CROW", [P, 512], F32))
    TWA = E2(nc.sbuf_tensor("TWA", [P, 512], F32))
    TWB = E2(nc.sbuf_tensor("TWB", [P, 512], F32))
    X2 = [E2(nc.sbuf_tensor(f"X2_{k}", [P, DM], F32)) for k in range(2)]
    pA = E2(nc.psum_tensor("pA", [P, 512], F32))
    pB = E2(nc.psum_tensor("pB", [P, 512], F32))
    pZ = E2(nc.psum_tensor("pZ", [1, 512], F32))
    pC = E2(nc.psum_tensor("pC", [P, 512], F32))
    pH = E2(nc.psum_tensor("pH", [P, DM], BF16))
    pX = E2(nc.psum_tensor("pX", [P, DM], F32))

    def barrier(eng):
        eng.wait_ge(S.s_wcast, n_rounds + 1)
        eng.wait_ge(S.s_acc, 16 * n_tiles)
        eng.wait_ge(S.s_h2a, n_tiles)
        eng.wait_ge(S.s_gsT, n_tiles)
        eng.wait_ge(S.s_gfa, 7)
        eng.wait_ge(S.s_gfv, 3)

    blk2_cm = nc.Block()
    b2 = blk2_cm.__enter__()

    import os as _os0
    n_cyc = 3 if lvl2b >= 40 else 1
    if _os0.environ.get('KCYC'):
        n_cyc = int(_os0.environ['KCYC'])

    @b2.sync
    def _(eng):
        barrier(eng)
        for s_i in range(n_super):
            for k in range(4):
                if s_i >= 2:
                    if lvl2b >= 45:
                        eng.wait_ge(S.s_x2, 4 * (s_i - 1))
                    else:
                        eng.wait_ge(S.s_h2b, 4 * (s_i - 1))
                eng.dma_start(X1S[s_i % 2][:, DM * k:DM * (k + 1)],
                              x1_scr[P * (4 * s_i + k):P * (4 * s_i + k + 1), :]
                              ).then_inc(S.s_x1b, 16)
        if lvl2b == 50:
            eng.wait_ge(S.s_outw, 16 * n_tiles)
            import os as _os
            if _os.environ.get("KDBG2"):
                eng.dma_start(out_d[0:1, 0:4], GFROW[:]).then_inc(S.s_outw, 16)
                eng.dma_start(out_d[1:2, 0:4], ALPHAR[:]).then_inc(S.s_outw, 16)
                eng.dma_start(out_d[2:3, 0:16], GCT.ap().bitcast(F32)[0:1, 0:16]).then_inc(S.s_outw, 16)
                eng.wait_ge(S.s_outw, 16 * n_tiles + 48)
        else:
            if lvl2b == 0:
                eng.wait_ge(S.s_x1b, 16 * 4 * n_super)
            elif lvl2b == 5:
                eng.wait_ge(S.s_h2b, 4 * n_super)
            elif lvl2b == 6:
                eng.wait_ge(S.s_obf, 4 * n_super)
            elif lvl2b == 7:
                eng.wait_ge(S.s_of32, 4 * n_super)
            elif lvl2b == 10:
                eng.wait_ge(S.s_of32, 4 * n_super)
                eng.wait_ge(S.s_obf, 4 * n_super)
            elif lvl2b == 20:
                eng.wait_ge(S.s_evc, 14 * n_super)
                import os as _os
                if _os.environ.get("KDBG3"):
                    ob = out_d.bitcast(BF16)
                    eng.dma_start(ob[0:P, :], T2B[:]).then_inc(S.s_outw, 16)
                    eng.wait_ge(S.s_outw, 16)
                    return
            elif lvl2b == 30:
                eng.wait_ge(S.s_cbc, n_super)
                import os as _os
                if _os.environ.get("KDBG4"):
                    ob = out_d.bitcast(BF16)
                    eng.dma_start(ob[0:P, 0:1024], T3B[:]).then_inc(S.s_outw, 16)
                    eng.dma_start(out_d[P:P + 1, 0:512], TFR[:]).then_inc(S.s_outw, 16)
                    eng.dma_start(out_d[P + 1:P + 2, 0:512], CROW[0:1, :]).then_inc(S.s_outw, 16)
                    eng.wait_ge(S.s_outw, 48)
                    return
            elif lvl2b == 40:
                eng.wait_ge(S.s_tw, n_cyc * n_super)
                import os as _os
                if _os.environ.get("KDBG5"):
                    eng.wait_ge(S.s_of32, 4 * n_super)
                    for rr in range(4):
                        eng.dma_start(out_d[P * rr:P * (rr + 1), :],
                                      OF32[:, 1024 * rr:1024 * (rr + 1)]).then_inc(S.s_outw, 16)
                    eng.wait_ge(S.s_outw, 64)
                    return
            elif lvl2b == 45:
                eng.wait_ge(S.s_x2, 4 * n_super)
            eng.dma_start(out_d, x1_scr).then_inc(S.s_outw, 16)
            eng.wait_ge(S.s_outw, 16)

    @b2.tensor
    def _(eng):
        barrier(eng)
        n_mm = 0
        n_h2t = 0
        n_x2t = 0
        n_cbc = 0
        n_cast = 0
        n_crow = 0
        n_zt = 0
        for s_i in range(n_super if lvl2b >= 6 else 0):
            for k in range(4):
                eng.wait_ge(S.s_h2b, 4 * s_i + k + 1)
                if n_h2t >= 1:
                    if lvl2b != 7:
                        eng.wait_ge(S.s_obf, n_h2t)
                    if lvl2b != 6:
                        eng.wait_ge(S.s_of32, n_h2t)
                for c in range(8):
                    ins = eng.transpose(pH[:, P * c:P * (c + 1)],
                                        H2B[:, P * c:P * (c + 1)], IDB[:])
                    if c == 7:
                        ins.then_inc(S.s_h2t, 1)
                n_h2t += 1
            for cyc in range(n_cyc if lvl2b >= 20 else 0):
                if cyc == 0:
                    eng.wait_ge(S.s_obf, n_h2t)
                jorder = (0, 4, 1, 5, 2, 6, 3, 7) if cyc > 0 else tuple(range(8))
                for (n_o, nj, Wsb, src, in_w) in (
                        (8, 8, W1, OBF, DM), (4, 8, W2, T1B, 512),
                        (2, 4, W3, T2B, 256)):
                    for o in range(n_o):
                        pp = pA if o % 2 == 0 else pB
                        if o == 0:
                            if n_mm >= 1:
                                eng.wait_ge(S.s_evc, n_mm)
                        elif n_mm >= 2:
                            eng.wait_ge(S.s_evc, n_mm - 1)
                        js = jorder if nj == 8 and src is OBF else range(nj)
                        for j_i, j in enumerate(js):
                            if (cyc > 0 and src is OBF and o == 0
                                    and lvl2b >= 40 and j_i % 2 == 0):
                                eng.wait_ge(S.s_cast,
                                            8 * s_i + 4 * (cyc - 1) + j_i // 2 + 1)
                            ins = eng.matmul(
                                pp[:],
                                Wsb[:, in_w * j + P * o:in_w * j + P * (o + 1)],
                                src[:, 512 * j:512 * (j + 1)],
                                start=(j_i == 0), stop=(j_i == nj - 1))
                            if j_i == nj - 1:
                                ins.then_inc(S.s_mmc, 1)
                        n_mm += 1
                if lvl2b < 30:
                    continue
                eng.wait_ge(S.s_evc, n_mm)
                if n_zt >= 1:
                    eng.wait_ge(S.s_crow, n_crow)
                for j in range(2):
                    ins = eng.matmul(pZ[:], W4[:, j:j + 1],
                                     T3B[:, 512 * j:512 * (j + 1)],
                                     start=(j == 0), stop=(j == 1))
                    if j == 1:
                        ins.then_inc(S.s_zt, 1)
                n_zt += 1
                eng.wait_ge(S.s_crow, n_crow + 1)
                n_crow += 1
                if n_cbc >= 1 and lvl2b >= 40:
                    eng.wait_ge(S.s_tw, n_cbc)
                eng.matmul(pC[:], E0C[:], CROW[:],
                           start=True, stop=True).then_inc(S.s_cbc, 1)
                n_cbc += 1
                if cyc < 2:
                    n_cast += 1
            if lvl2b < 45:
                continue
            eng.wait_ge(S.s_tw, n_cbc)
            for k in range(4):
                if n_x2t >= 1:
                    eng.wait_ge(S.s_x2, n_x2t)
                for c in range(8):
                    ins = eng.transpose(
                        pX[:, P * c:P * (c + 1)],
                        OF32[:, 512 * c + P * k:512 * c + P * (k + 1)], IDF[:])
                    if c == 7:
                        ins.then_inc(S.s_x2t, 1)
                n_x2t += 1

    @b2.vector
    def _(eng):
        barrier(eng)
        vch = [0]

        def vs(ins):
            ins.then_inc(S.s_vc2, 1)
            vch[0] += 1

        def vw():
            eng.wait_ge(S.s_vc2, vch[0])

        n_tw = 0
        n_x2 = 0
        for s_i in range(n_super if lvl2b >= 5 else 0):
            for k in range(4):
                i_tile = 4 * s_i + k
                eng.wait_ge(S.s_x1b, 16 * (4 * s_i + k + 1))
                if (s_i > 0 or k > 0) and lvl2b >= 6:
                    eng.wait_ge(S.s_h2t, 4 * s_i + k)
                eng.tensor_scalar(out=H2B[:],
                                  in0=X1S[s_i % 2][:, DM * k:DM * (k + 1)],
                                  scalar1=STR[:, i_tile:i_tile + 1],
                                  scalar2=STB[:, i_tile:i_tile + 1],
                                  op0=MUL, op1=ADD).then_inc(S.s_h2b, 1)
                if lvl2b < 6 or lvl2b == 7:
                    continue
                eng.wait_ge(S.s_h2t, 4 * s_i + k + 1)
                for c in range(8):
                    ins = eng.tensor_copy(
                        OBF[:, 512 * c + P * k:512 * c + P * (k + 1)],
                        pH[:, P * c:P * (c + 1)])
                    if c == 7:
                        ins.then_inc(S.s_obf, 1)
            for cyc in range(n_cyc if lvl2b >= 40 else 0):
                eng.wait_ge(S.s_cbc, n_tw + 1)
                if cyc == 0:
                    eng.wait_ge(S.s_of32, 4 * (s_i + 1))
                pairs = ((0, 4, ADD, SUB), (1, 5, ADD, SUB),
                         (2, 6, SUB, ADD), (3, 7, SUB, ADD))
                vw()
                for (a, b, opa, opb) in pairs:
                    ca = OF32[:, 512 * a:512 * (a + 1)]
                    cb = OF32[:, 512 * b:512 * (b + 1)]
                    vs(eng.tensor_tensor(TWB[:], cb, pC[:], op=MUL))
                    vs(eng.tensor_tensor(TWA[:], ca, pC[:], op=MUL))
                    vw()
                    vs(eng.tensor_tensor(cb, cb, TWA[:], op=opb))
                    vs(eng.tensor_tensor(ca, ca, TWB[:], op=opa))
                    vw()
                    eng.sem_inc(S.s_twp, 1)
                eng.sem_inc(S.s_tw, 1)
                n_tw += 1
            for k in range(4 if lvl2b >= 45 else 0):
                eng.wait_ge(S.s_x2t, n_x2 + 1)
                if n_x2 >= 2 and lvl2b == 50:
                    eng.wait_ge(S.s_outw, 16 * (n_x2 - 1))
                eng.tensor_tensor(X2[n_x2 % 2][:], pX[:],
                                  X1S[s_i % 2][:, DM * k:DM * (k + 1)],
                                  op=ADD).then_inc(S.s_x2, 1)
                n_x2 += 1

    @b2.scalar
    def _(eng):
        barrier(eng)
        n_mm = 0
        n_tw = 0
        n_crow = 0
        for s_i in range(n_super if (lvl2b == 7 or lvl2b >= 10) else 0):
            for k in range(4):
                eng.wait_ge(S.s_h2t, 4 * s_i + k + 1)
                if lvl2b >= 10:
                    eng.wait_ge(S.s_obf, 4 * s_i + k + 1)
                if s_i >= 1 and lvl2b >= 45:
                    eng.wait_ge(S.s_x2t, 4 * s_i)
                for c in range(8):
                    ins = eng.copy(
                        OF32[:, 512 * c + P * k:512 * c + P * (k + 1)],
                        pH[:, P * c:P * (c + 1)])
                    if c == 7:
                        ins.then_inc(S.s_of32, 1)
            for cyc in range(n_cyc if lvl2b >= 20 else 0):
                for (n_o, bias_c, dst) in ((8, B1C, T1B), (4, B2C, T2B),
                                           (2, B3C, T3B)):
                    for o in range(n_o):
                        pp = pA if o % 2 == 0 else pB
                        eng.wait_ge(S.s_mmc, n_mm + 1)
                        n_mm += 1
                        eng.activation(dst[:, 512 * o:512 * (o + 1)], pp[:],
                                       AF.Gelu,
                                       bias=bias_c[:, o:o + 1]).then_inc(S.s_evc, 1)
                if lvl2b < 30:
                    continue
                eng.wait_ge(S.s_zt, n_cyc * s_i + cyc + 1)
                if n_crow >= 1:
                    eng.wait_ge(S.s_crow, n_crow)
                eng.activation(TFR[:], pZ[:], AF.Tanh,
                               bias=B4H[0:1, 0:1], scale=0.5).then_inc(S.s_tfr, 1)
                n_crow += 1
                if cyc < 2 and lvl2b >= 40:
                    for p_i, (aa, bb) in enumerate(((0, 4), (1, 5), (2, 6), (3, 7))):
                        eng.wait_ge(S.s_twp, 4 * (n_cyc * s_i + cyc) + p_i + 1)
                        eng.copy(OBF[:, 512 * aa:512 * (aa + 1)],
                                 OF32[:, 512 * aa:512 * (aa + 1)])
                        eng.copy(OBF[:, 512 * bb:512 * (bb + 1)],
                                 OF32[:, 512 * bb:512 * (bb + 1)]).then_inc(S.s_cast, 1)
            if lvl2b == 50:
                for k in range(4):
                    ki = 4 * s_i + k
                    eng.wait_ge(S.s_x2, ki + 1)
                    eng.dma_start(out_d[P * ki:P * (ki + 1), :],
                                  X2[ki % 2][:]).then_inc(S.s_outw, 16)

    @b2.gpsimd
    def _(eng):
        barrier(eng)
        eng.memset(CROW[:], 0.0).then_inc(S.s_gch, 1)
        eng.wait_ge(S.s_gch, 1)
        for s_i in range(n_super if lvl2b >= 30 else 0):
            seq = s_i // (n_super // n_seq)
            for cyc in range(n_cyc):
                n = n_cyc * s_i + cyc
                eng.wait_ge(S.s_tfr, n + 1)
                eng.tensor_scalar(out=CROW[0:1, :], in0=TFR[:],
                                  scalar1=BETA[0:1, 0:1],
                                  scalar2=ALPHAR[0:1, seq:seq + 1],
                                  op0=MUL, op1=ADD).then_inc(S.s_crow, 1)

    blk2_cm.__exit__(None, None, None)
    st.close()
    return nc, n_tiles


def _host_inputs(inputs, shard):
    f32 = np.float32
    bf = ml_dtypes.bfloat16
    d = {}
    d["x"] = np.ascontiguousarray(shard.reshape(-1, DM))
    d["wqT"] = np.ascontiguousarray(inputs["L0_attn_W_Q"].T)
    d["wkT"] = np.ascontiguousarray(inputs["L0_attn_W_K"].T)
    d["wvT"] = np.ascontiguousarray(inputs["L0_attn_W_V"].T)
    d["w1"] = np.ascontiguousarray(inputs["L0_ffn_cn_w1"])
    d["w2"] = np.ascontiguousarray(inputs["L0_ffn_cn_w2"])
    d["w3"] = np.ascontiguousarray(inputs["L0_ffn_cn_w3"])
    d["w4c"] = np.ascontiguousarray(inputs["L0_ffn_cn_w4"].reshape(2, 128).T)
    d["gw1"] = np.ascontiguousarray(inputs["L0_ffn_gc_w1"])
    d["gw2"] = np.ascontiguousarray(inputs["L0_ffn_gc_w2"])
    d["gw3c"] = np.ascontiguousarray(inputs["L0_ffn_gc_w3"].reshape(2, 128).T)
    d["b1c"] = np.ascontiguousarray(inputs["L0_ffn_cn_b1"].reshape(8, 128).T)
    d["b2c"] = np.ascontiguousarray(inputs["L0_ffn_cn_b2"].reshape(4, 128).T)
    d["b3c"] = np.ascontiguousarray(inputs["L0_ffn_cn_b3"].reshape(2, 128).T)
    d["b4"] = np.ascontiguousarray(inputs["L0_ffn_cn_b4"].reshape(1, 1))
    d["gb1"] = np.ascontiguousarray(inputs["L0_ffn_gc_b1"].reshape(4, 128).T)
    d["gb2"] = np.ascontiguousarray(inputs["L0_ffn_gc_b2"].reshape(2, 128).T)
    d["gb3"] = np.ascontiguousarray(inputs["L0_ffn_gc_b3"].reshape(1, 1))
    d["ar"] = np.ascontiguousarray(inputs["L0_ffn_adaptive_range"].reshape(1, 1))
    d["identb"] = np.eye(P, dtype=bf)
    d["identf"] = np.eye(P, dtype=f32)
    mask1 = np.kron(np.eye(8, dtype=f32), np.ones((16, 16), dtype=f32))
    d["maskc"] = np.tile(mask1, (1, 4)).astype(f32)
    d["onescol"] = np.ones((P, 1), dtype=bf)
    e0 = np.zeros((P, P), dtype=f32)
    e0[0, :] = 1.0
    d["e0c"] = e0
    d["epsc"] = np.full((P, 1), 1e-5, dtype=f32)
    for k in d:
        if d[k].dtype == np.float64:
            raise ValueError(k)
    return d


def kernel(**inputs):
    x = np.asarray(inputs["L0_attn_input_seq_embs"])
    n_cores = 8
    shards = np.split(x, n_cores, axis=0)
    nc, n_tiles = build(n_seq=4)
    in_maps = [_host_inputs(inputs, s) for s in shards]
    res = run_bass_kernel_spmd(nc, in_maps, core_ids=list(range(n_cores)))
    outs = [np.asarray(r["out"], dtype=np.float32) for r in res.results]
    return np.concatenate(outs, axis=0).reshape(32, 2048, 1024)


# version 05
# speedup vs baseline: 1.3765x; 1.0079x over previous
import numpy as np
import ml_dtypes
import concourse.bass as bass
import concourse.mybir as mybir
from concourse.bass_utils import run_bass_kernel_spmd
from concourse.alu_op_type import AluOpType
from contextlib import ExitStack

F32 = mybir.dt.float32
BF16 = mybir.dt.bfloat16
AF = mybir.ActivationFunctionType
AX = mybir.AxisListType
ADD = AluOpType.add
SUB = AluOpType.subtract
MUL = AluOpType.mult
MIN = AluOpType.min

P = 128
DM = 1024
NH = 16
HD = 64
NSEQT = 16


def build(n_seq: int, nseqt: int = None, stage: str = 'full'):
    lvl2b = {'2b0': 0, '2bb': 5, '2bh1': 6, '2bh2': 7, '2bh': 10, '2bc': 20,
             '2bz': 30, '2bt': 40, '2bx': 45}.get(stage, 50)
    stage = 'full'
    global NSEQT
    if nseqt is not None:
        NSEQT = nseqt
    n_tiles = n_seq * NSEQT
    NT = n_tiles * P
    n_super = n_tiles // 4
    nc = bass.Bass("TRN2", target_bir_lowering=False)

    x_in = nc.dram_tensor("x", [NT, DM], F32, kind="ExternalInput").ap()
    wqT = nc.dram_tensor("wqT", [DM, DM], F32, kind="ExternalInput").ap()
    wkT = nc.dram_tensor("wkT", [DM, DM], F32, kind="ExternalInput").ap()
    wvT = nc.dram_tensor("wvT", [DM, DM], F32, kind="ExternalInput").ap()
    w1_in = nc.dram_tensor("w1", [DM, DM], F32, kind="ExternalInput").ap()
    w2_in = nc.dram_tensor("w2", [DM, 512], F32, kind="ExternalInput").ap()
    w3_in = nc.dram_tensor("w3", [512, 256], F32, kind="ExternalInput").ap()
    w4c_in = nc.dram_tensor("w4c", [P, 2], F32, kind="ExternalInput").ap()
    gw1_in = nc.dram_tensor("gw1", [DM, 512], F32, kind="ExternalInput").ap()
    gw2_in = nc.dram_tensor("gw2", [512, 256], F32, kind="ExternalInput").ap()
    gw3c_in = nc.dram_tensor("gw3c", [P, 2], F32, kind="ExternalInput").ap()
    b1c_in = nc.dram_tensor("b1c", [P, 8], F32, kind="ExternalInput").ap()
    b2c_in = nc.dram_tensor("b2c", [P, 4], F32, kind="ExternalInput").ap()
    b3c_in = nc.dram_tensor("b3c", [P, 2], F32, kind="ExternalInput").ap()
    b4_in = nc.dram_tensor("b4", [1, 1], F32, kind="ExternalInput").ap()
    gb1_in = nc.dram_tensor("gb1", [P, 4], F32, kind="ExternalInput").ap()
    gb2_in = nc.dram_tensor("gb2", [P, 2], F32, kind="ExternalInput").ap()
    gb3_in = nc.dram_tensor("gb3", [1, 1], F32, kind="ExternalInput").ap()
    ar_in = nc.dram_tensor("ar", [1, 1], F32, kind="ExternalInput").ap()
    identb_in = nc.dram_tensor("identb", [P, P], BF16, kind="ExternalInput").ap()
    identf_in = nc.dram_tensor("identf", [P, P], F32, kind="ExternalInput").ap()
    mask_in = nc.dram_tensor("maskc", [P, 512], F32, kind="ExternalInput").ap()
    onescol_in = nc.dram_tensor("onescol", [P, 1], BF16, kind="ExternalInput").ap()
    e0_in = nc.dram_tensor("e0c", [P, P], F32, kind="ExternalInput").ap()
    eps_in = nc.dram_tensor("epsc", [P, 1], F32, kind="ExternalInput").ap()

    out_d = nc.dram_tensor("out", [NT, DM], F32, kind="ExternalOutput").ap()
    sq_scr = nc.dram_tensor("sq_scr", [NT, DM], BF16, kind="Internal").ap()
    sk_scr = nc.dram_tensor("sk_scr", [NT, DM], BF16, kind="Internal").ap()
    v_scr = nc.dram_tensor("v_scr", [NT, DM], BF16, kind="Internal").ap()
    x1_scr = nc.dram_tensor("x1_scr", [NT, DM], F32, kind="Internal").ap()

    st = ExitStack()
    E_ = st.enter_context

    WQ = E_(nc.sbuf_tensor("WQ", [P, 8 * DM], BF16))
    WK = E_(nc.sbuf_tensor("WK", [P, 8 * DM], BF16))
    WV = E_(nc.sbuf_tensor("WV", [P, 8 * DM], BF16))
    W1 = E_(nc.sbuf_tensor("W1", [P, 8 * DM], BF16))
    W2 = E_(nc.sbuf_tensor("W2", [P, 8 * 512], BF16))
    W3 = E_(nc.sbuf_tensor("W3", [P, 4 * 256], BF16))
    W4 = E_(nc.sbuf_tensor("W4", [P, 2], BF16))
    G1 = E_(nc.sbuf_tensor("G1", [P, 8 * 512], BF16))
    G2 = E_(nc.sbuf_tensor("G2", [P, 4 * 256], BF16))
    G3 = E_(nc.sbuf_tensor("G3", [P, 2], BF16))
    B1C = E_(nc.sbuf_tensor("B1C", [P, 8], F32))
    B2C = E_(nc.sbuf_tensor("B2C", [P, 4], F32))
    B3C = E_(nc.sbuf_tensor("B3C", [P, 2], F32))
    B4 = E_(nc.sbuf_tensor("B4", [1, 1], F32))
    GB1C = E_(nc.sbuf_tensor("GB1C", [P, 4], F32))
    GB2C = E_(nc.sbuf_tensor("GB2C", [P, 2], F32))
    E0C = E_(nc.sbuf_tensor("E0C", [P, P], F32))
    GB3 = E_(nc.sbuf_tensor("GB3", [1, 1], F32))
    ARC = E_(nc.sbuf_tensor("ARC", [1, 1], F32))
    IDB = E_(nc.sbuf_tensor("IDB", [P, P], BF16))
    IDF = E_(nc.sbuf_tensor("IDF", [P, P], F32))
    MASKC = E_(nc.sbuf_tensor("MASKC", [P, 512], F32))
    ONESC = E_(nc.sbuf_tensor("ONESC", [P, 1], BF16))
    EPSC = E_(nc.sbuf_tensor("EPSC", [P, 1], F32))
    STR = E_(nc.sbuf_tensor("STR", [P, n_tiles], F32))
    STB = E_(nc.sbuf_tensor("STB", [P, n_tiles], F32))
    GCT = E_(nc.sbuf_tensor("GCT", [P, 4 * 8], BF16))
    GCTA = E_(nc.sbuf_tensor("GCTA", [P, 4 * 8], F32))
    GFROW = E_(nc.sbuf_tensor("GFROW", [1, 4], F32))
    ALPHAR = E_(nc.sbuf_tensor("ALPHAR", [1, 4], F32))
    BETA = E_(nc.sbuf_tensor("BETA", [1, 1], F32))
    B4H = E_(nc.sbuf_tensor("B4H", [1, 1], F32))
    GB3H = E_(nc.sbuf_tensor("GB3H", [1, 1], F32))
    GS1F = E_(nc.sbuf_tensor("GS1F", [P, 16], BF16))
    GS2F = E_(nc.sbuf_tensor("GS2F", [P, 8], BF16))
    W4F = E_(nc.sbuf_tensor("W4F", [P, 2], F32))
    G3F = E_(nc.sbuf_tensor("G3F", [P, 2], F32))

    cast_plan = []

    def plan_w(dram, sbuf, total_cols):
        v = dram.rearrange("(c p) n -> p c n", p=P)
        nchunk = v.shape[1]
        ncols = v.shape[2]
        per = max(1, 2048 // ncols)
        c0 = 0
        while c0 < nchunk:
            k = min(per, nchunk - c0)
            cast_plan.append((v[:, c0:c0 + k, :],
                              sbuf[:, c0 * ncols:(c0 + k) * ncols], k * ncols))
            c0 += k

    plan_w(wqT, WQ, 8 * DM)
    plan_w(wkT, WK, 8 * DM)
    plan_w(wvT, WV, 8 * DM)
    plan_w(w1_in, W1, 8 * DM)
    plan_w(w2_in, W2, 8 * 512)
    plan_w(w3_in, W3, 4 * 256)
    plan_w(gw1_in, G1, 8 * 512)
    plan_w(gw2_in, G2, 4 * 256)
    n_rounds = len(cast_plan)

    st1 = ExitStack()
    E1 = st1.enter_context
    XB = [E1(nc.sbuf_tensor(f"XB{k}", [P, DM], F32)) for k in range(2)]
    EBUF = E1(nc.sbuf_tensor("EBUF", [P, 2 * DM], F32))
    DUMP = E1(nc.sbuf_tensor("DUMP", [P, DM], F32))
    LSUM = E1(nc.sbuf_tensor("LSUM", [P, 1], F32))
    LNM = E1(nc.sbuf_tensor("LNM", [P, 1], F32))
    LVAR = E1(nc.sbuf_tensor("LVAR", [P, 1], F32))
    LRS = E1(nc.sbuf_tensor("LRS", [P, 1], F32))
    LNB = E1(nc.sbuf_tensor("LNB", [P, 1], F32))
    ASUM = E1(nc.sbuf_tensor("ASUM", [P, 1], F32))
    ANM = E1(nc.sbuf_tensor("ANM", [P, 1], F32))
    AVAR = E1(nc.sbuf_tensor("AVAR", [P, 1], F32))
    ATMP = E1(nc.sbuf_tensor("ATMP", [P, 1], F32))
    NMQ = E1(nc.sbuf_tensor("NMQ", [P, 1], F32))
    NMK = E1(nc.sbuf_tensor("NMK", [P, 1], F32))
    TMP1 = E1(nc.sbuf_tensor("TMP1", [P, 1], F32))
    TMPK = E1(nc.sbuf_tensor("TMPK", [P, 1], F32))
    ZQ = E1(nc.sbuf_tensor("ZQ", [P, NH], F32))
    ZK = E1(nc.sbuf_tensor("ZK", [P, HD], F32))
    HBF = [E1(nc.sbuf_tensor(f"HBF{k}", [P, DM], BF16)) for k in range(2)]
    HFM = [E1(nc.sbuf_tensor(f"HFM{k}", [P, DM], BF16)) for k in range(2)]
    SQTM = [E1(nc.sbuf_tensor(f"SQTM{k}", [P, DM], BF16)) for k in range(2)]
    SKTM = [E1(nc.sbuf_tensor(f"SKTM{k}", [P, DM], BF16)) for k in range(2)]
    VTM = [E1(nc.sbuf_tensor(f"VTM{k}", [P, DM], BF16)) for k in range(2)]
    SQS = [E1(nc.sbuf_tensor(f"SQS{k}", [P, DM], BF16)) for k in range(2)]
    SKS = [E1(nc.sbuf_tensor(f"SKS{k}", [P, DM], BF16)) for k in range(2)]
    VS = [E1(nc.sbuf_tensor(f"VS{k}", [P, DM], BF16)) for k in range(2)]
    TSQ = [E1(nc.sbuf_tensor(f"TSQ{k}", [P, 2 * DM], BF16)) for k in range(2)]
    TSK = [E1(nc.sbuf_tensor(f"TSK{k}", [P, 2 * DM], BF16)) for k in range(2)]
    MBD = [E1(nc.sbuf_tensor(f"MBD{k}", [P, 512], BF16)) for k in range(2)]
    AS = [E1(nc.sbuf_tensor(f"AS{k}", [P, DM], F32)) for k in range(2)]
    X1A = [E1(nc.sbuf_tensor(f"X1A{k}", [P, DM], F32)) for k in range(2)]
    H2A = [E1(nc.sbuf_tensor(f"H2A{k}", [P, DM], BF16)) for k in range(2)]

    p_t1 = E1(nc.psum_tensor("p_t1", [P, DM], BF16))
    p_tsq = E1(nc.psum_tensor("p_tsq", [64, DM], BF16))
    p_q = E1(nc.psum_tensor("p_q", [P, DM], F32))
    p_k = E1(nc.psum_tensor("p_k", [P, DM], F32))
    p_v = E1(nc.psum_tensor("p_v", [P, DM], F32))

    sem_names = ("s_wld s_wcast s_cld s_x s_stk s_x1c s_wscr s_acc "
                 "s_hbf s_nm s_sm s_tse s_mask s_hfm s_eq s_ek s_vev s_aev "
                 "s_ht s_qq s_qk s_qv s_ts1 s_ts2 s_gram s_app4 "
                 "s_x1l s_h2a s_gsT s_gct s_gfp s_gfv s_gfa "
                 "s_x1b s_h2b s_h2t s_obf s_of32 s_mmc s_evc s_zt s_tfr "
                 "s_crow s_cbc s_tw s_cast s_x2t s_x2 s_outw s_gse s_vch s_ach s_vc2 s_gch s_twp").split()
    SM = {}
    for name in sem_names:
        SM[name] = E_(nc.semaphore(name))
    S = type("S", (), SM)

    def stk_view(dram, i):
        return dram[P * i:P * (i + 1), :].rearrange(
            "(g tl) (h d) -> (tl h) g d", g=16, tl=8, h=16)

    N_CLD = 16

    blk1_cm = nc.Block()
    b1 = blk1_cm.__enter__()

    @b1.sync
    def _(eng):
        for ap, dst in ((identb_in, IDB), (identf_in, IDF), (mask_in, MASKC),
                        (onescol_in, ONESC),
                        (e0_in, E0C),
                        (b1c_in, B1C), (b2c_in, B2C), (b3c_in, B3C),
                        (b4_in, B4), (gb1_in, GB1C), (gb2_in, GB2C),
                        (gb3_in, GB3), (ar_in, ARC), (eps_in, EPSC),
                        (w4c_in, W4F), (gw3c_in, G3F)):
            eng.dma_start(dst[:], ap).then_inc(S.s_cld, 16)
        for r, (src, dst, n) in enumerate(cast_plan):
            eng.wait_ge(S.s_wcast, r)
            eng.dma_start(EBUF[:, 0:n], src).then_inc(S.s_wld, 16)

        def load_2a(j):
            eng.wait_ge(S.s_acc, 16 * (j + 1))
            if j >= 2:
                eng.wait_ge(S.s_h2a, j - 1)
            eng.dma_start(X1A[j % 2][:],
                          x1_scr[P * j:P * (j + 1), :]).then_inc(S.s_x1l, 16)

        def stack_read(t):
            eng.wait_ge(S.s_wscr, 48 * (t + 1))
            if t >= 2:
                eng.wait_ge(S.s_gram, 4 * (t - 1))
                eng.wait_ge(S.s_app4, 4 * (t - 1))
            eng.dma_start(SQS[t % 2][:], stk_view(sq_scr, t)).then_inc(S.s_stk, 16)
            eng.dma_start(SKS[t % 2][:], stk_view(sk_scr, t)).then_inc(S.s_stk, 16)
            eng.dma_start(VS[t % 2][:], stk_view(v_scr, t)).then_inc(S.s_stk, 16)

        for i in range(n_tiles):
            if i >= 2:
                eng.wait_ge(S.s_hbf, i - 1)
                eng.wait_ge(S.s_x1c, 16 * (i - 1))
            else:
                eng.wait_ge(S.s_wcast, n_rounds + 1)
            eng.dma_start(XB[i % 2][:], x_in[P * i:P * (i + 1), :]).then_inc(S.s_x, 16)
            if i >= 1:
                stack_read(i - 1)
            if i >= 2:
                load_2a(i - 2)
        stack_read(n_tiles - 1)
        for j in range(n_tiles - 2, n_tiles):
            load_2a(j)

    @b1.gpsimd
    def _(eng):
        def accum(t):
            eng.wait_ge(S.s_aev, 2 * (t + 1))
            eng.dma_start(
                x1_scr[P * t:P * (t + 1), :].rearrange(
                    "(g tl) (h e) -> (tl h) g e", g=16, tl=8, h=16),
                AS[t % 2][:], accum_op=ADD).then_inc(S.s_acc, 16)

        for i in range(n_tiles):
            eng.wait_ge(S.s_x, 16 * (i + 1))
            eng.dma_start(x1_scr[P * i:P * (i + 1), :],
                          XB[i % 2][:]).then_inc(S.s_x1c, 16)
            eng.wait_ge(S.s_sm, 2 * i + 1)
            eng.dma_start(sq_scr[P * i:P * (i + 1), :], SQTM[i % 2][:]).then_inc(S.s_wscr, 16)
            eng.wait_ge(S.s_sm, 2 * i + 2)
            eng.dma_start(sk_scr[P * i:P * (i + 1), :], SKTM[i % 2][:]).then_inc(S.s_wscr, 16)
            eng.wait_ge(S.s_vev, 2 * (i + 1))
            eng.dma_start(v_scr[P * i:P * (i + 1), :], VTM[i % 2][:]).then_inc(S.s_wscr, 16)
            if i >= 1:
                accum(i - 1)
        accum(n_tiles - 1)

    @b1.tensor
    def _(eng):
        eng.wait_ge(S.s_wcast, n_rounds + 1)
        eng.wait_ge(S.s_cld, 16 * N_CLD)

        def pe_gsum(j):
            eng.wait_ge(S.s_h2a, j + 1)
            if j >= 1:
                eng.wait_ge(S.s_gse, j)
            eng.wait_ge(S.s_vev, min(2 * (j + 4) + 1, 2 * n_tiles))
            for c in range(8):
                ins = eng.matmul(p_v[:, c:c + 1],
                                 H2A[j % 2][:, P * c:P * (c + 1)], ONESC[:],
                                 start=True, stop=True)
                if c == 7:
                    ins.then_inc(S.s_gsT, 1)

        for i in range(n_tiles + 1):
            t = i - 1
            if i < n_tiles:
                eng.wait_ge(S.s_hbf, i + 1)
                if i >= 2:
                    eng.wait_ge(S.s_tse, 4 * (i - 1))
                for c in range(8):
                    ins = eng.transpose(p_t1[:, P * c:P * (c + 1)],
                                        HBF[i % 2][:, P * c:P * (c + 1)], IDB[:])
                    if c == 7:
                        ins.then_inc(S.s_ht, 1)
                eng.wait_ge(S.s_hfm, i + 1)
                if i >= 1:
                    eng.wait_ge(S.s_eq, 2 * i)
                    eng.wait_ge(S.s_mask, 4 * (i - 1))
                for j in range(8):
                    lhs = HFM[i % 2][:, P * j:P * (j + 1)]
                    eng.matmul(p_q[:, 0:512], lhs, WQ[:, DM * j:DM * j + 512],
                               start=(j == 0), stop=(j == 7))
                    ins = eng.matmul(p_q[:, 512:DM], lhs,
                                     WQ[:, DM * j + 512:DM * (j + 1)],
                                     start=(j == 0), stop=(j == 7))
                    if j == 7:
                        ins.then_inc(S.s_qq, 1)
            if i >= 1:
                eng.wait_ge(S.s_stk, 48 * i)
                if i >= 2:
                    eng.wait_ge(S.s_tse, 4 * (i - 2) + 2)
                for gg in range(8):
                    ins = eng.transpose(p_tsq[0:64, P * gg:P * (gg + 1)],
                                        SQS[t % 2][:, HD * gg:HD * (gg + 1)],
                                        IDB[:])
                    if gg == 7:
                        ins.then_inc(S.s_ts1, 1)
            if i < n_tiles:
                if i >= 1:
                    eng.wait_ge(S.s_ek, 2 * i)
                if i >= 2:
                    eng.wait_ge(S.s_aev, 2 * (i - 1))
                for j in range(8):
                    lhs = HFM[i % 2][:, P * j:P * (j + 1)]
                    eng.matmul(p_k[:, 0:512], lhs, WK[:, DM * j:DM * j + 512],
                               start=(j == 0), stop=(j == 7))
                    ins = eng.matmul(p_k[:, 512:DM], lhs,
                                     WK[:, DM * j + 512:DM * (j + 1)],
                                     start=(j == 0), stop=(j == 7))
                    if j == 7:
                        ins.then_inc(S.s_qk, 1)
            if i >= 1:
                eng.wait_ge(S.s_tse, 4 * t + 1)
                for gg in range(8):
                    g = 8 + gg
                    ins = eng.transpose(p_tsq[0:64, P * gg:P * (gg + 1)],
                                        SQS[t % 2][:, HD * g:HD * (g + 1)],
                                        IDB[:])
                    if gg == 7:
                        ins.then_inc(S.s_ts1, 1)
                if i == n_tiles:
                    eng.wait_ge(S.s_hfm, n_tiles)
                if i >= 2:
                    eng.wait_ge(S.s_tse, 4 * (i - 2) + 4)
                for gg in range(8):
                    ins = eng.transpose(p_t1[0:64, P * gg:P * (gg + 1)],
                                        SKS[t % 2][:, HD * gg:HD * (gg + 1)],
                                        IDB[:])
                    if gg == 7:
                        ins.then_inc(S.s_ts2, 1)
            if i < n_tiles:
                if i >= 1:
                    eng.wait_ge(S.s_vev, 2 * i - 1)
                if i >= 5:
                    eng.wait_ge(S.s_gse, i - 4)
                for j in range(8):
                    lhs = HFM[i % 2][:, P * j:P * (j + 1)]
                    ins = eng.matmul(p_v[:, 0:512], lhs,
                                     WV[:, DM * j:DM * j + 512],
                                     start=(j == 0), stop=(j == 7))
                    if j == 7:
                        ins.then_inc(S.s_qv, 1)
            if i >= 1:
                eng.wait_ge(S.s_tse, 4 * t + 3)
                for gg in range(8):
                    g = 8 + gg
                    ins = eng.transpose(p_t1[0:64, P * gg:P * (gg + 1)],
                                        SKS[t % 2][:, HD * g:HD * (g + 1)],
                                        IDB[:])
                    if gg == 7:
                        ins.then_inc(S.s_ts2, 1)
            if i < n_tiles:
                if i >= 1:
                    eng.wait_ge(S.s_vev, 2 * i)
                for j in range(8):
                    lhs = HFM[i % 2][:, P * j:P * (j + 1)]
                    ins = eng.matmul(p_v[:, 512:DM], lhs,
                                     WV[:, DM * j + 512:DM * (j + 1)],
                                     start=(j == 0), stop=(j == 7))
                    if j == 7:
                        ins.then_inc(S.s_qv, 1)
            if i >= 1:
                for r in range(4):
                    gbase = 512 if r % 2 == 0 else 0
                    eng.wait_ge(S.s_tse, 4 * t + (3 if r < 2 else 4))
                    if r == 0:
                        eng.wait_ge(S.s_eq, min(2 * (i + 1), 2 * n_tiles))
                    elif r >= 2:
                        eng.wait_ge(S.s_mask, 4 * t + r - 1)
                    for gg in range(4):
                        g = 4 * r + gg
                        ins = eng.matmul(
                            p_q[:, gbase + P * gg:gbase + P * (gg + 1)],
                            TSK[t % 2][:, P * g:P * (g + 1)],
                            TSQ[t % 2][:, P * g:P * (g + 1)],
                            start=True, stop=True)
                        if gg == 3:
                            ins.then_inc(S.s_gram, 1)
                    if r >= 1:
                        b = r - 1
                        eng.wait_ge(S.s_mask, 4 * t + b + 1)
                        if b == 0:
                            eng.wait_ge(S.s_ek, min(2 * i + 1, 2 * n_tiles - 1))
                        elif b == 2:
                            eng.wait_ge(S.s_ek, min(2 * (i + 1), 2 * n_tiles))
                        for gg in range(4):
                            g = 4 * b + gg
                            ins = eng.matmul(
                                p_k[:, HD * g:HD * (g + 1)],
                                MBD[b % 2][:, P * gg:P * (gg + 1)],
                                VS[t % 2][:, HD * g:HD * (g + 1)],
                                start=True, stop=True)
                            if gg == 3:
                                ins.then_inc(S.s_app4, 1)
                eng.wait_ge(S.s_mask, 4 * t + 4)
                for gg in range(4):
                    g = 12 + gg
                    ins = eng.matmul(p_k[:, HD * g:HD * (g + 1)],
                                     MBD[1][:, P * gg:P * (gg + 1)],
                                     VS[t % 2][:, HD * g:HD * (g + 1)],
                                     start=True, stop=True)
                    if gg == 3:
                        ins.then_inc(S.s_app4, 1)
            if i >= 4:
                pe_gsum(i - 4)
        for j in range(n_tiles - 3, n_tiles):
            pe_gsum(j)
        eng.wait_ge(S.s_gct, 1)
        gft = 0
        for oc in range(4):
            if gft >= 1:
                eng.wait_ge(S.s_gfa, gft)
            for c in range(8):
                gct_c = GCT[:].rearrange("p (s c) -> p c s", c=8)[:, c, :]
                ins = eng.matmul(p_q[:, 0:4], G1[:, 512 * c + P * oc:512 * c + P * (oc + 1)],
                                 gct_c, start=(c == 0), stop=(c == 7))
                if c == 7:
                    ins.then_inc(S.s_gfp, 1)
            gft += 1
        for o2 in range(2):
            eng.wait_ge(S.s_gfa, gft)
            for c2 in range(4):
                ins = eng.matmul(p_q[:, 0:4],
                                 G2[:, 256 * c2 + P * o2:256 * c2 + P * (o2 + 1)],
                                 GS1F[:, 4 * c2:4 * c2 + 4],
                                 start=(c2 == 0), stop=(c2 == 3))
                if c2 == 3:
                    ins.then_inc(S.s_gfp, 1)
            gft += 1
        eng.wait_ge(S.s_gfa, gft)
        for c3 in range(2):
            ins = eng.matmul(p_q[0:1, 0:4], G3[:, c3:c3 + 1],
                             GS2F[:, 4 * c3:4 * c3 + 4],
                             start=(c3 == 0), stop=(c3 == 1))
            if c3 == 1:
                ins.then_inc(S.s_gfp, 1)

    @b1.vector
    def _(eng):
        vch = [0]

        def vs(ins):
            ins.then_inc(S.s_vch, 1)
            vch[0] += 1

        def vw():
            eng.wait_ge(S.s_vch, vch[0])

        for r, (src, dst, n) in enumerate(cast_plan):
            eng.wait_ge(S.s_wld, 16 * (r + 1))
            eng.tensor_copy(dst, EBUF[:, 0:n]).then_inc(S.s_wcast, 1)
        eng.wait_ge(S.s_cld, 16 * N_CLD)
        eng.tensor_copy(W4[:], W4F[:])
        eng.tensor_copy(G3[:], G3F[:])
        eng.memset(GCTA[:], 0.0)
        eng.tensor_scalar_mul(B4H[:], B4[:], 0.5)
        eng.tensor_scalar_mul(GB3H[:], GB3[:], 0.5)
        for k_t, t_b in enumerate(TSQ + TSK):
            ins = eng.memset(t_b[64:P, :], 0.0)
        ins.then_inc(S.s_wcast, 1)

        def gcta(j):
            eng.wait_ge(S.s_gsT, j + 1)
            sq = j // NSEQT
            eng.tensor_tensor(GCTA[:, 8 * sq:8 * sq + 8],
                              GCTA[:, 8 * sq:8 * sq + 8],
                              p_v[:, 0:8], op=ADD).then_inc(S.s_gse, 1)

        for i in range(n_tiles + 1):
            t = i - 1
            if i < n_tiles:
                eng.wait_ge(S.s_ht, i + 1)
                if i >= 2:
                    eng.wait_ge(S.s_qv, 2 * (i - 1))
                eng.tensor_copy(HFM[i % 2][:], p_t1[:]).then_inc(S.s_hfm, 1)
                eng.wait_ge(S.s_qq, i + 1)
                vs(eng.reduce_max(NMQ[:], p_q[:, 0:512], axis=AX.X, negate=True))
                vs(eng.reduce_max(TMP1[:], p_q[:, 512:DM], axis=AX.X, negate=True))
                vw()
                eng.tensor_tensor(NMQ[:], NMQ[:], TMP1[:], op=MIN).then_inc(S.s_nm, 1)
            if i >= 1:
                eng.wait_ge(S.s_ts1, 2 * t + 1)
                eng.tensor_copy(TSQ[t % 2][0:64, 0:DM],
                                p_tsq[0:64, :]).then_inc(S.s_tse, 1)
            if i < n_tiles:
                eng.wait_ge(S.s_qk, i + 1)
                vs(eng.reduce_max(NMK[:], p_k[:, 0:512], axis=AX.X, negate=True))
                vs(eng.reduce_max(TMPK[:], p_k[:, 512:DM], axis=AX.X, negate=True))
                vw()
                eng.tensor_tensor(NMK[:], NMK[:], TMPK[:], op=MIN).then_inc(S.s_nm, 1)
            if i >= 1:
                eng.wait_ge(S.s_ts1, 2 * t + 2)
                eng.tensor_copy(TSQ[t % 2][0:64, DM:2 * DM],
                                p_tsq[0:64, :]).then_inc(S.s_tse, 1)
                eng.wait_ge(S.s_ts2, 2 * t + 1)
                eng.tensor_copy(TSK[t % 2][0:64, 0:DM],
                                p_t1[0:64, :]).then_inc(S.s_tse, 1)
            if i < n_tiles:
                eng.wait_ge(S.s_eq, 2 * (i + 1))
                eq = EBUF[:, 0:DM]
                vs(eng.reduce_sum(ZQ[:], eq.rearrange("p (h d) -> p h d", h=NH), axis=AX.X))
                vw()
                vs(eng.reciprocal(ZQ[:], ZQ[:]))
                vw()
                if i >= 2:
                    eng.wait_ge(S.s_wscr, 48 * (i - 1))
                eng.tensor_tensor(
                    SQTM[i % 2][:].rearrange("p (h d) -> p h d", h=NH),
                    eq.rearrange("p (h d) -> p h d", h=NH),
                    ZQ[:, :, None].broadcast_to([P, NH, HD]),
                    op=MUL).then_inc(S.s_sm, 1)
            if i >= 1:
                eng.wait_ge(S.s_ts2, 2 * t + 2)
                eng.tensor_copy(TSK[t % 2][0:64, DM:2 * DM],
                                p_t1[0:64, :]).then_inc(S.s_tse, 1)
            if i < n_tiles:
                eng.wait_ge(S.s_ek, 2 * (i + 1))
                ek = EBUF[:, DM:2 * DM]
                vs(eng.reduce_sum(ZK[:], ek.rearrange("p (h d) -> p d h", h=NH), axis=AX.X))
                vw()
                vs(eng.reciprocal(ZK[:], ZK[:]))
                vw()
                eng.tensor_tensor(
                    SKTM[i % 2][:].rearrange("p (h d) -> p h d", h=NH),
                    ek.rearrange("p (h d) -> p h d", h=NH),
                    ZK[:, None, :].broadcast_to([P, NH, HD]),
                    op=MUL).then_inc(S.s_sm, 1)
            if i >= 1:
                for r in range(4):
                    gbase = 512 if r % 2 == 0 else 0
                    eng.wait_ge(S.s_gram, 4 * t + r + 1)
                    if 4 * t + r - 1 >= 1:
                        eng.wait_ge(S.s_app4, 4 * t + r - 1)
                    eng.tensor_tensor(MBD[r % 2][:], p_q[:, gbase:gbase + 512],
                                      MASKC[:], op=MUL).then_inc(S.s_mask, 1)
            if i >= 4:
                gcta(i - 4)
        for j in range(n_tiles - 3, n_tiles):
            gcta(j)
        eng.wait_ge(S.s_gse, n_tiles)
        eng.tensor_scalar_mul(GCT[:], GCTA[:], 1.0 / (NSEQT * P)).then_inc(S.s_gct, 1)
        eng.wait_ge(S.s_gfa, 7)
        vs(eng.tensor_scalar_mul(ALPHAR[:], GFROW[:], 0.7))
        vw()
        vs(eng.tensor_scalar(out=ALPHAR[:], in0=ALPHAR[:], scalar1=ARC[0:1, 0:1],
                             scalar2=0.1, op0=MUL, op1=ADD))
        vw()
        eng.tensor_scalar_mul(BETA[:], ARC[0:1, 0:1], 0.3).then_inc(S.s_gfv, 3)

    @b1.scalar
    def _(eng):
        eng.wait_ge(S.s_cld, 16 * N_CLD)
        ach = [0]

        def as_(ins):
            ins.then_inc(S.s_ach, 1)
            ach[0] += 1

        def aw():
            eng.wait_ge(S.s_ach, ach[0])

        def act_ln(j):
            eng.wait_ge(S.s_x, 16 * (j + 1))
            as_(eng.activation(DUMP[:], XB[j % 2][:], AF.Copy,
                               accum_out=LSUM[:, 0:1]))
            aw()
            as_(eng.mul(LNM[:], LSUM[:], -1.0 / DM))
            aw()
            as_(eng.activation(DUMP[:], XB[j % 2][:], AF.Square,
                               bias=LNM[:, 0:1], accum_out=LVAR[:, 0:1]))
            aw()
            as_(eng.activation(LRS[:], LVAR[:], AF.Ln, bias=EPSC[:, 0:1],
                               scale=1.0 / DM))
            aw()
            as_(eng.activation(LRS[:], LRS[:], AF.Exp, scale=-0.5))
            aw()
            as_(eng.mul(LNB[:], LNM[:], LRS[:, 0:1]))
            aw()
            if j >= 2:
                eng.wait_ge(S.s_ht, j - 1)
            eng.activation(HBF[j % 2][:], XB[j % 2][:], AF.Identity,
                           scale=LRS[:, 0:1], bias=LNB[:, 0:1]).then_inc(S.s_hbf, 1)

        def act_2a(j):
            eng.wait_ge(S.s_x1l, 16 * (j + 1))
            as_(eng.activation(DUMP[:], X1A[j % 2][:], AF.Copy,
                               accum_out=ASUM[:, 0:1]))
            aw()
            as_(eng.mul(ANM[:], ASUM[:], -1.0 / DM))
            aw()
            as_(eng.activation(DUMP[:], X1A[j % 2][:], AF.Square,
                               bias=ANM[:, 0:1], accum_out=AVAR[:, 0:1]))
            aw()
            as_(eng.activation(ATMP[:], AVAR[:], AF.Ln, bias=EPSC[:, 0:1],
                               scale=1.0 / DM))
            aw()
            as_(eng.activation(STR[:, j:j + 1], ATMP[:], AF.Exp, scale=-0.5))
            aw()
            as_(eng.mul(STB[:, j:j + 1], ANM[:], STR[:, j:j + 1]))
            aw()
            if j >= 2:
                eng.wait_ge(S.s_gsT, j - 1)
            eng.activation(H2A[j % 2][:], X1A[j % 2][:], AF.Identity,
                           scale=STR[:, j:j + 1],
                           bias=STB[:, j:j + 1]).then_inc(S.s_h2a, 1)

        def aev(t):
            eng.wait_ge(S.s_app4, 4 * t + 2)
            if t >= 2:
                eng.wait_ge(S.s_acc, 16 * (t - 1))
            eng.copy(AS[t % 2][:, 0:512], p_k[:, 0:512]).then_inc(S.s_aev, 1)
            eng.wait_ge(S.s_app4, 4 * t + 4)
            eng.copy(AS[t % 2][:, 512:DM], p_k[:, 512:DM]).then_inc(S.s_aev, 1)

        act_ln(0)
        for i in range(n_tiles):
            if i + 1 < n_tiles:
                act_ln(i + 1)
            eng.wait_ge(S.s_nm, 2 * i + 1)
            if i >= 1:
                eng.wait_ge(S.s_sm, 2 * i - 1)
            eng.activation(EBUF[:, 0:512], p_q[:, 0:512], AF.Exp,
                           bias=NMQ[:, 0:1]).then_inc(S.s_eq, 1)
            eng.activation(EBUF[:, 512:DM], p_q[:, 512:DM], AF.Exp,
                           bias=NMQ[:, 0:1]).then_inc(S.s_eq, 1)
            eng.wait_ge(S.s_nm, 2 * i + 2)
            if i >= 1:
                eng.wait_ge(S.s_sm, 2 * i)
            eng.activation(EBUF[:, DM:DM + 512], p_k[:, 0:512], AF.Exp,
                           bias=NMK[:, 0:1]).then_inc(S.s_ek, 1)
            eng.activation(EBUF[:, DM + 512:2 * DM], p_k[:, 512:DM], AF.Exp,
                           bias=NMK[:, 0:1]).then_inc(S.s_ek, 1)
            eng.wait_ge(S.s_qv, 2 * i + 1)
            if i >= 2:
                eng.wait_ge(S.s_wscr, 48 * (i - 1))
            eng.copy(VTM[i % 2][:, 0:512], p_v[:, 0:512]).then_inc(S.s_vev, 1)
            eng.wait_ge(S.s_qv, 2 * i + 2)
            eng.copy(VTM[i % 2][:, 512:DM], p_v[:, 512:DM]).then_inc(S.s_vev, 1)
            if i >= 3:
                act_2a(i - 3)
            if i >= 1:
                aev(i - 1)
        aev(n_tiles - 1)
        for j in range(n_tiles - 3, n_tiles):
            act_2a(j)
        for oc in range(4):
            eng.wait_ge(S.s_gfp, oc + 1)
            eng.activation(GS1F[:, 4 * oc:4 * (oc + 1)], p_q[:, 0:4], AF.Gelu,
                           bias=GB1C[:, oc:oc + 1]).then_inc(S.s_gfa, 1)
        for o2 in range(2):
            eng.wait_ge(S.s_gfp, 4 + o2 + 1)
            eng.activation(GS2F[:, 4 * o2:4 * (o2 + 1)], p_q[:, 0:4], AF.Gelu,
                           bias=GB2C[:, o2:o2 + 1]).then_inc(S.s_gfa, 1)
        eng.wait_ge(S.s_gfp, 7)
        eng.activation(GFROW[:], p_q[0:1, 0:4], AF.Tanh,
                       bias=GB3H[0:1, 0:1], scale=0.5).then_inc(S.s_gfa, 1)

    blk1_cm.__exit__(None, None, None)
    st1.close()
    E2 = st.enter_context
    X1S = [E2(nc.sbuf_tensor(f"X1S{k}", [P, 4 * DM], F32)) for k in range(2)]
    H2B = E2(nc.sbuf_tensor("H2B", [P, DM], BF16))
    OBF = E2(nc.sbuf_tensor("OBF", [P, 8 * 512], BF16))
    OF32 = E2(nc.sbuf_tensor("OF32", [P, 8 * 512], F32))
    T1B = E2(nc.sbuf_tensor("T1B", [P, 8 * 512], BF16))
    T2B = E2(nc.sbuf_tensor("T2B", [P, 4 * 512], BF16))
    T3B = E2(nc.sbuf_tensor("T3B", [P, 2 * 512], BF16))
    TFR = E2(nc.sbuf_tensor("TFR", [1, 512], F32))
    CROW = E2(nc.sbuf_tensor("CROW", [P, 512], F32))
    TWA = E2(nc.sbuf_tensor("TWA", [P, 512], F32))
    TWB = E2(nc.sbuf_tensor("TWB", [P, 512], F32))
    X2 = [E2(nc.sbuf_tensor(f"X2_{k}", [P, DM], F32)) for k in range(2)]
    pA = E2(nc.psum_tensor("pA", [P, 512], F32))
    pB = E2(nc.psum_tensor("pB", [P, 512], F32))
    pZ = E2(nc.psum_tensor("pZ", [1, 512], F32))
    pC = E2(nc.psum_tensor("pC", [P, 512], F32))
    pH = E2(nc.psum_tensor("pH", [P, DM], BF16))
    pX = E2(nc.psum_tensor("pX", [P, DM], F32))

    def barrier(eng):
        eng.wait_ge(S.s_wcast, n_rounds + 1)
        eng.wait_ge(S.s_acc, 16 * n_tiles)
        eng.wait_ge(S.s_h2a, n_tiles)
        eng.wait_ge(S.s_gsT, n_tiles)
        eng.wait_ge(S.s_gfa, 7)
        eng.wait_ge(S.s_gfv, 3)

    blk2_cm = nc.Block()
    b2 = blk2_cm.__enter__()

    import os as _os0
    n_cyc = 3 if lvl2b >= 40 else 1
    if _os0.environ.get('KCYC'):
        n_cyc = int(_os0.environ['KCYC'])

    @b2.sync
    def _(eng):
        barrier(eng)
        for s_i in range(n_super):
            for k in range(4):
                if s_i >= 2:
                    if lvl2b >= 45:
                        eng.wait_ge(S.s_x2, 4 * (s_i - 1))
                    else:
                        eng.wait_ge(S.s_h2b, 4 * (s_i - 1))
                eng.dma_start(X1S[s_i % 2][:, DM * k:DM * (k + 1)],
                              x1_scr[P * (4 * s_i + k):P * (4 * s_i + k + 1), :]
                              ).then_inc(S.s_x1b, 16)
        if lvl2b == 50:
            eng.wait_ge(S.s_outw, 16 * n_tiles)
            import os as _os
            if _os.environ.get("KDBG2"):
                eng.dma_start(out_d[0:1, 0:4], GFROW[:]).then_inc(S.s_outw, 16)
                eng.dma_start(out_d[1:2, 0:4], ALPHAR[:]).then_inc(S.s_outw, 16)
                eng.dma_start(out_d[2:3, 0:16], GCT.ap().bitcast(F32)[0:1, 0:16]).then_inc(S.s_outw, 16)
                eng.wait_ge(S.s_outw, 16 * n_tiles + 48)
        else:
            if lvl2b == 0:
                eng.wait_ge(S.s_x1b, 16 * 4 * n_super)
            elif lvl2b == 5:
                eng.wait_ge(S.s_h2b, 4 * n_super)
            elif lvl2b == 6:
                eng.wait_ge(S.s_obf, 4 * n_super)
            elif lvl2b == 7:
                eng.wait_ge(S.s_of32, 4 * n_super)
            elif lvl2b == 10:
                eng.wait_ge(S.s_of32, 4 * n_super)
                eng.wait_ge(S.s_obf, 4 * n_super)
            elif lvl2b == 20:
                eng.wait_ge(S.s_evc, 14 * n_super)
                import os as _os
                if _os.environ.get("KDBG3"):
                    ob = out_d.bitcast(BF16)
                    eng.dma_start(ob[0:P, :], T2B[:]).then_inc(S.s_outw, 16)
                    eng.wait_ge(S.s_outw, 16)
                    return
            elif lvl2b == 30:
                eng.wait_ge(S.s_cbc, n_super)
                import os as _os
                if _os.environ.get("KDBG4"):
                    ob = out_d.bitcast(BF16)
                    eng.dma_start(ob[0:P, 0:1024], T3B[:]).then_inc(S.s_outw, 16)
                    eng.dma_start(out_d[P:P + 1, 0:512], TFR[:]).then_inc(S.s_outw, 16)
                    eng.dma_start(out_d[P + 1:P + 2, 0:512], CROW[0:1, :]).then_inc(S.s_outw, 16)
                    eng.wait_ge(S.s_outw, 48)
                    return
            elif lvl2b == 40:
                eng.wait_ge(S.s_tw, n_cyc * n_super)
                import os as _os
                if _os.environ.get("KDBG5"):
                    eng.wait_ge(S.s_of32, 4 * n_super)
                    for rr in range(4):
                        eng.dma_start(out_d[P * rr:P * (rr + 1), :],
                                      OF32[:, 1024 * rr:1024 * (rr + 1)]).then_inc(S.s_outw, 16)
                    eng.wait_ge(S.s_outw, 64)
                    return
            elif lvl2b == 45:
                eng.wait_ge(S.s_x2, 4 * n_super)
            eng.dma_start(out_d, x1_scr).then_inc(S.s_outw, 16)
            eng.wait_ge(S.s_outw, 16)

    @b2.tensor
    def _(eng):
        barrier(eng)
        n_mm = 0
        n_h2t = 0
        n_x2t = 0
        n_cbc = 0
        n_cast = 0
        n_crow = 0
        n_zt = 0
        for s_i in range(n_super if lvl2b >= 6 else 0):
            for k in range(4):
                eng.wait_ge(S.s_h2b, 4 * s_i + k + 1)
                if n_h2t >= 1:
                    if lvl2b != 7:
                        eng.wait_ge(S.s_obf, n_h2t)
                    if lvl2b != 6:
                        eng.wait_ge(S.s_of32, n_h2t)
                for c in range(8):
                    ins = eng.transpose(pH[:, P * c:P * (c + 1)],
                                        H2B[:, P * c:P * (c + 1)], IDB[:])
                    if c == 7:
                        ins.then_inc(S.s_h2t, 1)
                n_h2t += 1
            for cyc in range(n_cyc if lvl2b >= 20 else 0):
                if cyc == 0:
                    eng.wait_ge(S.s_obf, n_h2t)
                jorder = (0, 4, 1, 5, 2, 6, 3, 7) if cyc > 0 else tuple(range(8))
                for (n_o, nj, Wsb, src, in_w) in (
                        (8, 8, W1, OBF, DM), (4, 8, W2, T1B, 512),
                        (2, 4, W3, T2B, 256)):
                    for o in range(n_o):
                        pp = pA if o % 2 == 0 else pB
                        if o == 0:
                            if n_mm >= 1:
                                eng.wait_ge(S.s_evc, n_mm)
                        elif n_mm >= 2:
                            eng.wait_ge(S.s_evc, n_mm - 1)
                        js = jorder if nj == 8 and src is OBF else range(nj)
                        for j_i, j in enumerate(js):
                            if (cyc > 0 and src is OBF and o == 0
                                    and lvl2b >= 40 and j_i % 2 == 0):
                                eng.wait_ge(S.s_cast,
                                            8 * s_i + 4 * (cyc - 1) + j_i // 2 + 1)
                            ins = eng.matmul(
                                pp[:],
                                Wsb[:, in_w * j + P * o:in_w * j + P * (o + 1)],
                                src[:, 512 * j:512 * (j + 1)],
                                start=(j_i == 0), stop=(j_i == nj - 1))
                            if j_i == nj - 1:
                                ins.then_inc(S.s_mmc, 1)
                        n_mm += 1
                if lvl2b < 30:
                    continue
                eng.wait_ge(S.s_evc, n_mm)
                if n_zt >= 1:
                    eng.wait_ge(S.s_crow, n_crow)
                for j in range(2):
                    ins = eng.matmul(pZ[:], W4[:, j:j + 1],
                                     T3B[:, 512 * j:512 * (j + 1)],
                                     start=(j == 0), stop=(j == 1))
                    if j == 1:
                        ins.then_inc(S.s_zt, 1)
                n_zt += 1
                eng.wait_ge(S.s_crow, n_crow + 1)
                n_crow += 1
                if n_cbc >= 1 and lvl2b >= 40:
                    eng.wait_ge(S.s_tw, n_cbc)
                eng.matmul(pC[:], E0C[:], CROW[:],
                           start=True, stop=True).then_inc(S.s_cbc, 1)
                n_cbc += 1
                if cyc < 2:
                    n_cast += 1
            if lvl2b < 45:
                continue
            eng.wait_ge(S.s_tw, n_cbc)
            for k in range(4):
                if n_x2t >= 1:
                    eng.wait_ge(S.s_x2, n_x2t)
                for c in range(8):
                    ins = eng.transpose(
                        pX[:, P * c:P * (c + 1)],
                        OF32[:, 512 * c + P * k:512 * c + P * (k + 1)], IDF[:])
                    if c == 7:
                        ins.then_inc(S.s_x2t, 1)
                n_x2t += 1

    @b2.vector
    def _(eng):
        barrier(eng)
        vch = [0]

        def vs(ins):
            ins.then_inc(S.s_vc2, 1)
            vch[0] += 1

        def vw():
            eng.wait_ge(S.s_vc2, vch[0])

        n_tw = 0
        n_x2 = 0
        for s_i in range(n_super if lvl2b >= 5 else 0):
            for k in range(4):
                i_tile = 4 * s_i + k
                eng.wait_ge(S.s_x1b, 16 * (4 * s_i + k + 1))
                if (s_i > 0 or k > 0) and lvl2b >= 6:
                    eng.wait_ge(S.s_h2t, 4 * s_i + k)
                eng.tensor_scalar(out=H2B[:],
                                  in0=X1S[s_i % 2][:, DM * k:DM * (k + 1)],
                                  scalar1=STR[:, i_tile:i_tile + 1],
                                  scalar2=STB[:, i_tile:i_tile + 1],
                                  op0=MUL, op1=ADD).then_inc(S.s_h2b, 1)
                if lvl2b < 6 or lvl2b == 7:
                    continue
                eng.wait_ge(S.s_h2t, 4 * s_i + k + 1)
                for c in range(8):
                    ins = eng.tensor_copy(
                        OBF[:, 512 * c + P * k:512 * c + P * (k + 1)],
                        pH[:, P * c:P * (c + 1)])
                    if c == 7:
                        ins.then_inc(S.s_obf, 1)
            for cyc in range(n_cyc if lvl2b >= 40 else 0):
                eng.wait_ge(S.s_cbc, n_tw + 1)
                if cyc == 0:
                    eng.wait_ge(S.s_of32, 4 * (s_i + 1))
                pairs = ((0, 4, ADD, SUB), (1, 5, ADD, SUB),
                         (2, 6, SUB, ADD), (3, 7, SUB, ADD))
                vw()
                for (a, b, opa, opb) in pairs:
                    ca = OF32[:, 512 * a:512 * (a + 1)]
                    cb = OF32[:, 512 * b:512 * (b + 1)]
                    vs(eng.tensor_tensor(TWB[:], cb, pC[:], op=MUL))
                    vs(eng.tensor_tensor(TWA[:], ca, pC[:], op=MUL))
                    vw()
                    vs(eng.tensor_tensor(cb, cb, TWA[:], op=opb))
                    vs(eng.tensor_tensor(ca, ca, TWB[:], op=opa))
                    vw()
                    eng.sem_inc(S.s_twp, 1)
                eng.sem_inc(S.s_tw, 1)
                n_tw += 1
            for k in range(4 if lvl2b >= 45 else 0):
                eng.wait_ge(S.s_x2t, n_x2 + 1)
                if n_x2 >= 2 and lvl2b == 50:
                    eng.wait_ge(S.s_outw, 16 * (n_x2 - 1))
                eng.tensor_tensor(X2[n_x2 % 2][:], pX[:],
                                  X1S[s_i % 2][:, DM * k:DM * (k + 1)],
                                  op=ADD).then_inc(S.s_x2, 1)
                n_x2 += 1

    @b2.scalar
    def _(eng):
        barrier(eng)
        n_mm = 0
        n_tw = 0
        n_crow = 0
        for s_i in range(n_super if (lvl2b == 7 or lvl2b >= 10) else 0):
            for k in range(4):
                eng.wait_ge(S.s_h2t, 4 * s_i + k + 1)
                if lvl2b >= 10:
                    eng.wait_ge(S.s_obf, 4 * s_i + k + 1)
                if s_i >= 1 and lvl2b >= 45:
                    eng.wait_ge(S.s_x2t, 4 * s_i)
                for c in range(8):
                    ins = eng.copy(
                        OF32[:, 512 * c + P * k:512 * c + P * (k + 1)],
                        pH[:, P * c:P * (c + 1)])
                    if c == 7:
                        ins.then_inc(S.s_of32, 1)
            for cyc in range(n_cyc if lvl2b >= 20 else 0):
                for (n_o, bias_c, dst) in ((8, B1C, T1B), (4, B2C, T2B),
                                           (2, B3C, T3B)):
                    for o in range(n_o):
                        pp = pA if o % 2 == 0 else pB
                        eng.wait_ge(S.s_mmc, n_mm + 1)
                        n_mm += 1
                        eng.activation(dst[:, 512 * o:512 * (o + 1)], pp[:],
                                       AF.Gelu,
                                       bias=bias_c[:, o:o + 1]).then_inc(S.s_evc, 1)
                if lvl2b < 30:
                    continue
                eng.wait_ge(S.s_zt, n_cyc * s_i + cyc + 1)
                if n_crow >= 1:
                    eng.wait_ge(S.s_crow, n_crow)
                eng.activation(TFR[:], pZ[:], AF.Tanh,
                               bias=B4H[0:1, 0:1], scale=0.5).then_inc(S.s_tfr, 1)
                n_crow += 1
                if cyc < 2 and lvl2b >= 40:
                    for p_i, (aa, bb) in enumerate(((0, 4), (1, 5), (2, 6), (3, 7))):
                        eng.wait_ge(S.s_twp, 4 * (n_cyc * s_i + cyc) + p_i + 1)
                        eng.copy(OBF[:, 512 * aa:512 * (aa + 1)],
                                 OF32[:, 512 * aa:512 * (aa + 1)])
                        eng.copy(OBF[:, 512 * bb:512 * (bb + 1)],
                                 OF32[:, 512 * bb:512 * (bb + 1)]).then_inc(S.s_cast, 1)
            if lvl2b == 50:
                for k in range(4):
                    ki = 4 * s_i + k
                    eng.wait_ge(S.s_x2, ki + 1)
                    eng.dma_start(out_d[P * ki:P * (ki + 1), :],
                                  X2[ki % 2][:]).then_inc(S.s_outw, 16)

    @b2.gpsimd
    def _(eng):
        barrier(eng)
        eng.memset(CROW[:], 0.0).then_inc(S.s_gch, 1)
        eng.wait_ge(S.s_gch, 1)
        for s_i in range(n_super if lvl2b >= 30 else 0):
            seq = s_i // (n_super // n_seq)
            for cyc in range(n_cyc):
                n = n_cyc * s_i + cyc
                eng.wait_ge(S.s_tfr, n + 1)
                eng.tensor_scalar(out=CROW[0:1, :], in0=TFR[:],
                                  scalar1=BETA[0:1, 0:1],
                                  scalar2=ALPHAR[0:1, seq:seq + 1],
                                  op0=MUL, op1=ADD).then_inc(S.s_crow, 1)

    blk2_cm.__exit__(None, None, None)
    st.close()
    return nc, n_tiles


def _host_inputs(inputs, shard):
    f32 = np.float32
    bf = ml_dtypes.bfloat16
    d = {}
    d["x"] = np.ascontiguousarray(shard.reshape(-1, DM))
    d["wqT"] = np.ascontiguousarray(inputs["L0_attn_W_Q"].T)
    d["wkT"] = np.ascontiguousarray(inputs["L0_attn_W_K"].T)
    d["wvT"] = np.ascontiguousarray(inputs["L0_attn_W_V"].T)
    d["w1"] = np.ascontiguousarray(inputs["L0_ffn_cn_w1"])
    d["w2"] = np.ascontiguousarray(inputs["L0_ffn_cn_w2"])
    d["w3"] = np.ascontiguousarray(inputs["L0_ffn_cn_w3"])
    d["w4c"] = np.ascontiguousarray(inputs["L0_ffn_cn_w4"].reshape(2, 128).T)
    d["gw1"] = np.ascontiguousarray(inputs["L0_ffn_gc_w1"])
    d["gw2"] = np.ascontiguousarray(inputs["L0_ffn_gc_w2"])
    d["gw3c"] = np.ascontiguousarray(inputs["L0_ffn_gc_w3"].reshape(2, 128).T)
    d["b1c"] = np.ascontiguousarray(inputs["L0_ffn_cn_b1"].reshape(8, 128).T)
    d["b2c"] = np.ascontiguousarray(inputs["L0_ffn_cn_b2"].reshape(4, 128).T)
    d["b3c"] = np.ascontiguousarray(inputs["L0_ffn_cn_b3"].reshape(2, 128).T)
    d["b4"] = np.ascontiguousarray(inputs["L0_ffn_cn_b4"].reshape(1, 1))
    d["gb1"] = np.ascontiguousarray(inputs["L0_ffn_gc_b1"].reshape(4, 128).T)
    d["gb2"] = np.ascontiguousarray(inputs["L0_ffn_gc_b2"].reshape(2, 128).T)
    d["gb3"] = np.ascontiguousarray(inputs["L0_ffn_gc_b3"].reshape(1, 1))
    d["ar"] = np.ascontiguousarray(inputs["L0_ffn_adaptive_range"].reshape(1, 1))
    d["identb"] = np.eye(P, dtype=bf)
    d["identf"] = np.eye(P, dtype=f32)
    mask1 = np.kron(np.eye(8, dtype=f32), np.ones((16, 16), dtype=f32))
    d["maskc"] = np.tile(mask1, (1, 4)).astype(f32)
    d["onescol"] = np.ones((P, 1), dtype=bf)
    e0 = np.zeros((P, P), dtype=f32)
    e0[0, :] = 1.0
    d["e0c"] = e0
    d["epsc"] = np.full((P, 1), 1e-5, dtype=f32)
    for k in d:
        if d[k].dtype == np.float64:
            raise ValueError(k)
    return d


def kernel(**inputs):
    x = np.asarray(inputs["L0_attn_input_seq_embs"])
    n_cores = 8
    shards = np.split(x, n_cores, axis=0)
    nc, n_tiles = build(n_seq=4)
    in_maps = [_host_inputs(inputs, s) for s in shards]
    res = run_bass_kernel_spmd(nc, in_maps, core_ids=list(range(n_cores)))
    outs = [np.asarray(r["out"], dtype=np.float32) for r in res.results]
    return np.concatenate(outs, axis=0).reshape(32, 2048, 1024)
